# version 01
# baseline (speedup 1.0000x reference)
import numpy as np
from contextlib import ExitStack
import concourse.bass as bass
import concourse.mybir as mybir
from concourse.bass_utils import run_bass_kernel_spmd
from concourse.bass import IndirectOffsetOnAxis

F32 = mybir.dt.float32; BF16 = mybir.dt.bfloat16; I32 = mybir.dt.int32; U32 = mybir.dt.uint32
AF = mybir.ActivationFunctionType; ALU = mybir.AluOpType
AX = mybir.AxisListType.X

T = 2048; D = 1024; NCH = 8; NT = 16
RNE_C = 12582912.0
TWO_PI = 6.283185307179586
TP_HI = 6.28125
TP_LO = 0.0019353071795864769
HALF_PI = 1.5707963267948966
PI = 3.141592653589793

class Sem:
    __slots__ = ("h", "n")
    def __init__(self, nc, ctx, name):
        self.h = ctx.enter_context(nc.semaphore(name)); self.n = 0
    def inc(self, instr, k=1):
        self.n += k; instr.then_inc(self.h, k); return instr


class Env: pass


def barrier(env):
    b = env.s_bar; blk = env.blk
    target = b.n + 5
    b.n = target
    def fs(eng, _t=target):
        eng.nop().then_inc(b.h, 1)
        eng.wait_ge(b.h, _t)
    def ft(eng, _t=target):
        eng.transpose(env.ps_P1.ap().bitcast(BF16)[0:2, 0:128], env.identb[:, 0:2], env.identb[:]).then_inc(b.h, 1)
        eng.wait_ge(b.h, _t)
    def fv(eng, _t=target):
        eng.tensor_copy(env.fceA[:, :], env.fceB[:, :]).then_inc(b.h, 1)
        eng.wait_ge(b.h, _t)
    def fc(eng, _t=target):
        eng.copy(out=env.fceA[:, :], in_=env.fceB[:, :]).then_inc(b.h, 1)
        eng.wait_ge(b.h, _t)
    def fg(eng, _t=target):
        eng.memset(env.fceB[:, :], 0.0)
        eng.memset(env.fceB[:, :], 0.0).then_inc(b.h, 1)
        eng.wait_ge(b.h, _t)
    blk.sync(fs); blk.tensor(ft); blk.vector(fv); blk.scalar(fc); blk.gpsimd(fg)


def transpose_tiles(env, pairs):
    blk = env.blk; mm, sc = env.s_mm, env.s_sc
    mm0 = mm.n
    sc0 = sc.n
    n = len(pairs)
    def t(t):
        for k in range(n):
            if k >= 2:
                t.wait_ge(sc.h, sc0 + (k - 2 + 1))
            mm.inc(t.transpose(env.ps_t[k % 2][:, :], pairs[k][0], env.identb[:]))
    def s(s):
        for k in range(n):
            s.wait_ge(mm.h, mm0 + k + 1)
            sc.inc(s.copy(out=pairs[k][1], in_=env.ps_t[k % 2][:, :]))
    blk.tensor(t); blk.scalar(s)


def build_program(upto):
    nc = bass.Bass("TRN2", target_bir_lowering=False, debug=False, num_devices=8, detect_race_conditions=False)
    env = Env(); env.nc = nc
    ctx = ExitStack(); env.ctx = ctx

    dram = {}
    def din(name, shape, dtype=F32):
        dram[name] = nc.dram_tensor(name, shape, dtype, kind="ExternalInput").ap(); return dram[name]
    env.dram = dram
    din("x", [T, D])
    din("l0_qkvu_wT", [D, 4*D])
    din("l0_qkvu_b", [1, 4*D])
    din("l0_out_wT", [D, D])
    din("l0_out_b", [1, D])
    din("l0_theta", [128, 8])
    din("l0_slope", [1, 8])
    din("l0f_wgT", [D, 4096])
    din("l0f_wuT", [D, 4096])
    din("l0f_wdT", [4096, D])
    din("l1f_wq", [D, 2048])
    din("l1f_keysT", [16, 128, 256])
    din("l1f_gamma", [128, 16])
    din("l1f_beta", [128, 16])
    din("l1f_tcat", [65536, 2048])
    din("c_iota256", [128, 256], I32)
    din("l1_wqT", [2, D, 128])
    din("l1_wkT", [D, 128])
    din("l1_wvT", [D, 128])
    din("l1_woT", [256, D])
    din("c_rope_cos", [64, T])
    din("c_rope_sin", [64, T])
    din("c_diagmask", [128, 128])
    din("c_ident", [128, 128])
    din("c_triu", [128, 128])
    din("l0_qkvu_b_col", [128, 32])
    env.y = nc.dram_tensor("y", [T, D], F32, kind="ExternalOutput").ap()
    env.wqd_dram = nc.dram_tensor("wqd_dram", [32, 128, D], BF16, kind="Internal").ap()
    env.wqg_dram = nc.dram_tensor("wqg_dram", [NCH, 128, 4096], BF16, kind="Internal").ap()
    env.wqu_dram = nc.dram_tensor("wqu_dram", [NCH, 128, 4096], BF16, kind="Internal").ap()
    env.h2_dram = nc.dram_tensor("h2_dram", [NT, 4, 128, 1024], BF16, kind="Internal").ap()
    env.bn_in = nc.dram_tensor("bn_in", [128, 32], F32, kind="Internal").ap()
    env.bn_out = nc.dram_tensor("bn_out", [128, 32], F32, kind="Internal").ap()
    env.dbg = nc.dram_tensor("dbg", [128, 8, 2048], F32, kind="ExternalOutput").ap()

    S_ = lambda n_, sh, dt=F32: ctx.enter_context(nc.sbuf_tensor(n_, sh, dt))
    P_ = lambda n_, sh, dt=F32: ctx.enter_context(nc.psum_tensor(n_, sh, dt))
    env.S_, env.P_ = S_, P_

    env.stream = S_("stream", [128, NT, D])
    env.xln_tm = S_("xln_tm", [128, NT, D], BF16)
    env.xln_fm = S_("xln_fm", [128, NCH, T], BF16)
    env.attn_out = env.xln_tm
    env.identb = S_("identb", [128, 128], BF16)
    env.triub = S_("triub", [128, 128], BF16)
    env.onesb = S_("onesb", [1, 128], BF16)
    env.onesf = S_("onesf", [1, 128])
    env.colA = S_("colA", [128, 8])
    env.stat = S_("stat", [128, NT, 8])
    env.scr1 = S_("scr1", [128, 2048])
    env.scr2 = S_("scr2", [128, 2048])
    env.qf = S_("qf", [128, 2, T], BF16)
    env.kf = S_("kf", [128, 2, T], BF16)
    env.ktm = S_("ktm", [128, NT, 256], BF16)
    env.vtm = S_("vtm", [128, NT, 128], BF16)
    env.cosf = S_("cosf", [128, T], BF16)
    env.sinf = S_("sinf", [128, T], BF16)
    env.Sp = None
    env.Psnap = None
    env.Pfin = S_("Pfin", [128, 256], BF16)
    env.Psb = S_("Psb", [128, 256])
    env.maskh = S_("maskh", [128, 128], BF16)
    env.acol = S_("acol", [128, 2])
    env.wpro = S_("wpro", [128, NCH, 128], BF16)
    env.wbig = S_("wbig", [128, NCH, 512], BF16)
    env.zt = S_("zt", [128, 512], BF16)
    env.l0bias_rowb = S_("l0bias_rowb", [1, 2*D], BF16)
    env.l0bias_col = S_("l0bias_col", [128, 32])
    env.l0slope = S_("l0slope", [1, 8])
    env.l0bo = S_("l0bo", [1, D], BF16)
    env.fceA = S_("fceA", [1, 2])
    env.fceB = S_("fceB", [1, 2])
    env.wacc = S_("wacc", [128, 6])
    env.wcols = S_("wcols", [128, 6])
    env.stat2 = S_("stat2", [128, NT, 4])
    env.onescol = S_("onescol", [128, 1])
    env.iota256 = S_("iota256", [128, 256], I32)

    env.ps_a = P_("ps_a", [128, 512])
    env.ps_b = P_("ps_b", [128, 512])
    env.ps_c = P_("ps_c", [128, 128])
    env.ps_d = P_("ps_d", [128, 128])
    env.ps_t = [P_(f"ps_t{i}", [128, 128], BF16) for i in range(2)]
    env.ps_P0 = P_("ps_P0", [128, 128])
    env.ps_P1 = P_("ps_P1", [128, 128])

    for s in ("ld", "mm", "ve", "sc", "gp", "st", "bar", "ld2", "mm2", "ve2", "sc2", "gp2", "init", "gd"):
        setattr(env, "s_" + s, Sem(nc, ctx, "s_" + s))

    env.Sp = env.cosf.ap().rearrange("p (t c) -> p t c", t=NT)
    env.Psnap1 = env.scr1.ap().rearrange("p (t c) -> p t c", t=8)
    env.Psnap2 = env.scr2.ap().rearrange("p (t c) -> p t c", t=8)
    env.Mc = env.sinf.ap().rearrange("p (k c) -> p k c", k=4)[:, :, 0:256]


    import os as _os2
    env.bn_group = list(range(int(_os2.environ.get("MK_BNG", "8"))))

    blk = ctx.enter_context(nc.Block()); env.blk = blk

    p0_load(env)
    if upto >= 2:
        p_l0_attn(env)
    if upto >= 3:
        import os as _os
        if _os.environ.get("MK_STAGE") == "x1":
            pass
        else:
            p_l0_ffn(env)
    if upto >= 4:
        p_l1_attn(env)
    if upto >= 5:
        p_l1_ffn(env)
    p_store(env)
    ctx.close()
    return nc


def p0_load(env):
    blk = env.blk; ld = env.s_ld; ld2 = env.s_ld2
    init = env.s_init
    def ginit(g):
        g.memset(env.fceA[:, :], 0.0)
        init.inc(g.memset(env.fceB[:, :], 0.0), 1)
        ld2.inc(g.dma_start(out=env.identb[:], in_=env.dram["c_ident"][:, :]), 16)
    blk.gpsimd(ginit)
    tgt = init.n; tgt2 = ld2.n
    for name in ("sync", "tensor", "vector", "scalar"):
        def fw(eng, _t=tgt, _t2=tgt2):
            eng.wait_ge(init.h, _t)
            eng.wait_ge(ld2.h, _t2)
        getattr(blk, name)(fw)
    def f(s):
        for tc in range(NT):
            ld.inc(s.dma_start(out=env.stream[:, tc, :], in_=env.dram["x"][tc*128:(tc+1)*128, :]), 16)
    blk.sync(f)
    def g(g):
        ld2.inc(g.dma_start(out=env.triub[:], in_=env.dram["c_triu"][:, :]), 16)
        ld2.inc(g.dma_start(out=env.colA[:, 0:8], in_=env.dram["l0_theta"][:, :]), 16)
        ld2.inc(g.dma_start(out=env.l0bias_rowb[:, :], in_=env.dram["l0_qkvu_b"][:, 2048:4096]), 16)
        ld2.inc(g.dma_start(out=env.l0bias_col[:, :], in_=env.dram["l0_qkvu_b_col"][:, :]), 16)
        ld2.inc(g.dma_start(out=env.l0slope[:, :], in_=env.dram["l0_slope"][:, :]), 16)
        ld2.inc(g.dma_start(out=env.l0bo[:, :], in_=env.dram["l0_out_b"][:, :]), 16)
    blk.gpsimd(g)
    ldn = ld.n; ldn2 = ld2.n
    def v(v):
        v.memset(env.onesb[:, :], 1.0)
        v.memset(env.onesf[:, :], 1.0)
        v.memset(env.onescol[:, :], 1.0)
        v.wait_ge(ld.h, ldn)
        v.wait_ge(ld2.h, ldn2)
    blk.vector(v)
    barrier(env)


def emit_layernorm(env, need_fm=True, eps=1e-5):
    blk = env.blk
    ve, mm, sc = env.s_ve, env.s_mm, env.s_sc
    stat = env.stat; scr = env.scr1
    def v(v):
        for tc in range(NT):
            v.reduce_sum(stat[:, tc, 2:3], env.stream[:, tc, :], axis=AX)
            v.scalar_tensor_tensor(out=scr[:, 0:1024], in0=env.stream[:, tc, :], scalar=1.0,
                                   in1=env.stream[:, tc, :], op0=ALU.mult, op1=ALU.mult,
                                   accum_out=stat[:, tc, 3:4])
        for tc in range(NT):
            v.tensor_scalar(out=stat[:, tc, 0:1], in0=stat[:, tc, 2:3], scalar1=1.0/D, scalar2=None, op0=ALU.mult)
        for tc in range(NT):
            v.tensor_tensor(stat[:, tc, 5:6], stat[:, tc, 0:1], stat[:, tc, 0:1], ALU.mult)
        for tc in range(NT):
            v.tensor_scalar(out=stat[:, tc, 6:7], in0=stat[:, tc, 3:4], scalar1=1.0/D, scalar2=eps, op0=ALU.mult, op1=ALU.add)
        for tc in range(NT):
            v.tensor_tensor(stat[:, tc, 6:7], stat[:, tc, 6:7], stat[:, tc, 5:6], ALU.subtract)
        ve.inc(v.tensor_copy(env.fceA[:, :], env.fceB[:, :]))
    blk.vector(v)
    ve1 = ve.n
    def s(s):
        s.wait_ge(ve.h, ve1)
        for tc in range(NT):
            s.activation(out=stat[:, tc, 7:8], in_=stat[:, tc, 6:7], func=AF.Sqrt)
        sc.inc(s.copy(out=env.fceA[:, :], in_=env.fceB[:, :]))
    blk.scalar(s)
    sc1 = sc.n
    def v2(v):
        v.wait_ge(sc.h, sc1)
        for tc in range(NT):
            v.reciprocal(stat[:, tc, 1:2], stat[:, tc, 7:8])
        for tc in range(NT):
            v.tensor_tensor(stat[:, tc, 5:6], stat[:, tc, 1:2], stat[:, tc, 1:2], ALU.mult)
        for tc in range(NT):
            v.tensor_tensor(stat[:, tc, 5:6], stat[:, tc, 5:6], stat[:, tc, 6:7], ALU.mult)
        for tc in range(NT):
            v.tensor_scalar(out=stat[:, tc, 5:6], in0=stat[:, tc, 5:6], scalar1=-0.5, scalar2=1.5, op0=ALU.mult, op1=ALU.add)
        for tc in range(NT):
            v.tensor_tensor(stat[:, tc, 1:2], stat[:, tc, 1:2], stat[:, tc, 5:6], ALU.mult)
        for tc in range(NT):
            v.tensor_tensor(stat[:, tc, 4:5], stat[:, tc, 0:1], stat[:, tc, 1:2], ALU.mult)
        for tc in range(NT):
            v.tensor_scalar(out=stat[:, tc, 4:5], in0=stat[:, tc, 4:5], scalar1=-1.0, scalar2=None, op0=ALU.mult)
        for tc in range(NT):
            ve.inc(v.tensor_scalar(out=env.xln_tm[:, tc, :], in0=env.stream[:, tc, :],
                                   scalar1=stat[:, tc, 1:2], scalar2=stat[:, tc, 4:5],
                                   op0=ALU.mult, op1=ALU.add))
    blk.vector(v2)
    if need_fm:
        barrier(env)
        pairs = []
        for tc in range(NT):
            for c in range(NCH):
                pairs.append((env.xln_tm[:, tc, c*128:(c+1)*128], env.xln_fm[:, c, tc*128:(tc+1)*128]))
        transpose_tiles(env, pairs)
    barrier(env)


def dump_tm(env, src_tiles, width=1024):
    blk = env.blk; ve = env.s_ve
    def v(v):
        for tc in range(NT):
            ve.inc(v.tensor_copy(env.stream[:, tc, 0:width], src_tiles[:, tc, :]))
        if width < 1024:
            for tc in range(NT):
                ve.inc(v.memset(env.stream[:, tc, width:1024], 0.0))
    blk.vector(v)
    barrier(env)


def dump_fm(env, src, chunks=8):
    blk = env.blk; st = env.s_st
    def g(g):
        for c in range(chunks):
            st.inc(g.dma_start(out=env.dbg[:, c, :], in_=src[:, c, :]), 16)
        g.wait_ge(st.h, st.n)
    blk.gpsimd(g)
    barrier(env)


import os
def p_l0_attn(env):
    blk = env.blk
    stage = os.environ.get("MK_STAGE", "")
    emit_layernorm(env, need_fm=(stage != "ln_nofm"))
    if stage in ("ln", "ln_nofm"):
        def gst(g):
            env.s_st.inc(g.dma_start(out=env.dbg[:, 0, 0:128], in_=env.stat[:, :, :].rearrange("p t c -> p (t c)")), 16)
            g.wait_ge(env.s_st.h, env.s_st.n)
        blk.gpsimd(gst)
        dump_tm(env, env.xln_tm); return
    if stage == "lnfm":
        dump_fm(env, env.xln_fm); return
    nheads = 1 if stage in ("qk0", "attn0") else 8
    for h in range(nheads):
        stopped = _l0_head(env, h)
        if stopped:
            return
        if stage == "qk0" and h == 0:
            dump_fm(env, env.qf, chunks=2)
            def g2(g):
                env.s_st.inc(g.dma_start(out=env.dbg[:, 2, :], in_=env.kf[:, 0, :]), 16)
                env.s_st.inc(g.dma_start(out=env.dbg[:, 3, :], in_=env.kf[:, 1, :]), 16)
                env.s_st.inc(g.dma_start(out=env.dbg[:, 4, :], in_=env.cosf[:, :]), 16)
                env.s_st.inc(g.dma_start(out=env.dbg[:, 5, :], in_=env.sinf[:, :]), 16)
                g.wait_ge(env.s_st.h, env.s_st.n)
            blk.gpsimd(g2)
            barrier(env)
            return
        if stage == "attn0" and h == 0:
            dump_tm(env, env.attn_out); return
    if stage == "attn":
        dump_tm(env, env.attn_out); return
    _l0_epilogue(env)


def _l0_head(env, h):
    import os as _os
    hstop = _os.environ.get("MK_HSTOP", "")
    blk = env.blk
    ld, mm, ve, sc, gp = env.s_ld, env.s_mm, env.s_ve, env.s_sc, env.s_gp
    ld2 = env.s_ld2
    ve2, sc2 = env.s_ve2, env.s_sc2
    wT = env.dram["l0_qkvu_wT"]

    def g(g):
        for c in range(NCH):
            ld2.inc(g.dma_start(out=env.wpro[:, c, :], in_=wT[c*128:(c+1)*128, h*128:(h+1)*128]), 16)
    blk.gpsimd(g)
    ld_w = ld2.n

    import os as _os2
    skip_tab = _os2.environ.get("MK_SKIPTAB")
    scr, scr2 = env.scr1, env.scr2
    if skip_tab:
        def snop(s):
            sc.inc(s.copy(out=env.fceA[:, :], in_=env.fceB[:, :]))
        blk.scalar(snop)
        sc_tab = sc.n
        if hstop == "tab":
            barrier(env)
            def gtb(g):
                env.s_st.inc(g.dma_start(out=env.dbg[:, 2, :], in_=env.xln_fm[:, 0, :]), 16)
                g.wait_ge(env.s_st.h, env.s_st.n)
            blk.gpsimd(gtb)
            barrier(env); return
    def gi(g):
        gp.inc(g.iota(scr[:, :], pattern=[[1, T]], base=0, channel_multiplier=0, allow_small_or_imprecise_dtypes=True))
    if not skip_tab:
        blk.gpsimd(gi)
    gp0 = gp.n
    def v(v):
        v.wait_ge(gp.h, gp0)
        v.tensor_scalar(out=scr[:, :], in0=scr[:, :], scalar1=env.colA[:, h:h+1], scalar2=None, op0=ALU.mult)
        v.tensor_scalar(out=scr2[:, :], in0=scr[:, :], scalar1=1.0/TWO_PI, scalar2=RNE_C, op0=ALU.mult, op1=ALU.add)
        v.tensor_scalar(out=scr2[:, :], in0=scr2[:, :], scalar1=RNE_C, scalar2=None, op0=ALU.subtract)
        v.scalar_tensor_tensor(out=scr[:, :], in0=scr2[:, :], scalar=-TP_HI, in1=scr[:, :], op0=ALU.mult, op1=ALU.add)
        v.scalar_tensor_tensor(out=scr[:, :], in0=scr2[:, :], scalar=-TP_LO, in1=scr[:, :], op0=ALU.mult, op1=ALU.add)
        v.tensor_scalar(out=scr[:, :], in0=scr[:, :], scalar1=PI, scalar2=-PI, op0=ALU.min, op1=ALU.max)
        ve.inc(v.tensor_copy(env.fceA[:, :], env.fceB[:, :]))
    if not skip_tab:
        blk.vector(v)
    ve_ang = ve.n
    def s(s):
        s.wait_ge(ve.h, ve_ang)
        s.activation(out=env.sinf[:, :], in_=scr[:, :], func=AF.Sin)
        sc.inc(s.copy(out=env.fceA[:, :], in_=env.fceB[:, :]))
    if not skip_tab:
        blk.scalar(s)
    sc_sin = sc.n
    def vc(v):
        v.wait_ge(sc.h, sc_sin)
        v.tensor_scalar(out=scr2[:, :], in0=scr[:, :], scalar1=1.0, scalar2=HALF_PI, op0=ALU.mult, op1=ALU.add)
        v.tensor_scalar(out=scr[:, :], in0=scr2[:, :], scalar1=1.0/TWO_PI, scalar2=RNE_C, op0=ALU.mult, op1=ALU.add)
        v.tensor_scalar(out=scr[:, :], in0=scr[:, :], scalar1=RNE_C, scalar2=None, op0=ALU.subtract)
        v.scalar_tensor_tensor(out=scr2[:, :], in0=scr[:, :], scalar=-TP_HI, in1=scr2[:, :], op0=ALU.mult, op1=ALU.add)
        v.scalar_tensor_tensor(out=scr2[:, :], in0=scr[:, :], scalar=-TP_LO, in1=scr2[:, :], op0=ALU.mult, op1=ALU.add)
        v.tensor_scalar(out=scr2[:, :], in0=scr2[:, :], scalar1=PI, scalar2=-PI, op0=ALU.min, op1=ALU.max)
        ve.inc(v.tensor_copy(env.fceA[:, :], env.fceB[:, :]))
    if not skip_tab:
        blk.vector(vc)
    ve_cosarg = ve.n
    def s1b(s):
        s.wait_ge(ve.h, ve_cosarg)
        s.activation(out=env.cosf[:, :], in_=scr2[:, :], func=AF.Sin)
        sc.inc(s.copy(out=env.fceA[:, :], in_=env.fceB[:, :]))
    if not skip_tab:
        blk.scalar(s1b)
        sc_tab = sc.n
    if hstop == "tab":
        barrier(env)
        def gtb(g):
            env.s_st.inc(g.dma_start(out=env.dbg[:, 2, :], in_=env.xln_fm[:, 0, :]), 16)
            g.wait_ge(env.s_st.h, env.s_st.n)
        blk.gpsimd(gtb)
        barrier(env); return True

    sc_qk_base = sc.n
    for role in range(2):
        dstf = env.qf if role == 0 else env.kf
        if role == 1:
            mm_before = mm.n
            def gk(g, _mm=mm_before):
                g.wait_ge(mm.h, _mm)
                for c in range(NCH):
                    ld2.inc(g.dma_start(out=env.wpro[:, c, :], in_=wT[c*128:(c+1)*128, 1024+h*128:1024+(h+1)*128]), 16)
            blk.gpsimd(gk)
        ld_w2 = ld2.n
        mm0 = mm.n
        def t(t, _ld=ld_w2, _role=role):
            t.wait_ge(ld2.h, _ld)
            for n4 in range(4):
                ps = env.ps_a if n4 % 2 == 0 else env.ps_b
                kg = _role*4 + n4
                if kg >= 2:
                    t.wait_ge(sc.h, sc_qk_base + (kg - 2 + 1))
                for c in range(NCH):
                    mm.inc(t.matmul(ps[:, :], env.wpro[:, c, :], env.xln_fm[:, c, n4*512:(n4+1)*512],
                                    start=(c == 0), stop=(c == NCH-1)))
        def s2(s, _role=role, _mm0=mm0, _dst=dstf):
            bc = env.l0bias_col[:, h:h+1] if _role == 0 else env.l0bias_col[:, 8+h:9+h]
            for n4 in range(4):
                ps = env.ps_a if n4 % 2 == 0 else env.ps_b
                s.wait_ge(mm.h, _mm0 + (n4+1)*NCH)
                sc.inc(s.activation(out=_dst[:, 0, n4*512:(n4+1)*512], in_=ps[:, :], func=AF.Silu, bias=bc, scale=1.0))
        blk.tensor(t); blk.scalar(s2)
    sc_evac = sc.n
    if hstop == "proj":
        barrier(env)
        def gpj(g):
            env.s_st.inc(g.dma_start(out=env.dbg[:, 0, :], in_=env.qf[:, 0, :]), 16)
            env.s_st.inc(g.dma_start(out=env.dbg[:, 1, :], in_=env.kf[:, 0, :]), 16)
            env.s_st.inc(g.dma_start(out=env.dbg[:, 2, :], in_=env.xln_fm[:, 0, :]), 16)
            env.s_st.inc(g.dma_start(out=env.dbg[:, 3, :], in_=env.xln_fm[:, 4, :]), 16)
            g.wait_ge(env.s_st.h, env.s_st.n)
        blk.gpsimd(gpj)
        barrier(env); return True
    def v3(v):
        v.wait_ge(sc.h, sc_evac)
        v.tensor_tensor(env.qf[:, 1, :], env.qf[:, 0, :], env.sinf[:, :], ALU.mult)
        v.tensor_tensor(env.kf[:, 1, :], env.kf[:, 0, :], env.sinf[:, :], ALU.mult)
        v.tensor_tensor(env.qf[:, 0, :], env.qf[:, 0, :], env.cosf[:, :], ALU.mult)
        ve.inc(v.tensor_tensor(env.kf[:, 0, :], env.kf[:, 0, :], env.cosf[:, :], ALU.mult))
    blk.vector(v3)
    if hstop == "rot":
        barrier(env); return True

    ve_rot = ve.n
    def tw(t):
        t.wait_ge(ve.h, ve_rot)
    blk.tensor(tw)
    pairs = []
    for tc in range(NT):
        for e in range(2):
            pairs.append((env.kf[:, e, tc*128:(tc+1)*128], env.ktm[:, tc, e*128:(e+1)*128]))
    transpose_tiles(env, pairs)
    if hstop == "ktm":
        barrier(env); return True

    mm_vw = mm.n
    def gv(g, _mm=mm_vw):
        g.wait_ge(mm.h, _mm)
        for c in range(NCH):
            ld2.inc(g.dma_start(out=env.wpro[:, c, :], in_=wT[c*128:(c+1)*128, 2048+h*128:2048+(h+1)*128]), 16)
    blk.gpsimd(gv)
    ld_v = ld2.n
    mm_v0 = mm.n
    def t3(t):
        t.wait_ge(ld2.h, ld_v)
        for tc in range(NT):
            for c in range(NCH):
                t.matmul(env.ps_c[:, :], env.xln_fm[:, c, tc*128:(tc+1)*128], env.wpro[:, c, :],
                         start=(c == 0), stop=False)
            mm.inc(t.matmul(env.ps_c[:, :], env.onesb[:, :], env.l0bias_rowb[:, h*128:(h+1)*128],
                     start=False, stop=True))
            t.wait_ge(ve2.h, ve2_v0 + tc + 1)
    ve2_v0 = ve2.n
    def v4(v):
        for tc in range(NT):
            v.wait_ge(mm.h, mm_v0 + tc + 1)
            ve2.inc(v.tensor_copy(env.vtm[:, tc, :], env.ps_c[:, :]))
    blk.tensor(t3); blk.vector(v4)
    if hstop == "vtm":
        barrier(env); return True

    def s5(s):
        s.activation(out=env.scr2[0:1, 0:1], in_=env.l0slope[:, h:h+1], func=AF.Exp)
        sc.inc(s.copy(out=env.fceA[:, :], in_=env.fceB[:, :]))
    blk.scalar(s5)
    sc_m1 = sc.n
    def t4(t):
        t.wait_ge(sc.h, sc_m1)
        mm.inc(t.matmul(env.ps_c[:, 0:1], env.onesf[:, :], env.scr2[0:1, 0:1], start=True, stop=True))
    blk.tensor(t4)
    mm_m1 = mm.n
    def v6(v):
        v.wait_ge(mm.h, mm_m1)
        v.tensor_scalar(out=env.acol[:, 0:1], in0=env.ps_c[:, 0:1], scalar1=-1.0, scalar2=None, op0=ALU.add)
        v.memset(env.Psb[:, :], 0.0)
        ve.inc(v.tensor_scalar(out=env.maskh[:, :], in0=env.triub[:, :], scalar1=env.acol[:, 0:1], scalar2=None,
                               op0=ALU.mult))
    blk.vector(v6)
    barrier(env)
    if hstop == "mask":
        return True

    mm_s0 = mm.n
    ve_s0 = ve.n
    def t5(t):
        for c in range(NT):
            t.matmul(env.ps_a[:, 0:128], env.kf[:, 0, c*128:(c+1)*128], env.qf[:, 0, c*128:(c+1)*128], start=True, stop=False)
            mm.inc(t.matmul(env.ps_a[:, 0:128], env.kf[:, 1, c*128:(c+1)*128], env.qf[:, 1, c*128:(c+1)*128], start=False, stop=True))
            t.matmul(env.ps_P0[:, :], env.ktm[:, c, 0:128], env.vtm[:, c, :], start=True, stop=True)
            mm.inc(t.matmul(env.ps_P1[:, :], env.ktm[:, c, 128:256], env.vtm[:, c, :], start=True, stop=True))
            t.wait_ge(ve.h, ve_s0 + c + 1)
    def v7(v):
        for c in range(NT):
            snap = env.Psnap1[:, c, :] if c < 8 else env.Psnap2[:, c - 8, :]
            v.wait_ge(mm.h, mm_s0 + 2*c + 1)
            v.tensor_tensor(env.Sp[:, c, :], env.maskh[:, :], env.ps_a[:, 0:128], ALU.mult)
            v.wait_ge(mm.h, mm_s0 + 2*c + 2)
            v.tensor_tensor(env.Psb[:, 0:128], env.Psb[:, 0:128], env.ps_P0[:, :], ALU.add)
            v.tensor_tensor(env.Psb[:, 128:256], env.Psb[:, 128:256], env.ps_P1[:, :], ALU.add)
            ve.inc(v.tensor_copy(snap, env.Psb[:, :]))
    blk.tensor(t5); blk.vector(v7)
    if hstop == "sweep1x":
        barrier(env)
        def gs1(g):
            env.s_st.inc(g.dma_start(out=env.dbg[:, 0, :], in_=env.cosf[:, :]), 16)
            env.s_st.inc(g.dma_start(out=env.dbg[:, 1, :], in_=env.scr1[:, :]), 16)
            env.s_st.inc(g.dma_start(out=env.dbg[:, 2, :], in_=env.scr2[:, :]), 16)
            env.s_st.inc(g.dma_start(out=env.dbg[:, 3, 0:256], in_=env.Psb[:, :]), 16)
            env.s_st.inc(g.dma_start(out=env.dbg[:, 4, 0:2048], in_=env.ktm[:, :, :].rearrange("p t c -> p (t c)")[:, 0:2048]), 16)
            env.s_st.inc(g.dma_start(out=env.dbg[:, 5, 0:2048], in_=env.vtm[:, :, :].rearrange("p t c -> p (t c)")), 16)
            g.wait_ge(env.s_st.h, env.s_st.n)
        blk.gpsimd(gs1)
        barrier(env)
        return True
    mm_sweep1 = mm.n
    ve_sweep1 = ve.n
    def s7(s):
        s.wait_ge(mm.h, mm_sweep1)
        s.wait_ge(ve.h, ve_sweep1)
        sc.inc(s.copy(out=env.Pfin[:, :], in_=env.Psb[:, :]))
    blk.scalar(s7)
    sc_pfin = sc.n

    ve_mc0 = ve.n
    mm_s2 = mm.n
    def vmc(v):
        v.wait_ge(sc.h, sc_pfin)
        for c in range(1, NT):
            snap = env.Psnap1[:, c-1, :] if c-1 < 8 else env.Psnap2[:, c-9, :]
            if c >= 5:
                v.wait_ge(mm.h, mm_s2 + c - 3)
            ve.inc(v.scalar_tensor_tensor(out=env.Mc[:, (c-1) % 4, :], in0=snap, scalar=env.acol[:, 0:1],
                                          in1=env.Psb[:, :], op0=ALU.mult, op1=ALU.add))
    blk.vector(vmc)
    def t6(t):
        t.wait_ge(sc.h, sc_pfin)
        for c in range(NT):
            ps = env.ps_c if c % 2 == 0 else env.ps_d
            t.matmul(ps[:, :], env.Sp[:, c, :], env.vtm[:, c, :], start=True, stop=False)
            if c == 0:
                t.matmul(ps[:, :], env.qf[:, 0, c*128:(c+1)*128], env.Pfin[:, 0:128], start=False, stop=False)
                mm.inc(t.matmul(ps[:, :], env.qf[:, 1, c*128:(c+1)*128], env.Pfin[:, 128:256], start=False, stop=True))
            else:
                t.wait_ge(ve.h, ve_mc0 + c)
                mc = env.Mc[:, (c-1) % 4, :]
                t.matmul(ps[:, :], env.qf[:, 0, c*128:(c+1)*128], mc[:, 0:128], start=False, stop=False)
                mm.inc(t.matmul(ps[:, :], env.qf[:, 1, c*128:(c+1)*128], mc[:, 128:256], start=False, stop=True))
            if c >= 1:
                t.wait_ge(sc2.h, sc2_o0 + c)
    sc2_o0 = sc2.n
    def s8(s):
        for c in range(NT):
            s.wait_ge(mm.h, mm_s2 + c + 1)
            sc2.inc(s.copy(out=env.attn_out[:, c, h*128:(h+1)*128], in_=(env.ps_c if c % 2 == 0 else env.ps_d)[:, :]))
    blk.tensor(t6); blk.scalar(s8)
    barrier(env)


def _l0_epilogue(env):
    blk = env.blk
    mm, ve, sc, ld = env.s_mm, env.s_ve, env.s_sc, env.s_ld
    ld2 = env.s_ld2
    stat, scr = env.stat, env.scr1
    EPS = 1e-6
    def v(v):
        for tc in range(NT):
            v.scalar_tensor_tensor(out=scr[:, 0:1024], in0=env.attn_out[:, tc, :], scalar=1.0,
                                   in1=env.attn_out[:, tc, :], op0=ALU.mult, op1=ALU.mult,
                                   accum_out=stat[:, tc, 3:4])
        for tc in range(NT):
            v.tensor_scalar(out=stat[:, tc, 6:7], in0=stat[:, tc, 3:4], scalar1=1.0/D, scalar2=EPS, op0=ALU.mult, op1=ALU.add)
        ve.inc(v.tensor_copy(env.fceA[:, :], env.fceB[:, :]))
    blk.vector(v)
    ve1 = ve.n
    def s(s):
        s.wait_ge(ve.h, ve1)
        for tc in range(NT):
            s.activation(out=stat[:, tc, 7:8], in_=stat[:, tc, 6:7], func=AF.Sqrt)
        sc.inc(s.copy(out=env.fceA[:, :], in_=env.fceB[:, :]))
    blk.scalar(s)
    sc1 = sc.n
    def v2(v):
        v.wait_ge(sc.h, sc1)
        for tc in range(NT):
            v.reciprocal(stat[:, tc, 1:2], stat[:, tc, 7:8])
        for tc in range(NT):
            v.tensor_tensor(stat[:, tc, 5:6], stat[:, tc, 1:2], stat[:, tc, 1:2], ALU.mult)
        for tc in range(NT):
            v.tensor_tensor(stat[:, tc, 5:6], stat[:, tc, 5:6], stat[:, tc, 6:7], ALU.mult)
        for tc in range(NT):
            v.tensor_scalar(out=stat[:, tc, 5:6], in0=stat[:, tc, 5:6], scalar1=-0.5, scalar2=1.5, op0=ALU.mult, op1=ALU.add)
        for tc in range(NT):
            v.tensor_tensor(stat[:, tc, 1:2], stat[:, tc, 1:2], stat[:, tc, 5:6], ALU.mult)
        ve.inc(v.tensor_copy(env.fceA[:, :], env.fceB[:, :]))
    blk.vector(v2)
    wT = env.dram["l0_qkvu_wT"]
    scr2 = env.scr2
    for half in range(2):
        def g(g, _h=half):
            for c in range(NCH):
                ld2.inc(g.dma_start(out=env.wbig[:, c, :], in_=wT[c*128:(c+1)*128, 3072+_h*512:3072+(_h+1)*512]), 16)
        blk.gpsimd(g)
        ld_u = env.s_ld2.n
        barrier(env)
        mm0 = mm.n
        ve_z0 = ve.n
        def t(t, _h=half, _ld=ld_u, _mm0=mm0, _ve0=ve_z0):
            t.wait_ge(env.s_ld2.h, _ld)
            for tc in range(NT):
                ps = env.ps_a if tc % 2 == 0 else env.ps_b
                for c in range(NCH):
                    t.matmul(ps[:, :], env.xln_fm[:, c, tc*128:(tc+1)*128], env.wbig[:, c, :],
                             start=(c == 0), stop=False)
                mm.inc(t.matmul(ps[:, :], env.onesb[:, :], env.l0bias_rowb[:, 1024+_h*512:1024+(_h+1)*512],
                         start=False, stop=True))
                if tc >= 1:
                    t.wait_ge(ve.h, _ve0 + tc)
        def v3(v, _h=half, _mm0=mm0):
            for tc in range(NT):
                v.wait_ge(mm.h, _mm0 + tc + 1)
                ps = env.ps_a if tc % 2 == 0 else env.ps_b
                v.tensor_scalar(out=scr2[:, 0:512], in0=env.attn_out[:, tc, _h*512:(_h+1)*512],
                                scalar1=stat[:, tc, 1:2], scalar2=None, op0=ALU.mult)
                ve.inc(v.tensor_tensor(env.attn_out[:, tc, _h*512:(_h+1)*512], scr2[:, 0:512], ps[:, :], ALU.mult))
        blk.tensor(t); blk.vector(v3)
        barrier(env)

    z_fm = env.xln_fm
    pairs = []
    for tc in range(NT):
        for c in range(NCH):
            pairs.append((env.attn_out[:, tc, c*128:(c+1)*128], z_fm[:, c, tc*128:(tc+1)*128]))
    transpose_tiles(env, pairs)
    barrier(env)

    wTo = env.dram["l0_out_wT"]
    for half in range(2):
        def g2(g, _h=half):
            for c in range(NCH):
                ld2.inc(g.dma_start(out=env.wbig[:, c, :], in_=wTo[c*128:(c+1)*128, _h*512:(_h+1)*512]), 16)
        blk.gpsimd(g2)
        ld_o = env.s_ld2.n
        barrier(env)
        mm_o0 = mm.n
        ve_o0 = ve.n
        def t3(t, _h=half, _ld=ld_o, _mm0=mm_o0, _ve0=ve_o0):
            t.wait_ge(env.s_ld2.h, _ld)
            for tc in range(NT):
                ps = env.ps_a if tc % 2 == 0 else env.ps_b
                for c in range(NCH):
                    t.matmul(ps[:, :], z_fm[:, c, tc*128:(tc+1)*128], env.wbig[:, c, :],
                             start=(c == 0), stop=False)
                mm.inc(t.matmul(ps[:, :], env.onesb[:, :], env.l0bo[:, _h*512:(_h+1)*512],
                         start=False, stop=True))
                if tc >= 1:
                    t.wait_ge(ve.h, _ve0 + tc)
        def v4(v, _h=half, _mm0=mm_o0):
            for tc in range(NT):
                v.wait_ge(mm.h, _mm0 + tc + 1)
                ps = env.ps_a if tc % 2 == 0 else env.ps_b
                ve.inc(v.tensor_tensor(env.stream[:, tc, _h*512:(_h+1)*512], env.stream[:, tc, _h*512:(_h+1)*512],
                                       ps[:, :], ALU.add))
        blk.tensor(t3); blk.vector(v4)
        barrier(env)


def _wquant_matrix(env, wT_dram, out_dram, nchunks, width, mcol):
    blk = env.blk
    ld, mm, ve, sc, gp = env.s_ld, env.s_mm, env.s_ve, env.s_sc, env.s_gp
    HW = 2048
    nh = width // HW if width >= HW else 1
    CW = min(width, HW)
    nld = nchunks * nh
    ve0 = ve.n
    ld_base = ld.n
    def floads(s):
        for k in range(nld):
            c, hh = divmod(k, nh)
            buf = env.scr1 if k % 2 == 0 else env.scr2
            if k >= 2:
                s.wait_ge(ve.h, ve0 + (k - 2 + 1))
            ld.inc(s.dma_start(out=buf[:, 0:CW], in_=wT_dram[c*128:(c+1)*128, hh*CW:(hh+1)*CW]), 16)
    blk.sync(floads)
    def vred(v):
        v.memset(env.wacc[:, 0:2], 0.0)
        for k in range(nld):
            buf = env.scr1 if k % 2 == 0 else env.scr2
            v.wait_ge(ld.h, ld_base + 16*(k+1))
            v.tensor_reduce(env.wacc[:, 2+k%2:3+k%2], buf[:, 0:CW], axis=AX, op=ALU.add, apply_absolute_value=True)
            if k >= 1:
                v.tensor_tensor(env.wacc[:, (k-1)%2:(k-1)%2+1], env.wacc[:, (k-1)%2:(k-1)%2+1], env.wacc[:, 2+(k-1)%2:3+(k-1)%2], ALU.add)
            ve.inc(v.tensor_copy(env.fceA[:, :], env.fceB[:, :]))
        v.tensor_scalar(out=env.wacc[:, 5:6], in0=env.wacc[:, 2:3], scalar1=1.0, scalar2=None, op0=ALU.mult)
        v.tensor_tensor(env.wacc[:, (nld-1)%2:(nld-1)%2+1], env.wacc[:, (nld-1)%2:(nld-1)%2+1], env.wacc[:, 2+(nld-1)%2:3+(nld-1)%2], ALU.add)
        v.tensor_scalar(out=env.wacc[:, 5:6], in0=env.wacc[:, 2:3], scalar1=1.0, scalar2=None, op0=ALU.mult)
        v.tensor_tensor(env.wacc[:, 4:5], env.wacc[:, 0:1], env.wacc[:, 1:2], ALU.add)
    blk.vector(vred)
    barrier(env)
    mm0 = mm.n
    def tsum(t):
        mm.inc(t.matmul(env.ps_c[0:1, 0:1], env.wacc[:, 4:5], env.onescol[:, :], start=True, stop=True))
    blk.tensor(tsum)
    mm_s = mm.n
    def vmean(v):
        v.wait_ge(mm.h, mm_s)
        ve.inc(v.tensor_scalar(out=env.wacc[0:1, 5:6], in0=env.ps_c[0:1, 0:1],
                               scalar1=1.0/(nchunks*128*width), scalar2=1e-5, op0=ALU.mult, op1=ALU.max))
    blk.vector(vmean)
    ve_m = ve.n
    def tb(t):
        t.wait_ge(ve.h, ve_m)
        mm.inc(t.matmul(env.ps_c[:, 0:1], env.onesf[:, :], env.wacc[0:1, 5:6], start=True, stop=True))
    blk.tensor(tb)
    mm_b = mm.n
    def vb(v):
        v.wait_ge(mm.h, mm_b)
        v.tensor_copy(env.wcols[:, mcol:mcol+1], env.ps_c[:, 0:1])
        ve.inc(v.reciprocal(env.wcols[:, mcol+1:mcol+2], env.ps_c[:, 0:1]))
    blk.vector(vb)
    barrier(env)
    ld_b2 = ld.n
    ve_q0 = ve.n
    gp_q0 = env.s_gd.n
    def floads2(s):
        for k in range(nld):
            c, hh = divmod(k, nh)
            buf = env.scr1 if k % 2 == 0 else env.scr2
            if k >= 2:
                s.wait_ge(ve.h, ve_q0 + (k - 2 + 1))
            ld.inc(s.dma_start(out=buf[:, 0:CW], in_=wT_dram[c*128:(c+1)*128, hh*CW:(hh+1)*CW]), 16)
    blk.sync(floads2)
    def vq(v):
        for k in range(nld):
            buf = env.scr1 if k % 2 == 0 else env.scr2
            stg = env.kf[:, k % 2, :]
            v.wait_ge(ld.h, ld_b2 + 16*(k+1))
            if k >= 2:
                v.wait_ge(env.s_gd.h, gp_q0 + 16*(k - 2 + 1))
            v.tensor_scalar(out=buf[:, 0:CW], in0=buf[:, 0:CW], scalar1=env.wcols[:, mcol+1:mcol+2], scalar2=RNE_C, op0=ALU.mult, op1=ALU.add)
            v.tensor_scalar(out=buf[:, 0:CW], in0=buf[:, 0:CW], scalar1=RNE_C, scalar2=None, op0=ALU.subtract)
            ve.inc(v.tensor_scalar(out=stg[0:128, 0:CW], in0=buf[:, 0:CW], scalar1=1.0, scalar2=-1.0, op0=ALU.min, op1=ALU.max))
    blk.vector(vq)
    def gst(g):
        for k in range(nld):
            c, hh = divmod(k, nh)
            stg = env.kf[:, k % 2, :]
            g.wait_ge(ve.h, ve_q0 + (k + 1))
            env.s_gd.inc(g.dma_start(out=out_dram[c, :, hh*CW:(hh+1)*CW], in_=stg[0:128, 0:CW]), 16)
    blk.gpsimd(gst)
    barrier(env)


def p_l0_ffn(env):
    blk = env.blk
    ld, ld2, mm, ve, sc, gp = env.s_ld, env.s_ld2, env.s_mm, env.s_ve, env.s_sc, env.s_gp
    ve2, sc2 = env.s_ve2, env.s_sc2
    stat, scr1, scr2 = env.stat, env.scr1, env.scr2
    emit_layernorm(env, need_fm=False)

    _wquant_matrix(env, env.dram["l0f_wgT"], env.wqg_dram, NCH, 4096, 0)
    _wquant_matrix(env, env.dram["l0f_wuT"], env.wqu_dram, NCH, 4096, 2)
    _wquant_matrix(env, env.dram["l0f_wdT"], env.wqd_dram, 32, 1024, 4)

    xq_fm = env.xln_fm
    xqt = env.vtm.ap().rearrange("p t c -> p (t c)")[:, 0:1024]
    for tc in range(NT):
        ve_t = ve.n
        mm_prev = mm.n
        def va(v, _tc=tc, _mmp=mm_prev):
            if _tc >= 1:
                v.wait_ge(mm.h, _mmp)
            v.tensor_scalar(out=scr1[:, 0:1024], in0=env.stream[:, _tc, :], scalar1=stat[:, _tc, 1:2], scalar2=stat[:, _tc, 4:5], op0=ALU.mult, op1=ALU.add)
            v.tensor_reduce(stat[:, _tc, 0:1], scr1[:, 0:1024], axis=AX, op=ALU.max, apply_absolute_value=True)
            v.tensor_scalar(out=scr1[:, 0:1024], in0=scr1[:, 0:1024], scalar1=1.0, scalar2=None, op0=ALU.mult)
            v.tensor_scalar(out=stat[:, _tc, 3:4], in0=stat[:, _tc, 0:1], scalar1=1e-5, scalar2=1.0/127.0, op0=ALU.max, op1=ALU.mult)
            v.tensor_scalar(out=scr2[:, 0:1024], in0=scr1[:, 0:1024], scalar1=1.0, scalar2=None, op0=ALU.mult)
            v.reciprocal(stat[:, _tc, 2:3], stat[:, _tc, 3:4])
            v.tensor_scalar(out=scr2[:, 0:1024], in0=scr1[:, 0:1024], scalar1=1.0, scalar2=None, op0=ALU.mult)
            v.tensor_scalar(out=scr1[:, 0:1024], in0=scr1[:, 0:1024], scalar1=stat[:, _tc, 2:3], scalar2=RNE_C, op0=ALU.mult, op1=ALU.add)
            v.tensor_scalar(out=scr1[:, 0:1024], in0=scr1[:, 0:1024], scalar1=RNE_C, scalar2=None, op0=ALU.subtract)
            ve.inc(v.tensor_scalar(out=xqt, in0=scr1[:, 0:1024], scalar1=127.0, scalar2=-128.0, op0=ALU.min, op1=ALU.max))
        blk.vector(va)
        pairs = []
        for c in range(NCH):
            pairs.append((xqt.rearrange("p (c x) -> p c x", c=NCH)[:, c, :], xq_fm[:, c, tc*128:(tc+1)*128]))
        def tw(t, _v=ve.n):
            t.wait_ge(ve.h, _v)
        blk.tensor(tw)
        transpose_tiles(env, pairs)
    barrier(env)

    _l0ffn_pass1(env)
    _l0ffn_pass2(env)


def _l0ffn_pass1(env):
    blk = env.blk
    ld, mm, ve, sc, gp = env.s_ld, env.s_mm, env.s_ve, env.s_sc, env.s_gp
    stat, scr1, scr2 = env.stat, env.scr1, env.scr2
    xq_fm = env.xln_fm
    wsl = env.xln_tm.ap().rearrange("p t c -> p (t c)")
    wg_sl = wsl[:, 0:8192].rearrange("p (c i) -> p c i", c=NCH)
    wu_sl = wsl[:, 8192:16384].rearrange("p (c i) -> p c i", c=NCH)
    gtile = env.ktm.ap().rearrange("p t c -> p (t c)")[:, 0:1024]
    utile = env.ktm.ap().rearrange("p t c -> p (t c)")[:, 1024:2048]
    h2stg = env.ktm.ap().rearrange("p t c -> p (t c)")[:, 2048:3072]
    def vz(v):
        for tc in range(NT):
            v.memset(stat[:, tc, 5:6], 0.0)
        for tc in range(NT):
            v.tensor_tensor(env.stat2[:, tc, 0:1], stat[:, tc, 3:4], env.wcols[:, 0:1], ALU.mult)
        for tc in range(NT):
            ve.inc(v.tensor_tensor(env.stat2[:, tc, 1:2], stat[:, tc, 3:4], env.wcols[:, 2:3], ALU.mult))
    blk.vector(vz)
    barrier(env)
    for isl in range(4):
        ld0 = ld.n
        def fw(s, _i=isl):
            for c in range(NCH):
                ld.inc(s.dma_start(out=wg_sl[:, c, :], in_=env.wqg_dram[c, :, _i*1024:(_i+1)*1024]), 16)
                ld.inc(s.dma_start(out=wu_sl[:, c, :], in_=env.wqu_dram[c, :, _i*1024:(_i+1)*1024]), 16)
        blk.sync(fw)
        ld_w = ld.n
        mm0 = mm.n
        sc_e0 = sc.n
        ve_e0 = ve.n
        gp_h0 = env.s_gd.n
        def tmm(t, _ld=ld_w, _sc0=sc_e0, _mm0=mm0):
            t.wait_ge(ld.h, _ld)
            for tc in range(NT):
                for half in range(2):
                    for role in range(2):
                        w = (wg_sl, wu_sl)[role]
                        ps = (env.ps_a, env.ps_b)[role]
                        k4 = tc*4 + half*2 + role
                        if k4 >= 2:
                            t.wait_ge(sc.h, _sc0 + (k4 - 2) + 1)
                        for c in range(NCH):
                            last = (c == NCH-1)
                            ins = t.matmul(ps[:, :], xq_fm[:, c, tc*128:(tc+1)*128], w[:, c, half*512:(half+1)*512],
                                           start=(c == 0), stop=last)
                            if last:
                                mm.inc(ins)
        def sevac(s, _mm0=mm0, _sc0=sc_e0, _ve0=ve_e0):
            for tc in range(NT):
                if tc >= 1:
                    s.wait_ge(ve.h, _ve0 + tc)
                for half in range(2):
                    s.wait_ge(mm.h, _mm0 + tc*4 + half*2 + 1)
                    sc.inc(s.activation(out=gtile.rearrange("p (h x) -> p h x", h=2)[:, half, :], in_=env.ps_a[:, :],
                                        func=AF.Silu, scale=env.stat2[:, tc, 0:1]))
                    s.wait_ge(mm.h, _mm0 + tc*4 + half*2 + 2)
                    sc.inc(s.activation(out=utile.rearrange("p (h x) -> p h x", h=2)[:, half, :], in_=env.ps_b[:, :],
                                        func=AF.Identity, scale=env.stat2[:, tc, 1:2]))
        def vprod(v, _sc0=sc_e0, _ve0=ve_e0, _gp0=gp_h0):
            for tc in range(NT):
                v.wait_ge(sc.h, _sc0 + tc*4 + 4)
                if tc >= 1:
                    v.wait_ge(env.s_gd.h, _gp0 + 16*tc)
                v.tensor_tensor(scr1[:, 0:1024], gtile, utile, ALU.mult)
                v.tensor_reduce(stat[:, tc, 7:8], scr1[:, 0:1024], axis=AX, op=ALU.max, apply_absolute_value=True)
                v.tensor_copy(h2stg, scr1[:, 0:1024])
                ve.inc(v.tensor_tensor(stat[:, tc, 5:6], stat[:, tc, 5:6], stat[:, tc, 7:8], ALU.max))
        def gh2(g, _ve0=ve_e0, _i=isl):
            for tc in range(NT):
                g.wait_ge(ve.h, _ve0 + tc + 1)
                env.s_gd.inc(g.dma_start(out=env.h2_dram[tc, _i, :, :], in_=h2stg), 16)
        blk.tensor(tmm); blk.scalar(sevac); blk.vector(vprod); blk.gpsimd(gh2)
        barrier(env)
    def vf(v):
        for tc in range(NT):
            v.tensor_scalar(out=stat[:, tc, 3:4], in0=stat[:, tc, 5:6], scalar1=1e-5, scalar2=1.0/127.0, op0=ALU.max, op1=ALU.mult)
        for tc in range(NT):
            v.reciprocal(stat[:, tc, 2:3], stat[:, tc, 3:4])
        for tc in range(NT):
            ve.inc(v.tensor_tensor(env.stat2[:, tc, 2:3], stat[:, tc, 3:4], env.wcols[:, 4:5], ALU.mult))
    blk.vector(vf)
    barrier(env)


def _l0ffn_pass2(env):
    blk = env.blk
    ld, mm, ve, sc, gp = env.s_ld, env.s_mm, env.s_ve, env.s_sc, env.s_gp
    stat, scr1 = env.stat, env.scr1
    wsl = env.xln_tm.ap().rearrange("p t c -> p (t c)")
    wd_sl = wsl[:, 0:8192].rearrange("p (c o) -> p c o", c=NCH)
    h2t = env.ktm.ap().rearrange("p t c -> p (t c)")[:, 0:1024]
    xq2t = env.ktm.ap().rearrange("p t c -> p (t c)")[:, 1024:2048]
    xq2fm = env.vtm.ap().rearrange("p t c -> p (t c)")[:, 0:1024].rearrange("p (c x) -> p c x", c=8)
    for ib in range(4):
        ld0 = ld.n
        def fw(s, _ib=ib):
            for c in range(NCH):
                ld.inc(s.dma_start(out=wd_sl[:, c, :], in_=env.wqd_dram[_ib*8 + c, :, :]), 16)
        blk.sync(fw)
        ld_w = ld.n
        for tc in range(NT):
            ld2_0 = env.s_ld2.n
            ve_prevq = ve.n
            def gh(g, _ib=ib, _tc=tc, _vp=ve_prevq):
                if _ib + _tc > 0:
                    g.wait_ge(ve.h, _vp)
                g.dma_start(out=h2t, in_=env.h2_dram[_tc, _ib, :, :]).then_inc(env.s_ld2.h, 16)
            env.s_ld2.n += 16
            blk.gpsimd(gh)
            ve_q = ve.n
            mm_prevt = mm.n
            def vq(v, _tc=tc, _l2=env.s_ld2.n, _mmp=mm_prevt):
                v.wait_ge(env.s_ld2.h, _l2)
                v.wait_ge(mm.h, _mmp)
                v.tensor_scalar(out=scr1[:, 0:1024], in0=h2t, scalar1=stat[:, _tc, 2:3], scalar2=RNE_C, op0=ALU.mult, op1=ALU.add)
                v.tensor_scalar(out=scr1[:, 0:1024], in0=scr1[:, 0:1024], scalar1=RNE_C, scalar2=None, op0=ALU.subtract)
                ve.inc(v.tensor_scalar(out=xq2t, in0=scr1[:, 0:1024], scalar1=127.0, scalar2=-128.0, op0=ALU.min, op1=ALU.max))
            blk.vector(vq)
            ve_done = ve.n
            def twt(t, _v=ve_done):
                t.wait_ge(ve.h, _v)
            blk.tensor(twt)
            pairs = []
            for c in range(NCH):
                pairs.append((xq2t.rearrange("p (c x) -> p c x", c=NCH)[:, c, :], xq2fm[:, c, :]))
            transpose_tiles(env, pairs)
            mm0 = mm.n
            def tdn(t, _ld=ld_w, _tc=tc):
                t.wait_ge(ld.h, _ld)
                for half in range(2):
                    ps = (env.ps_a, env.ps_b)[half]
                    for c in range(NCH):
                        ins = t.matmul(ps[:, :], xq2fm[:, c, :], wd_sl[:, c, half*512:(half+1)*512],
                                       start=(c == 0), stop=(c == NCH-1))
                        if c == NCH-1:
                            mm.inc(ins)
            blk.tensor(tdn)
            mm_d = mm.n
            def vadd(v, _tc=tc, _mm=mm_d):
                v.wait_ge(mm.h, _mm - 1)
                v.scalar_tensor_tensor(out=env.stream[:, _tc, 0:512], in0=env.ps_a[:, :], scalar=stat[:, _tc, 6:7] if False else env.stat2[:, _tc, 2:3],
                                       in1=env.stream[:, _tc, 0:512], op0=ALU.mult, op1=ALU.add)
                v.wait_ge(mm.h, _mm)
                ve.inc(v.scalar_tensor_tensor(out=env.stream[:, _tc, 512:1024], in0=env.ps_b[:, :], scalar=env.stat2[:, _tc, 2:3],
                                       in1=env.stream[:, _tc, 512:1024], op0=ALU.mult, op1=ALU.add))
            blk.vector(vadd)
            ve_a = ve.n
            def tg(t, _v=ve_a):
                t.wait_ge(ve.h, _v)
            blk.tensor(tg)
        barrier(env)


def p_l1_attn(env):
    blk = env.blk
    ld2, mm, ve, sc = env.s_ld2, env.s_mm, env.s_ve, env.s_sc
    ve2, sc2, mm2 = env.s_ve2, env.s_sc2, env.s_mm2
    stat, scr1, scr2 = env.stat, env.scr1, env.scr2
    emit_layernorm(env, need_fm=True)
    barrier(env)
    SCL = float(128.0 ** -0.5)

    kfm = env.qf.ap()[:, 0, :]
    ktmp = env.qf.ap()[:, 1, :]
    qfm = env.kf
    cosr = env.cosf
    sinr = env.sinf
    wq_sb = env.wpro
    attn_tm = env.ktm
    dmask = env.maskh
    qraw = scr2[:, 0:1024].bitcast(BF16)
    probs_b = scr2[:, 0:1024].bitcast(BF16)

    def gld(g):
        ld2.inc(g.dma_start(out=cosr[0:64, :], in_=env.dram["c_rope_cos"][:, :]), 16)
        ld2.inc(g.dma_start(out=cosr[64:128, :], in_=env.dram["c_rope_cos"][:, :]), 16)
        ld2.inc(g.dma_start(out=sinr[0:64, :], in_=env.dram["c_rope_sin"][:, :]), 16)
        ld2.inc(g.dma_start(out=sinr[64:128, :], in_=env.dram["c_rope_sin"][:, :]), 16)
        ld2.inc(g.dma_start(out=dmask[:, :], in_=env.dram["c_diagmask"][:, :]), 16)
        for c in range(NCH):
            ld2.inc(g.dma_start(out=wq_sb[:, c, :], in_=env.dram["l1_wkT"][c*128:(c+1)*128, :]), 16)
    blk.gpsimd(gld)
    ld_k = ld2.n

    mm0 = mm.n
    def tk(t):
        t.wait_ge(ld2.h, ld_k)
        for n4 in range(4):
            ps = env.ps_a if n4 % 2 == 0 else env.ps_b
            if n4 >= 2:
                t.wait_ge(ve.h, ve_k0 + (n4 - 2 + 1))
            for c in range(NCH):
                mm.inc(t.matmul(ps[:, 0:512], wq_sb[:, c, :], env.xln_fm[:, c, n4*512:(n4+1)*512],
                                start=(c == 0), stop=(c == NCH-1)))
    ve_k0 = ve.n
    def vk(v):
        for n4 in range(4):
            ps = env.ps_a if n4 % 2 == 0 else env.ps_b
            v.wait_ge(mm.h, mm0 + (n4+1)*NCH)
            ve.inc(v.tensor_copy(ktmp.rearrange("p (a b) -> p a b", a=4)[:, n4, :], ps[:, 0:512]))
    blk.tensor(tk); blk.vector(vk)
    ve_kraw = ve.n
    kswp = scr1[:, 0:1024].bitcast(BF16)
    rtmp = scr1[:, 1024:2048].bitcast(BF16)
    def gkswp(g):
        g.wait_ge(ve.h, ve_kraw)
        ld2.inc(g.dma_start(out=kswp[0:64, :], in_=ktmp[64:128, :]), 16)
        ld2.inc(g.dma_start(out=kswp[64:128, :], in_=ktmp[0:64, :]), 16)
    blk.gpsimd(gkswp)
    ld_kswp = ld2.n
    def vrot(v):
        v.wait_ge(ld2.h, ld_kswp)
        v.tensor_tensor(kfm[:, :], ktmp[:, :], cosr[:, :], ALU.mult)
        v.tensor_tensor(rtmp[:, :], kswp[:, :], sinr[:, :], ALU.mult)
        v.tensor_tensor(kfm[0:64, :], kfm[0:64, :], rtmp[0:64, :], ALU.subtract)
        ve.inc(v.tensor_tensor(kfm[64:128, :], kfm[64:128, :], rtmp[64:128, :], ALU.add))
    blk.vector(vrot)

    def gv(g):
        g.wait_ge(mm.h, mm0 + 32)
        for c in range(NCH):
            ld2.inc(g.dma_start(out=wq_sb[:, c, :], in_=env.dram["l1_wvT"][c*128:(c+1)*128, :]), 16)
    blk.gpsimd(gv)
    ld_v = ld2.n
    mm_v0 = mm.n
    def tv(t):
        t.wait_ge(ld2.h, ld_v)
        for tc in range(NT):
            for c in range(NCH):
                mm.inc(t.matmul(env.ps_c[:, :], env.xln_fm[:, c, tc*128:(tc+1)*128], wq_sb[:, c, :],
                                start=(c == 0), stop=(c == NCH-1)))
            t.wait_ge(ve2.h, ve2_v0 + tc + 1)
    ve2_v0 = ve2.n
    def vv(v):
        for tc in range(NT):
            v.wait_ge(mm.h, mm_v0 + (tc+1)*NCH)
            ve2.inc(v.tensor_copy(env.vtm[:, tc, :], env.ps_c[:, :]))
    blk.tensor(tv); blk.vector(vv)
    barrier(env)

    for hh in range(2):
        mm_qprev = mm.n
        def gq(g, _hh=hh, _mmp=mm_qprev):
            g.wait_ge(mm.h, _mmp)
            for c in range(NCH):
                ld2.inc(g.dma_start(out=wq_sb[:, c, :], in_=env.dram["l1_wqT"][_hh, c*128:(c+1)*128, :]), 16)
        blk.gpsimd(gq)
        ld_q = ld2.n
        mm_q0 = mm.n
        ve_qpre = ve.n
        def tq(t, _ld=ld_q, _vq=ve_qpre):
            t.wait_ge(ld2.h, _ld)
            t.wait_ge(ve.h, _vq)
            for n4 in range(4):
                ps = env.ps_a if n4 % 2 == 0 else env.ps_b
                if n4 >= 2:
                    t.wait_ge(ve.h, vq_e0 + (n4 - 2 + 1))
                for c in range(NCH):
                    mm.inc(t.matmul(ps[:, 0:512], wq_sb[:, c, :], env.xln_fm[:, c, n4*512:(n4+1)*512],
                                    start=(c == 0), stop=(c == NCH-1)))
        vq_e0 = ve.n
        def vq(v, _mm0=mm_q0):
            for n4 in range(4):
                ps = env.ps_a if n4 % 2 == 0 else env.ps_b
                v.wait_ge(mm.h, _mm0 + (n4+1)*NCH)
                ve.inc(v.tensor_copy(qraw.rearrange("p (a b) -> p a b", a=4)[:, n4, :], ps[:, 0:512]))
        blk.tensor(tq); blk.vector(vq)
        ve_qraw = ve.n
        def gqswp(g, _vq=ve_qraw):
            g.wait_ge(ve.h, _vq)
            ld2.inc(g.dma_start(out=kswp[0:64, :], in_=qraw[64:128, :]), 16)
            ld2.inc(g.dma_start(out=kswp[64:128, :], in_=qraw[0:64, :]), 16)
        blk.gpsimd(gqswp)
        ld_qswp = ld2.n
        def vqr(v, _hh=hh, _lq=ld_qswp):
            v.wait_ge(ld2.h, _lq)
            dst = qfm[:, _hh, :]
            v.tensor_tensor(dst[:, :], qraw[:, :], cosr[:, :], ALU.mult)
            v.tensor_tensor(rtmp[:, :], kswp[:, :], sinr[:, :], ALU.mult)
            v.tensor_tensor(dst[0:64, :], dst[0:64, :], rtmp[0:64, :], ALU.subtract)
            ve.inc(v.tensor_tensor(dst[64:128, :], dst[64:128, :], rtmp[64:128, :], ALU.add))
        blk.vector(vqr)
    barrier(env)

    prev_se = sc.n
    prev_tp = mm2.n
    prev_av = mm.n
    for hh in range(2):
        for qc in range(NT):
            mm_s = mm.n
            ve_pre = ve.n
            def ts(t, _hh=hh, _qc=qc, _vp=ve_pre):
                _kmax = (_qc + 1) * 128
                t.wait_ge(ve.h, _vp)
                for ks in range((_kmax + 511) // 512):
                    kw = min(512, _kmax - ks*512)
                    ps = env.ps_a if ks % 2 == 0 else env.ps_b
                    if ks >= 2:
                        t.wait_ge(ve.h, ve_sc0 + (ks - 2 + 1))
                    mm.inc(t.matmul(ps[:, 0:kw], qfm[:, _hh, _qc*128:(_qc+1)*128], kfm[:, ks*512:ks*512+kw],
                                    start=True, stop=True))
            ve_sc0 = ve.n
            def vs(v, _qc=qc, _mm=mm_s, _psc=prev_se):
                _kmax = (_qc + 1) * 128
                v.wait_ge(sc.h, _psc)
                for ks in range((_kmax + 511) // 512):
                    kw = min(512, _kmax - ks*512)
                    ps = env.ps_a if ks % 2 == 0 else env.ps_b
                    v.wait_ge(mm.h, _mm + ks + 1)
                    if ks*512 + kw == _kmax:
                        if kw > 128:
                            v.tensor_copy(scr1[:, ks*512:ks*512+kw-128], ps[:, 0:kw-128])
                        ve.inc(v.tensor_tensor(scr1[:, _kmax-128:_kmax], ps[:, kw-128:kw], dmask[:, :], ALU.add))
                    else:
                        ve.inc(v.tensor_copy(scr1[:, ks*512:ks*512+kw], ps[:, 0:kw]))
            blk.tensor(ts); blk.vector(vs)
            def vm(v, _qc=qc):
                _kmax = (_qc + 1) * 128
                v.tensor_reduce(stat[:, 0, 2:3], scr1[:, 0:_kmax], axis=AX, op=ALU.max)
                v.tensor_copy(env.fceA[:, :], env.fceB[:, :])
                ve.inc(v.tensor_scalar(out=stat[:, 0, 3:4], in0=stat[:, 0, 2:3], scalar1=-SCL, scalar2=None, op0=ALU.mult))
            blk.vector(vm)
            ve_m = ve.n
            def se(s, _qc=qc, _ptp=prev_tp, _vm=ve_m):
                _kmax = (_qc + 1) * 128
                s.wait_ge(ve.h, _vm)
                s.wait_ge(mm2.h, _ptp)
                sc.inc(s.activation(out=probs_b[:, 0:_kmax], in_=scr1[:, 0:_kmax], func=AF.Exp,
                                    bias=stat[:, 0, 3:4], scale=SCL, accum_out=stat[:, 0, 4:5]))
            blk.scalar(se)
            prev_se = sc.n
            sc_edone = sc.n
            mm2_t0 = mm2.n
            def tav(t, _qc=qc, _sce=sc_edone):
                t.wait_ge(sc.h, _sce)
                for kt in range(_qc + 1):
                    if kt >= 2:
                        t.wait_ge(sc2.h, sc2_t0 + (kt - 2 + 1))
                    mm2.inc(t.transpose(env.ps_t[kt % 2][:, :], probs_b[:, kt*128:(kt+1)*128], env.identb[:]))
            sc2_t0 = sc2.n
            def s2t(s, _qc=qc, _m2=mm2_t0, _pav=prev_av):
                s.wait_ge(mm.h, _pav)
                for kt in range(_qc + 1):
                    s.wait_ge(mm2.h, _m2 + kt + 1)
                    sc2.inc(s.copy(out=env.Sp[:, kt, :], in_=env.ps_t[kt % 2][:, :]))
            blk.tensor(tav); blk.scalar(s2t)
            prev_tp = mm2.n
            def tav2(t, _hh=hh, _qc=qc, _s2=sc2_t0):
                psv = env.ps_c if (_hh*NT+_qc) % 2 == 0 else env.ps_d
                for kt in range(_qc + 1):
                    t.wait_ge(sc2.h, _s2 + kt + 1)
                    mm.inc(t.matmul(psv[:, :], env.Sp[:, kt, :], env.vtm[:, kt, :],
                                    start=(kt == 0), stop=(kt == _qc)))
            blk.tensor(tav2)
            mm_av = mm.n
            prev_av = mm.n
            def vf(v, _hh=hh, _qc=qc, _mma=mm_av):
                psv = env.ps_c if (_hh*NT+_qc) % 2 == 0 else env.ps_d
                v.wait_ge(mm.h, _mma)
                v.reciprocal(stat[:, 0, 5:6], stat[:, 0, 4:5])
                v.tensor_copy(env.fceA[:, :], env.fceB[:, :])
                v.tensor_tensor(stat[:, 0, 6:7], stat[:, 0, 4:5], stat[:, 0, 5:6], ALU.mult)
                v.tensor_copy(env.fceA[:, :], env.fceB[:, :])
                v.tensor_scalar(out=stat[:, 0, 6:7], in0=stat[:, 0, 6:7], scalar1=-1.0, scalar2=2.0, op0=ALU.mult, op1=ALU.add)
                v.tensor_copy(env.fceA[:, :], env.fceB[:, :])
                v.tensor_tensor(stat[:, 0, 5:6], stat[:, 0, 5:6], stat[:, 0, 6:7], ALU.mult)
                v.tensor_copy(env.fceA[:, :], env.fceB[:, :])
                ve2.inc(v.tensor_scalar(out=attn_tm[:, _qc, _hh*128:(_hh+1)*128], in0=psv[:, :],
                                        scalar1=stat[:, 0, 5:6], scalar2=None, op0=ALU.mult))
            blk.vector(vf)
            ve2_f = ve2.n
            def tg(t, _v=ve2_f):
                t.wait_ge(ve2.h, _v)
            blk.tensor(tg)
    barrier(env)

    attn_fm = env.qf
    pairs = []
    for ch in range(2):
        for tc in range(NT):
            pairs.append((attn_tm[:, tc, ch*128:(ch+1)*128], attn_fm[:, ch, tc*128:(tc+1)*128]))
    transpose_tiles(env, pairs)
    barrier(env)
    wo_sb = env.wbig.ap().rearrange("p c x -> p (c x)")[:, 0:2048].rearrange("p (c x) -> p c x", c=2)
    def gw(g):
        for ch in range(2):
            ld2.inc(g.dma_start(out=wo_sb[:, ch, :], in_=env.dram["l1_woT"][ch*128:(ch+1)*128, :]), 16)
    blk.gpsimd(gw)
    ld_o = ld2.n
    mm_o0 = mm.n
    ve_o0 = ve.n
    def to(t):
        t.wait_ge(ld2.h, ld_o)
        for tc in range(NT):
            if tc >= 1:
                t.wait_ge(ve.h, ve_o0 + tc)
            for half in range(2):
                ps = env.ps_a if half == 0 else env.ps_b
                for ch in range(2):
                    ins = t.matmul(ps[:, :], attn_fm[:, ch, tc*128:(tc+1)*128], wo_sb[:, ch, half*512:(half+1)*512],
                                   start=(ch == 0), stop=(ch == 1))
                    if ch == 1:
                        mm.inc(ins)
    def vo(v):
        for tc in range(NT):
            v.wait_ge(mm.h, mm_o0 + 2*tc + 1)
            v.tensor_tensor(env.stream[:, tc, 0:512], env.stream[:, tc, 0:512], env.ps_a[:, :], ALU.add)
            v.wait_ge(mm.h, mm_o0 + 2*tc + 2)
            ve.inc(v.tensor_tensor(env.stream[:, tc, 512:1024], env.stream[:, tc, 512:1024], env.ps_b[:, :], ALU.add))
    blk.tensor(to); blk.vector(vo)
    barrier(env)


def p_l1_ffn(env):
    blk = env.blk
    ld2, mm, ve, sc, gp = env.s_ld2, env.s_mm, env.s_ve, env.s_sc, env.s_gp
    ve2, sc2 = env.s_ve2, env.s_sc2
    gs = env.s_gd
    stat2 = env.stat2
    emit_layernorm(env, need_fm=True)
    barrier(env)
    N_TOK = 16384.0

    st2 = stat2.ap().rearrange("p a b -> p (a b)")
    gmb = env.l0bias_col
    wqv = env.wbig.ap().rearrange("p c x -> p (c x)")[:, 0:2048].rearrange("p (a c x) -> p a c x", a=2, c=8)
    keysv = env.vtm.ap().rearrange("p a b -> p (a b)")[:, 0:512].rearrange("p (a b) -> p a b", a=2)
    qhat = env.ktm.ap().rearrange("p a b -> p (a b)")[:, 0:2048]
    scr1, scr2 = env.scr1, env.scr2
    sxF = env.qf.ap().rearrange("p a b -> p (a b)").bitcast(F32)
    sxI = env.qf.ap().rearrange("p a b -> p (a b)").bitcast(I32)
    ixI = env.kf.ap().rearrange("p a b -> p (a b)").bitcast(I32)
    w8F = env.sinf.ap().bitcast(F32)
    idxI = env.cosf.ap().bitcast(I32)
    ring = env.xln_fm
    iot = env.iota256
    ztF = env.zt.ap().bitcast(F32)
    ztI = env.zt.ap().bitcast(I32)

    def gld(g):
        ld2.inc(g.dma_start(out=gmb[:, 0:16], in_=env.dram["l1f_gamma"][:, :]), 16)
        ld2.inc(g.dma_start(out=gmb[:, 16:32], in_=env.dram["l1f_beta"][:, :]), 16)
        ld2.inc(g.dma_start(out=iot[:, :], in_=env.dram["c_iota256"][:, :]), 16)
    blk.gpsimd(gld)

    p1_mm0 = mm.n
    p1_ve0 = ve.n
    p1_qmm_end = []
    for c in range(16):
        mmp = (p1_qmm_end[c-2] if c >= 2 else p1_mm0)
        def gw(g, _c=c, _mmp=mmp):
            g.wait_ge(mm.h, _mmp)
            for dch in range(NCH):
                ld2.inc(g.dma_start(out=wqv[:, _c % 2, dch, :],
                                    in_=env.dram["l1f_wq"][dch*128:(dch+1)*128, _c*128:(_c+1)*128]), 16)
        blk.gpsimd(gw)
        ld_w = ld2.n
        def tq(t, _c=c, _ld=ld_w):
            t.wait_ge(ld2.h, _ld)
            for n4 in range(4):
                gq = _c * 4 + n4
                ps = env.ps_a if gq % 2 == 0 else env.ps_b
                if gq >= 2:
                    t.wait_ge(ve.h, p1_ve0 + gq - 1)
                for dch in range(NCH):
                    mm.inc(t.matmul(ps[:, 0:512], wqv[:, _c % 2, dch, :],
                                    env.xln_fm[:, dch, n4*512:(n4+1)*512],
                                    start=(dch == 0), stop=(dch == NCH-1)))
        blk.tensor(tq)
        p1_qmm_end.append(mm.n)
        def vq(v, _c=c):
            for n4 in range(4):
                gq = _c * 4 + n4
                ps = env.ps_a if gq % 2 == 0 else env.ps_b
                v.wait_ge(mm.h, p1_mm0 + (gq+1)*NCH)
                ve.inc(v.tensor_copy(qhat[:, n4*512:(n4+1)*512], ps[:, 0:512]))
            v.reduce_sum(st2[:, _c:_c+1], qhat[:, :], axis=AX)
            ve2.inc(v.scalar_tensor_tensor(out=scr1[:, :], in0=qhat[:, :], scalar=1.0,
                                           in1=qhat[:, :], op0=ALU.mult, op1=ALU.mult,
                                           accum_out=st2[:, 16+_c:17+_c]))
        blk.vector(vq)
    def vfin(v):
        ve2.inc(v.tensor_copy(scr1[:, 0:512], scr1[:, 512:1024]))
    blk.vector(vfin)
    p1_done = ve2.n

    def gbn(g):
        g.wait_ge(ve2.h, p1_done)
        gs.inc(g.dma_start(out=env.bn_in[:, :], in_=st2[:, 0:32]), 16)
        g.wait_ge(gs.h, gs.n)
        gp.inc(g.collective_compute("AllReduce", ALU.add,
                                    replica_groups=[env.bn_group],
                                    ins=[env.bn_in[:, :]], outs=[env.bn_out[:, :]]), 1)
        g.wait_ge(gp.h, gp.n)
        ld2.inc(g.dma_start(out=st2[:, 0:32], in_=env.bn_out[:, :]), 16)
    blk.gpsimd(gbn)
    ld_bn = ld2.n

    def vab(v):
        def sp():
            for _ in range(6):
                v.tensor_copy(env.fceA[:, :], env.fceB[:, :])
        v.wait_ge(ld2.h, ld_bn)
        v.tensor_scalar(out=scr1[:, 0:16], in0=st2[:, 0:16], scalar1=1.0/N_TOK, scalar2=None, op0=ALU.mult)
        v.tensor_scalar(out=scr1[:, 16:32], in0=st2[:, 16:32], scalar1=1.0/N_TOK, scalar2=1e-5, op0=ALU.mult, op1=ALU.add)
        sp()
        v.tensor_tensor(scr1[:, 32:48], scr1[:, 0:16], scr1[:, 0:16], ALU.mult)
        sp()
        ve.inc(v.tensor_tensor(scr1[:, 48:64], scr1[:, 16:32], scr1[:, 32:48], ALU.subtract))
    blk.vector(vab)
    ve_var = ve.n
    def sab(s):
        s.wait_ge(ve.h, ve_var)
        sc.inc(s.activation(out=scr1[:, 64:80], in_=scr1[:, 48:64], func=AF.Sqrt))
    blk.scalar(sab)
    sc_sqrt = sc.n
    def vab2(v):
        def sp():
            for _ in range(6):
                v.tensor_copy(env.fceA[:, :], env.fceB[:, :])
        v.wait_ge(sc.h, sc_sqrt)
        v.reciprocal(scr1[:, 80:96], scr1[:, 64:80])
        sp()
        v.tensor_tensor(scr1[:, 96:112], scr1[:, 64:80], scr1[:, 80:96], ALU.mult)
        sp()
        v.tensor_scalar(out=scr1[:, 112:128], in0=scr1[:, 96:112], scalar1=-1.0, scalar2=2.0, op0=ALU.mult, op1=ALU.add)
        sp()
        v.tensor_tensor(scr1[:, 128:144], scr1[:, 80:96], scr1[:, 112:128], ALU.mult)
        sp()
        v.tensor_tensor(st2[:, 32:48], gmb[:, 0:16], scr1[:, 128:144], ALU.mult)
        sp()
        v.tensor_tensor(scr1[:, 144:160], scr1[:, 0:16], st2[:, 32:48], ALU.mult)
        sp()
        v.tensor_tensor(st2[:, 48:64], gmb[:, 16:32], scr1[:, 144:160], ALU.subtract)
        sp()
        ve2.inc(v.tensor_copy(scr1[:, 0:512], scr1[:, 512:1024]))
    blk.vector(vab2)
    ab_done = ve2.n

    p2_mm0 = mm.n
    p1_all_ve = None
    p2_sc0 = sc.n
    p2_qmm_end = []
    p2_sim_end = []
    p2_v20 = ve2.n
    pk_ve0 = ve.n
    for c in range(16):
        p, h = c // 8, c % 8
        mmp = (p2_qmm_end[c-2] if c >= 2 else (p1_qmm_end[-1] if p1_qmm_end else p2_mm0))
        mmk = (p2_sim_end[c-2] if c >= 2 else p2_mm0)
        def gw2(g, _c=c, _mmp=mmp, _mmk=mmk):
            g.wait_ge(mm.h, _mmp)
            for dch in range(NCH):
                ld2.inc(g.dma_start(out=wqv[:, _c % 2, dch, :],
                                    in_=env.dram["l1f_wq"][dch*128:(dch+1)*128, _c*128:(_c+1)*128]), 16)
            g.wait_ge(mm.h, _mmk)
            ld2.inc(g.dma_start(out=keysv[:, _c % 2, :], in_=env.dram["l1f_keysT"][_c, :, :]), 16)
        blk.gpsimd(gw2)
        ld_w = ld2.n
        mm_q0 = mm.n
        def tq2(t, _c=c, _ld=ld_w, _m0=mm_q0):
            t.wait_ge(ld2.h, _ld)
            if _c == 0:
                t.wait_ge(ve.h, pk_ve0)
            for n4 in range(4):
                gq = _c * 4 + n4
                ps = env.ps_a if gq % 2 == 0 else env.ps_b
                if gq >= 2:
                    t.wait_ge(sc.h, p2_sc0 + gq - 1)
                if n4 < 2 and _c > 0:
                    t.wait_ge(ve.h, pk_ve0 + _c * 16)
                for dch in range(NCH):
                    mm.inc(t.matmul(ps[:, 0:512], wqv[:, _c % 2, dch, :],
                                    env.xln_fm[:, dch, n4*512:(n4+1)*512],
                                    start=(dch == 0), stop=(dch == NCH-1)))
        blk.tensor(tq2)
        p2_qmm_end.append(mm.n)
        def saf(s, _c=c, _m0=mm_q0):
            if _c == 0:
                s.wait_ge(ve2.h, ab_done)
            for n4 in range(4):
                gq = _c * 4 + n4
                ps = env.ps_a if gq % 2 == 0 else env.ps_b
                s.wait_ge(mm.h, _m0 + (n4+1)*NCH)
                sc.inc(s.activation(out=qhat[:, n4*512:(n4+1)*512], in_=ps[:, 0:512], func=AF.Identity,
                                    scale=st2[:, 32+_c:33+_c], bias=st2[:, 48+_c:49+_c]))
        blk.scalar(saf)
        sc_af = sc.n
        mm_s0 = mm.n
        def tsim(t, _c=c, _saf=sc_af):
            t.wait_ge(sc.h, _saf)
            for n in range(16):
                ps = env.ps_a if n % 2 == 0 else env.ps_b
                if n >= 2:
                    t.wait_ge(ve.h, pk_ve0 + _c*16 + n - 1)
                mm.inc(t.matmul(ps[:, 0:256], qhat[:, n*128:(n+1)*128], keysv[:, _c % 2, :],
                                start=True, stop=True))
        blk.tensor(tsim)
        p2_sim_end.append(mm.n)
        def vpk(v, _c=c, _m0=mm_s0):
            v.wait_ge(ve2.h, p2_v20 + _c*17)
            for n in range(16):
                ps = env.ps_a if n % 2 == 0 else env.ps_b
                psI = ps.ap().bitcast(I32)
                dstF = (scr1[:, (n % 8)*256:(n % 8 + 1)*256] if n < 8
                        else scr2[:, (n % 8)*256:(n % 8 + 1)*256])
                dstI = dstF.bitcast(I32)
                v.wait_ge(mm.h, _m0 + n + 1)
                ve.inc(v.tensor_scalar(out=dstI, in0=psI[:, 0:256], scalar1=-256, scalar2=None, op0=ALU.bitwise_and))
                v.tensor_tensor(dstI, dstI, iot[:, :], ALU.bitwise_or)
                ve2.inc(v.max(ztF[:, n*8:(n+1)*8], dstF))
            cb0 = p * 1024 + 0 * 64 + h * 8
            for n in range(16):
                cb = p * 1024 + n * 64 + h * 8
                v.tensor_scalar(out=ixI[:, cb:cb+8], in0=ztI[:, n*8:(n+1)*8], scalar1=255, scalar2=None, op0=ALU.bitwise_and)
            for n in range(16):
                cb = p * 1024 + n * 64 + h * 8
                last = v.tensor_scalar(out=sxI[:, cb:cb+8], in0=ztI[:, n*8:(n+1)*8], scalar1=-256, scalar2=None, op0=ALU.bitwise_and)
                if n == 15:
                    ve2.inc(last)
        blk.vector(vpk)
    barrier(env)

    RNE_OFF = RNE_C - 0.4375
    for n in range(NT):
        ub = [u * 256 for u in range(8)]
        def S(u, col, w=8):
            return scr2[:, ub[u]+col:ub[u]+col+w]
        def vst(v, _n=n):
            def step(fn):
                for u in range(8):
                    fn(u)
            step(lambda u: v.tensor_scalar(out=S(u, 152), in0=ixI[:, _n*64+u*8:_n*64+u*8+8],
                                           scalar1=1.0, scalar2=0.0, op0=ALU.mult, op1=ALU.add))
            step(lambda u: v.tensor_scalar(out=S(u, 160), in0=ixI[:, 1024+_n*64+u*8:1024+_n*64+u*8+8],
                                           scalar1=1.0, scalar2=0.0, op0=ALU.mult, op1=ALU.add))
            for a in range(8):
                step(lambda u, _a=a: v.tensor_scalar(
                    out=S(u, _a*8), in0=sxF[:, 1024+_n*64+u*8:1024+_n*64+u*8+8],
                    scalar1=sxF[:, _n*64+u*8+_a:_n*64+u*8+_a+1], scalar2=None, op0=ALU.add))
            step(lambda u: v.tensor_scalar(out=S(u, 0, 64).bitcast(I32), in0=S(u, 0, 64).bitcast(I32),
                                           scalar1=-64, scalar2=None, op0=ALU.bitwise_and))
            step(lambda u: v.tensor_tensor(S(u, 0, 64).bitcast(I32), S(u, 0, 64).bitcast(I32),
                                           iot[:, 0:64], ALU.bitwise_or))
            step(lambda u: v.max(S(u, 64), S(u, 0, 64)))
            step(lambda u: v.tensor_scalar(out=S(u, 72).bitcast(I32), in0=S(u, 64).bitcast(I32),
                                           scalar1=63, scalar2=None, op0=ALU.bitwise_and))
            step(lambda u: v.tensor_scalar(out=S(u, 64).bitcast(I32), in0=S(u, 64).bitcast(I32),
                                           scalar1=-64, scalar2=None, op0=ALU.bitwise_and))
            step(lambda u: v.tensor_scalar(out=S(u, 80), in0=S(u, 72).bitcast(I32),
                                           scalar1=1.0, scalar2=0.0, op0=ALU.mult, op1=ALU.add))
            step(lambda u: v.tensor_scalar(out=S(u, 88), in0=S(u, 80),
                                           scalar1=0.125, scalar2=RNE_OFF, op0=ALU.mult, op1=ALU.add))
            step(lambda u: v.tensor_scalar(out=S(u, 88), in0=S(u, 88),
                                           scalar1=1.0, scalar2=RNE_C, op0=ALU.mult, op1=ALU.subtract))
            step(lambda u: v.scalar_tensor_tensor(out=S(u, 96), in0=S(u, 88), scalar=-8.0,
                                                  in1=S(u, 80), op0=ALU.mult, op1=ALU.add))
            for j in range(8):
                step(lambda u, _j=j: v.tensor_scalar(out=S(u, 104), in0=S(u, 88),
                                                     scalar1=float(_j), scalar2=None, op0=ALU.is_equal))
                step(lambda u, _j=j: v.tensor_scalar(out=S(u, 112), in0=S(u, 104),
                                                     scalar1=S(u, 152+_j, 1), scalar2=None, op0=ALU.mult))
                if j == 0:
                    step(lambda u: v.tensor_copy(S(u, 120), S(u, 112)))
                else:
                    step(lambda u: v.tensor_tensor(S(u, 120), S(u, 120), S(u, 112), ALU.add))
            for j in range(8):
                step(lambda u, _j=j: v.tensor_scalar(out=S(u, 104), in0=S(u, 96),
                                                     scalar1=float(_j), scalar2=None, op0=ALU.is_equal))
                step(lambda u, _j=j: v.tensor_scalar(out=S(u, 112), in0=S(u, 104),
                                                     scalar1=S(u, 160+_j, 1), scalar2=None, op0=ALU.mult))
                if j == 0:
                    step(lambda u: v.tensor_copy(S(u, 128), S(u, 112)))
                else:
                    step(lambda u: v.tensor_tensor(S(u, 128), S(u, 128), S(u, 112), ALU.add))
            step(lambda u: v.scalar_tensor_tensor(out=S(u, 136), in0=S(u, 120), scalar=256.0,
                                                  in1=S(u, 128), op0=ALU.mult, op1=ALU.add))
            step(lambda u: v.tensor_scalar(out=idxI[:, _n*64+u*8:_n*64+u*8+8], in0=S(u, 136),
                                           scalar1=1.0, scalar2=0.0, op0=ALU.mult, op1=ALU.add))
            step(lambda u: v.tensor_reduce(S(u, 144, 1), S(u, 64), axis=AX, op=ALU.max))
            for u in range(8):
                ins = v.tensor_scalar(out=S(u, 145, 1), in0=S(u, 144, 1), scalar1=-1.0, scalar2=None, op0=ALU.mult)
                if u == 7:
                    ve.inc(ins)
        blk.vector(vst)
        ve_s2 = ve.n
        def sst(s, _n=n, _v=ve_s2):
            s.wait_ge(ve.h, _v)
            for u in range(8):
                ins = s.activation(out=w8F[:, _n*64+u*8:_n*64+u*8+8], in_=S(u, 64), func=AF.Exp,
                                   bias=S(u, 145, 1), scale=1.0, accum_out=S(u, 146, 1))
                if u == 7:
                    sc.inc(ins)
        blk.scalar(sst)
        sc_s2 = sc.n
        def vsf(v, _n=n, _s=sc_s2):
            v.wait_ge(sc.h, _s)
            for u in range(8):
                v.reciprocal(S(u, 147, 1), S(u, 146, 1))
            for _ in range(8):
                v.tensor_copy(env.fceA[:, :], env.fceB[:, :])
            for u in range(8):
                ins = v.tensor_scalar(out=w8F[:, _n*64+u*8:_n*64+u*8+8], in0=w8F[:, _n*64+u*8:_n*64+u*8+8],
                                      scalar1=S(u, 147, 1), scalar2=None, op0=ALU.mult)
                if u == 7:
                    ve2.inc(ins)
        blk.vector(vsf)
    barrier(env)

    dn0 = ve.n
    up0 = mm.n
    ps0 = sc.n
    gs_base = gs.n
    res_done = []
    tcat = env.dram["l1f_tcat"]
    for tc in range(NT):
        def gg(g, _tc=tc):
            for j in range(64):
                gidx = _tc * 64 + j
                if gidx >= 8:
                    g.wait_ge(ve.h, dn0 + (gidx - 8 + 1))
                    g.wait_ge(mm.h, up0 + 2 * (gidx - 8 + 1))
                gs.inc(g.indirect_dma_start(
                    ring[:, gidx % 8, :], None, tcat[:, :],
                    IndirectOffsetOnAxis(ap=idxI[:, gidx:gidx+1], axis=0)), 16)
        blk.gpsimd(gg)
        def vdn(v, _tc=tc):
            for j in range(64):
                gidx = _tc * 64 + j
                v.wait_ge(gs.h, gs_base + 16 * (gidx + 1))
                ve.inc(v.scalar_tensor_tensor(out=scr1[:, 0:1024], in0=ring[:, gidx % 8, 0:1024],
                                              scalar=1.0, in1=env.xln_tm[:, _tc, :],
                                              op0=ALU.mult, op1=ALU.mult,
                                              accum_out=scr2[:, gidx:gidx+1]))
        blk.vector(vdn)
        def sgu(s, _tc=tc):
            def gelu(j):
                gidx = _tc * 64 + j
                s.wait_ge(ve.h, dn0 + gidx + 1)
                s.activation(out=scr1[:, 1024+j:1025+j], in_=scr2[:, gidx:gidx+1], func=AF.Gelu_apprx_tanh)
            def gcol(j):
                gidx = _tc * 64 + j
                s.activation(out=scr1[:, 1088+j:1089+j], in_=scr1[:, 1024+j:1025+j], func=AF.Identity,
                             scale=w8F[:, gidx:gidx+1])
            def presc(j):
                gidx = _tc * 64 + j
                sc.inc(s.activation(out=ring[:, gidx % 8, 1024:2048], in_=ring[:, gidx % 8, 1024:2048],
                                    func=AF.Identity, scale=scr1[:, 1088+j:1089+j]))
            for j in range(64):
                gelu(j)
                if j >= 2:
                    presc(j - 2)
                gcol(j)
            presc(62)
            presc(63)
        blk.scalar(sgu)
        def tup(t, _tc=tc):
            if _tc > 0:
                t.wait_ge(ve2.h, res_done[_tc - 1])
            for j in range(64):
                gidx = _tc * 64 + j
                t.wait_ge(sc.h, ps0 + _tc * 64 + j + 1)
                mm.inc(t.matmul(env.ps_a[:, :], env.identb[:], ring[:, gidx % 8, 1024:1536],
                                start=(j == 0), stop=(j == 63)))
                mm.inc(t.matmul(env.ps_b[:, :], env.identb[:], ring[:, gidx % 8, 1536:2048],
                                start=(j == 0), stop=(j == 63)))
        blk.tensor(tup)
        mm_up_end = mm.n
        def vres(v, _tc=tc, _m=mm_up_end):
            v.wait_ge(mm.h, _m)
            v.tensor_tensor(env.stream[:, _tc, 0:512], env.stream[:, _tc, 0:512], env.ps_a[:, :], ALU.add)
            ve2.inc(v.tensor_tensor(env.stream[:, _tc, 512:1024], env.stream[:, _tc, 512:1024], env.ps_b[:, :], ALU.add))
        blk.vector(vres)
        res_done.append(ve2.n)
    barrier(env)


def p_store(env):
    blk = env.blk; st = env.s_st
    def g(g):
        for tc in range(NT):
            st.inc(g.dma_start(out=env.y[tc*128:(tc+1)*128, :], in_=env.stream[:, tc, :]), 16)
        g.wait_ge(st.h, st.n)
    blk.gpsimd(g)


def host_prep(inputs):
    x = np.ascontiguousarray(np.asarray(inputs["L0_attn_x"], dtype=np.float32))
    qkvu_wT = np.ascontiguousarray(np.asarray(inputs["L0_attn_qkvu_w"], dtype=np.float32).T)
    qkvu_b = np.asarray(inputs["L0_attn_qkvu_b"], dtype=np.float32).reshape(1, -1)
    out_wT = np.ascontiguousarray(np.asarray(inputs["L0_attn_out_w"], dtype=np.float32).T)
    out_b = np.asarray(inputs["L0_attn_out_b"], dtype=np.float32).reshape(1, -1)
    theta = np.ascontiguousarray(np.asarray(inputs["L0_attn_theta"], dtype=np.float32).reshape(8, 128).T)
    slope = np.asarray(inputs["L0_attn_slope_rate"], dtype=np.float32).reshape(1, 8)
    ident = np.eye(128, dtype=np.float32)
    triu = np.triu(np.ones((128, 128), dtype=np.float32))
    wgT = np.ascontiguousarray(np.asarray(inputs["L0_ffn_w_gate"], dtype=np.float32).T)
    wuT = np.ascontiguousarray(np.asarray(inputs["L0_ffn_w_up"], dtype=np.float32).T)
    wdT = np.ascontiguousarray(np.asarray(inputs["L0_ffn_w_down"], dtype=np.float32).T)
    perm = np.concatenate([np.arange(0, 128, 2), np.arange(1, 128, 2)])
    Wq = np.asarray(inputs["L1_attn_Wq"], dtype=np.float32)
    l1_wqT = np.ascontiguousarray(np.stack([Wq[hh].T[:, perm] for hh in range(2)]))
    l1_wkT = np.ascontiguousarray(np.asarray(inputs["L1_attn_Wk"], dtype=np.float32).T[:, perm])
    l1_wvT = np.ascontiguousarray(np.asarray(inputs["L1_attn_Wv"], dtype=np.float32).T)
    l1_woT = np.ascontiguousarray(np.asarray(inputs["L1_attn_Wo"], dtype=np.float32).T)
    pos = np.arange(2048, dtype=np.float64)
    th_j = 10000.0 ** (-np.arange(0, 128, 2, dtype=np.float64) / 64.0)
    ang = th_j[:, None] * pos[None, :]
    rope_cos = np.cos(ang).astype(np.float32)
    rope_sin = np.sin(ang).astype(np.float32)
    qi = np.arange(128)[:, None]; ki = np.arange(128)[None, :]
    diagmask = np.where(ki > qi, np.float32(-1e30), np.float32(0.0)).astype(np.float32)
    l1f_wq = np.ascontiguousarray(np.asarray(inputs["L1_ffn_w_q"], dtype=np.float32))
    keys = np.asarray(inputs["L1_ffn_keys"], dtype=np.float32)
    keysT = np.zeros((16, 128, 256), dtype=np.float32)
    for p_ in range(2):
        for h_ in range(8):
            keysT[p_*8+h_] = keys[h_, :, p_, :].T
    l1f_gamma = np.ascontiguousarray(np.asarray(inputs["L1_ffn_bn_gamma"], dtype=np.float32).reshape(16, 128).T)
    l1f_beta = np.ascontiguousarray(np.asarray(inputs["L1_ffn_bn_beta"], dtype=np.float32).reshape(16, 128).T)
    l1f_tcat = np.ascontiguousarray(np.concatenate(
        [np.asarray(inputs["L1_ffn_down_embed"], dtype=np.float32),
         np.asarray(inputs["L1_ffn_up_embed"], dtype=np.float32)], axis=1))
    c_iota256 = np.ascontiguousarray(np.tile(np.arange(256, dtype=np.int32), (128, 1)))
    common = {
        "l1f_wq": l1f_wq, "l1f_keysT": keysT, "l1f_gamma": l1f_gamma, "l1f_beta": l1f_beta,
        "l1f_tcat": l1f_tcat, "c_iota256": c_iota256,
        "l1_wqT": l1_wqT, "l1_wkT": l1_wkT, "l1_wvT": l1_wvT, "l1_woT": l1_woT,
        "c_rope_cos": rope_cos, "c_rope_sin": rope_sin, "c_diagmask": diagmask,
        "l0f_wgT": wgT, "l0f_wuT": wuT, "l0f_wdT": wdT,
        "l0_qkvu_wT": qkvu_wT, "l0_qkvu_b": np.ascontiguousarray(qkvu_b),
        "l0_out_wT": out_wT, "l0_out_b": np.ascontiguousarray(out_b),
        "l0_theta": theta, "l0_slope": np.ascontiguousarray(slope),
        "c_ident": ident, "c_triu": np.ascontiguousarray(triu),
        "l0_qkvu_b_col": np.ascontiguousarray(qkvu_b.reshape(32, 128).T),
    }
    return [dict(common, x=np.ascontiguousarray(x[i])) for i in range(8)]


def _l0_attn_ref_jax(xln, inputs):
    import jax, jax.numpy as jnp
    EPS = 1e-6
    x = xln
    slope_rate = jnp.asarray(inputs["L0_attn_slope_rate"])
    qkvu_w = jnp.asarray(inputs["L0_attn_qkvu_w"]); qkvu_b = jnp.asarray(inputs["L0_attn_qkvu_b"])
    out_w = jnp.asarray(inputs["L0_attn_out_w"]); out_b = jnp.asarray(inputs["L0_attn_out_b"])
    theta = jnp.asarray(inputs["L0_attn_theta"])
    def _lrpe(xh, theta, n):
        idx = jnp.arange(n, dtype=xh.dtype).reshape(1, n, 1)
        th = theta * idx; th = th[None]
        return jnp.concatenate([xh * jnp.cos(th), xh * jnp.sin(th)], axis=-1)
    b, n, hd = x.shape; h = 8
    qkvu = x @ qkvu_w.T + qkvu_b
    q, k, v, u = jnp.split(qkvu, 4, axis=-1)
    to_heads = lambda t: t.reshape(b, n, h, hd // h).transpose(0, 2, 1, 3)
    q, k, v = to_heads(q), to_heads(k), to_heads(v)
    q = jax.nn.silu(q); k = jax.nn.silu(k)
    q = _lrpe(q, theta, n); k = _lrpe(k, theta, n)
    tril = jnp.tril(jnp.ones((n, n), dtype=x.dtype))
    mask = jnp.exp(slope_rate * tril)
    energy = jnp.einsum('bhne,bhme->bhnm', q, k) * mask[None]
    out = jnp.einsum('bhnm,bhmd->bhnd', energy, v)
    out = out.transpose(0, 2, 1, 3).reshape(b, n, hd)
    ms = jnp.mean(jnp.square(out), axis=-1, keepdims=True)
    out = out * jax.lax.rsqrt(ms + EPS)
    out = u * out
    out = out @ out_w.T + out_b
    return out



def kernel(**inputs):
    in_maps = host_prep(inputs)
    nc = build_program(upto=5)
    res = run_bass_kernel_spmd(nc, in_maps, core_ids=list(range(8)))
    return np.stack([np.asarray(r["y"]) for r in res.results])


# version 03 (final)
# speedup vs baseline: 1.1136x; 1.1136x over previous
import numpy as np
from contextlib import ExitStack
import concourse.bass as bass
import concourse.mybir as mybir
from concourse.bass_utils import run_bass_kernel_spmd
from concourse.bass import IndirectOffsetOnAxis

F32 = mybir.dt.float32; BF16 = mybir.dt.bfloat16; I32 = mybir.dt.int32; U32 = mybir.dt.uint32
AF = mybir.ActivationFunctionType; ALU = mybir.AluOpType
AX = mybir.AxisListType.X

T = 2048; D = 1024; NCH = 8; NT = 16
RNE_C = 12582912.0
TWO_PI = 6.283185307179586
TP_HI = 6.28125
TP_LO = 0.0019353071795864769
HALF_PI = 1.5707963267948966
PI = 3.141592653589793

class Sem:
    __slots__ = ("h", "n")
    def __init__(self, nc, ctx, name):
        self.h = ctx.enter_context(nc.semaphore(name)); self.n = 0
    def inc(self, instr, k=1):
        self.n += k; instr.then_inc(self.h, k); return instr


class Env: pass


def barrier(env):
    b = env.s_bar; blk = env.blk
    target = b.n + 5
    b.n = target
    def fs(eng, _t=target):
        eng.nop().then_inc(b.h, 1)
        eng.wait_ge(b.h, _t)
    def ft(eng, _t=target):
        eng.transpose(env.ps_P1.ap().bitcast(BF16)[0:2, 0:128], env.identb[:, 0:2], env.identb[:]).then_inc(b.h, 1)
        eng.wait_ge(b.h, _t)
    def fv(eng, _t=target):
        eng.tensor_copy(env.fceA[:, :], env.fceB[:, :]).then_inc(b.h, 1)
        eng.wait_ge(b.h, _t)
    def fc(eng, _t=target):
        eng.copy(out=env.fceA[:, :], in_=env.fceB[:, :]).then_inc(b.h, 1)
        eng.wait_ge(b.h, _t)
    def fg(eng, _t=target):
        eng.memset(env.fceB[:, :], 0.0)
        eng.memset(env.fceB[:, :], 0.0).then_inc(b.h, 1)
        eng.wait_ge(b.h, _t)
    blk.sync(fs); blk.tensor(ft); blk.vector(fv); blk.scalar(fc); blk.gpsimd(fg)


def transpose_tiles(env, pairs):
    blk = env.blk; mm, sc = env.s_mm, env.s_sc
    mm0 = mm.n
    sc0 = sc.n
    n = len(pairs)
    def t(t):
        for k in range(n):
            if k >= 2:
                t.wait_ge(sc.h, sc0 + (k - 2 + 1))
            mm.inc(t.transpose(env.ps_t[k % 2][:, :], pairs[k][0], env.identb[:]))
    def s(s):
        for k in range(n):
            s.wait_ge(mm.h, mm0 + k + 1)
            sc.inc(s.copy(out=pairs[k][1], in_=env.ps_t[k % 2][:, :]))
    blk.tensor(t); blk.scalar(s)


def build_program(upto):
    nc = bass.Bass("TRN2", target_bir_lowering=False, debug=False, num_devices=8, detect_race_conditions=False)
    env = Env(); env.nc = nc
    ctx = ExitStack(); env.ctx = ctx

    dram = {}
    def din(name, shape, dtype=F32):
        dram[name] = nc.dram_tensor(name, shape, dtype, kind="ExternalInput").ap(); return dram[name]
    env.dram = dram
    din("x", [T, D])
    din("l0_qkvu_wT", [D, 4*D])
    din("l0_qkvu_b", [1, 4*D])
    din("l0_out_wT", [D, D])
    din("l0_out_b", [1, D])
    din("l0_theta", [128, 8])
    din("l0_slope", [1, 8])
    din("l0f_wgT", [D, 4096])
    din("l0f_wuT", [D, 4096])
    din("l0f_wdT", [4096, D])
    din("l1f_wq", [D, 2048])
    din("l1f_keysT", [16, 128, 256])
    din("l1f_gamma", [128, 16])
    din("l1f_beta", [128, 16])
    din("l1f_tcat", [65536, 2048])
    din("c_iota256", [128, 256], I32)
    din("l1_wqT", [2, D, 128])
    din("l1_wkT", [D, 128])
    din("l1_wvT", [D, 128])
    din("l1_woT", [256, D])
    din("c_rope_cos", [64, T])
    din("c_rope_sin", [64, T])
    din("c_diagmask", [128, 128])
    din("c_ident", [128, 128])
    din("c_triu", [128, 128])
    din("l0_qkvu_b_col", [128, 32])
    env.y = nc.dram_tensor("y", [T, D], F32, kind="ExternalOutput").ap()
    env.wqd_dram = nc.dram_tensor("wqd_dram", [32, 128, D], BF16, kind="Internal").ap()
    env.wqg_dram = nc.dram_tensor("wqg_dram", [NCH, 128, 4096], BF16, kind="Internal").ap()
    env.wqu_dram = nc.dram_tensor("wqu_dram", [NCH, 128, 4096], BF16, kind="Internal").ap()
    env.h2_dram = nc.dram_tensor("h2_dram", [NT, 4, 128, 1024], BF16, kind="Internal").ap()
    env.bn_in = nc.dram_tensor("bn_in", [128, 32], F32, kind="Internal").ap()
    env.bn_out = nc.dram_tensor("bn_out", [128, 32], F32, kind="Internal").ap()
    env.dbg = nc.dram_tensor("dbg", [128, 8, 2048], F32, kind="ExternalOutput").ap()

    S_ = lambda n_, sh, dt=F32: ctx.enter_context(nc.sbuf_tensor(n_, sh, dt))
    P_ = lambda n_, sh, dt=F32: ctx.enter_context(nc.psum_tensor(n_, sh, dt))
    env.S_, env.P_ = S_, P_

    env.stream = S_("stream", [128, NT, D])
    env.xln_tm = S_("xln_tm", [128, NT, D], BF16)
    env.xln_fm = S_("xln_fm", [128, NCH, T], BF16)
    env.attn_out = env.xln_tm
    env.identb = S_("identb", [128, 128], BF16)
    env.triub = S_("triub", [128, 128], BF16)
    env.onesb = S_("onesb", [1, 128], BF16)
    env.onesf = S_("onesf", [1, 128])
    env.colA = S_("colA", [128, 8])
    env.stat = S_("stat", [128, NT, 8])
    env.scr1 = S_("scr1", [128, 2048])
    env.scr2 = S_("scr2", [128, 2048])
    env.qf = S_("qf", [128, 2, T], BF16)
    env.kf = S_("kf", [128, 2, T], BF16)
    env.ktm = S_("ktm", [128, NT, 256], BF16)
    env.vtm = S_("vtm", [128, NT, 128], BF16)
    env.cosf = S_("cosf", [128, T], BF16)
    env.sinf = S_("sinf", [128, T], BF16)
    env.Sp = None
    env.Psnap = None
    env.Pfin = S_("Pfin", [128, 256], BF16)
    env.Psb = S_("Psb", [128, 256])
    env.maskh = S_("maskh", [128, 128], BF16)
    env.acol = S_("acol", [128, 2])
    env.wpro = S_("wpro", [128, NCH, 128], BF16)
    env.wbig = S_("wbig", [128, NCH, 512], BF16)
    env.zt = S_("zt", [128, 512], BF16)
    env.l0bias_rowb = S_("l0bias_rowb", [1, 2*D], BF16)
    env.l0bias_col = S_("l0bias_col", [128, 32])
    env.l0slope = S_("l0slope", [1, 8])
    env.l0bo = S_("l0bo", [1, D], BF16)
    env.fceA = S_("fceA", [1, 2])
    env.fceB = S_("fceB", [1, 2])
    env.wacc = S_("wacc", [128, 6])
    env.wcols = S_("wcols", [128, 6])
    env.stat2 = S_("stat2", [128, NT, 4])
    env.onescol = S_("onescol", [128, 1])
    env.iota256 = S_("iota256", [128, 256], I32)

    env.ps_a = P_("ps_a", [128, 512])
    env.ps_b = P_("ps_b", [128, 512])
    env.ps_c = P_("ps_c", [128, 128])
    env.ps_d = P_("ps_d", [128, 128])
    env.ps_t = [P_(f"ps_t{i}", [128, 128], BF16) for i in range(2)]
    env.ps_P0 = P_("ps_P0", [128, 128])
    env.ps_P1 = P_("ps_P1", [128, 128])

    for s in ("ld", "mm", "ve", "sc", "gp", "st", "bar", "ld2", "mm2", "ve2", "sc2", "gp2", "init", "gd"):
        setattr(env, "s_" + s, Sem(nc, ctx, "s_" + s))

    env.Sp = env.cosf.ap().rearrange("p (t c) -> p t c", t=NT)
    env.Psnap1 = env.scr1.ap().rearrange("p (t c) -> p t c", t=8)
    env.Psnap2 = env.scr2.ap().rearrange("p (t c) -> p t c", t=8)
    env.Mc = env.sinf.ap().rearrange("p (k c) -> p k c", k=4)[:, :, 0:256]


    import os as _os2
    env.bn_group = list(range(int(_os2.environ.get("MK_BNG", "8"))))

    blk = ctx.enter_context(nc.Block()); env.blk = blk

    p0_load(env)
    if upto >= 2:
        p_l0_attn(env)
    if upto >= 3:
        import os as _os
        if _os.environ.get("MK_STAGE") == "x1":
            pass
        else:
            p_l0_ffn(env)
    if upto >= 4:
        p_l1_attn(env)
    if upto >= 5:
        p_l1_ffn(env)
    p_store(env)
    ctx.close()
    return nc


def p0_load(env):
    blk = env.blk; ld = env.s_ld; ld2 = env.s_ld2
    init = env.s_init
    def ginit(g):
        g.memset(env.fceA[:, :], 0.0)
        init.inc(g.memset(env.fceB[:, :], 0.0), 1)
        ld2.inc(g.dma_start(out=env.identb[:], in_=env.dram["c_ident"][:, :]), 16)
    blk.gpsimd(ginit)
    tgt = init.n; tgt2 = ld2.n
    for name in ("sync", "tensor", "vector", "scalar"):
        def fw(eng, _t=tgt, _t2=tgt2):
            eng.wait_ge(init.h, _t)
            eng.wait_ge(ld2.h, _t2)
        getattr(blk, name)(fw)
    def f(s):
        for tc in range(NT):
            ld.inc(s.dma_start(out=env.stream[:, tc, :], in_=env.dram["x"][tc*128:(tc+1)*128, :]), 16)
    blk.sync(f)
    def g(g):
        ld2.inc(g.dma_start(out=env.triub[:], in_=env.dram["c_triu"][:, :]), 16)
        ld2.inc(g.dma_start(out=env.colA[:, 0:8], in_=env.dram["l0_theta"][:, :]), 16)
        ld2.inc(g.dma_start(out=env.l0bias_rowb[:, :], in_=env.dram["l0_qkvu_b"][:, 2048:4096]), 16)
        ld2.inc(g.dma_start(out=env.l0bias_col[:, :], in_=env.dram["l0_qkvu_b_col"][:, :]), 16)
        ld2.inc(g.dma_start(out=env.l0slope[:, :], in_=env.dram["l0_slope"][:, :]), 16)
        ld2.inc(g.dma_start(out=env.l0bo[:, :], in_=env.dram["l0_out_b"][:, :]), 16)
    blk.gpsimd(g)
    ldn = ld.n; ldn2 = ld2.n
    def v(v):
        v.memset(env.onesb[:, :], 1.0)
        v.memset(env.onesf[:, :], 1.0)
        v.memset(env.onescol[:, :], 1.0)
        v.wait_ge(ld.h, ldn)
        v.wait_ge(ld2.h, ldn2)
    blk.vector(v)
    barrier(env)


def emit_layernorm(env, need_fm=True, eps=1e-5):
    blk = env.blk
    ve, mm, sc = env.s_ve, env.s_mm, env.s_sc
    stat = env.stat; scr = env.scr1
    def v(v):
        for tc in range(NT):
            v.reduce_sum(stat[:, tc, 2:3], env.stream[:, tc, :], axis=AX)
            v.scalar_tensor_tensor(out=scr[:, 0:1024], in0=env.stream[:, tc, :], scalar=1.0,
                                   in1=env.stream[:, tc, :], op0=ALU.mult, op1=ALU.mult,
                                   accum_out=stat[:, tc, 3:4])
        for tc in range(NT):
            v.tensor_scalar(out=stat[:, tc, 0:1], in0=stat[:, tc, 2:3], scalar1=1.0/D, scalar2=None, op0=ALU.mult)
        for tc in range(NT):
            v.tensor_tensor(stat[:, tc, 5:6], stat[:, tc, 0:1], stat[:, tc, 0:1], ALU.mult)
        for tc in range(NT):
            v.tensor_scalar(out=stat[:, tc, 6:7], in0=stat[:, tc, 3:4], scalar1=1.0/D, scalar2=eps, op0=ALU.mult, op1=ALU.add)
        for tc in range(NT):
            v.tensor_tensor(stat[:, tc, 6:7], stat[:, tc, 6:7], stat[:, tc, 5:6], ALU.subtract)
        ve.inc(v.tensor_copy(env.fceA[:, :], env.fceB[:, :]))
    blk.vector(v)
    ve1 = ve.n
    def s(s):
        s.wait_ge(ve.h, ve1)
        for tc in range(NT):
            s.activation(out=stat[:, tc, 7:8], in_=stat[:, tc, 6:7], func=AF.Sqrt)
        sc.inc(s.copy(out=env.fceA[:, :], in_=env.fceB[:, :]))
    blk.scalar(s)
    sc1 = sc.n
    def v2(v):
        v.wait_ge(sc.h, sc1)
        for tc in range(NT):
            v.reciprocal(stat[:, tc, 1:2], stat[:, tc, 7:8])
        for tc in range(NT):
            v.tensor_tensor(stat[:, tc, 5:6], stat[:, tc, 1:2], stat[:, tc, 1:2], ALU.mult)
        for tc in range(NT):
            v.tensor_tensor(stat[:, tc, 5:6], stat[:, tc, 5:6], stat[:, tc, 6:7], ALU.mult)
        for tc in range(NT):
            v.tensor_scalar(out=stat[:, tc, 5:6], in0=stat[:, tc, 5:6], scalar1=-0.5, scalar2=1.5, op0=ALU.mult, op1=ALU.add)
        for tc in range(NT):
            v.tensor_tensor(stat[:, tc, 1:2], stat[:, tc, 1:2], stat[:, tc, 5:6], ALU.mult)
        for tc in range(NT):
            v.tensor_tensor(stat[:, tc, 4:5], stat[:, tc, 0:1], stat[:, tc, 1:2], ALU.mult)
        for tc in range(NT):
            v.tensor_scalar(out=stat[:, tc, 4:5], in0=stat[:, tc, 4:5], scalar1=-1.0, scalar2=None, op0=ALU.mult)
        for tc in range(NT):
            ve.inc(v.tensor_scalar(out=env.xln_tm[:, tc, :], in0=env.stream[:, tc, :],
                                   scalar1=stat[:, tc, 1:2], scalar2=stat[:, tc, 4:5],
                                   op0=ALU.mult, op1=ALU.add))
    blk.vector(v2)
    if need_fm:
        barrier(env)
        pairs = []
        for tc in range(NT):
            for c in range(NCH):
                pairs.append((env.xln_tm[:, tc, c*128:(c+1)*128], env.xln_fm[:, c, tc*128:(tc+1)*128]))
        transpose_tiles(env, pairs)
    barrier(env)


def dump_tm(env, src_tiles, width=1024):
    blk = env.blk; ve = env.s_ve
    def v(v):
        for tc in range(NT):
            ve.inc(v.tensor_copy(env.stream[:, tc, 0:width], src_tiles[:, tc, :]))
        if width < 1024:
            for tc in range(NT):
                ve.inc(v.memset(env.stream[:, tc, width:1024], 0.0))
    blk.vector(v)
    barrier(env)


def dump_fm(env, src, chunks=8):
    blk = env.blk; st = env.s_st
    def g(g):
        for c in range(chunks):
            st.inc(g.dma_start(out=env.dbg[:, c, :], in_=src[:, c, :]), 16)
        g.wait_ge(st.h, st.n)
    blk.gpsimd(g)
    barrier(env)


import os
def p_l0_attn(env):
    blk = env.blk
    stage = os.environ.get("MK_STAGE", "")
    emit_layernorm(env, need_fm=(stage != "ln_nofm"))
    if stage in ("ln", "ln_nofm"):
        def gst(g):
            env.s_st.inc(g.dma_start(out=env.dbg[:, 0, 0:128], in_=env.stat[:, :, :].rearrange("p t c -> p (t c)")), 16)
            g.wait_ge(env.s_st.h, env.s_st.n)
        blk.gpsimd(gst)
        dump_tm(env, env.xln_tm); return
    if stage == "lnfm":
        dump_fm(env, env.xln_fm); return
    nheads = 1 if stage in ("qk0", "attn0") else 8
    for h in range(nheads):
        stopped = _l0_head(env, h)
        if stopped:
            return
        if stage == "qk0" and h == 0:
            dump_fm(env, env.qf, chunks=2)
            def g2(g):
                env.s_st.inc(g.dma_start(out=env.dbg[:, 2, :], in_=env.kf[:, 0, :]), 16)
                env.s_st.inc(g.dma_start(out=env.dbg[:, 3, :], in_=env.kf[:, 1, :]), 16)
                env.s_st.inc(g.dma_start(out=env.dbg[:, 4, :], in_=env.cosf[:, :]), 16)
                env.s_st.inc(g.dma_start(out=env.dbg[:, 5, :], in_=env.sinf[:, :]), 16)
                g.wait_ge(env.s_st.h, env.s_st.n)
            blk.gpsimd(g2)
            barrier(env)
            return
        if stage == "attn0" and h == 0:
            dump_tm(env, env.attn_out); return
    if stage == "attn":
        dump_tm(env, env.attn_out); return
    _l0_epilogue(env)


def _l0_head(env, h):
    import os as _os
    hstop = _os.environ.get("MK_HSTOP", "")
    blk = env.blk
    ld, mm, ve, sc, gp = env.s_ld, env.s_mm, env.s_ve, env.s_sc, env.s_gp
    ld2 = env.s_ld2
    ve2, sc2 = env.s_ve2, env.s_sc2
    wT = env.dram["l0_qkvu_wT"]

    def g(g):
        for c in range(NCH):
            ld2.inc(g.dma_start(out=env.wpro[:, c, :], in_=wT[c*128:(c+1)*128, h*128:(h+1)*128]), 16)
    blk.gpsimd(g)
    ld_w = ld2.n

    import os as _os2
    skip_tab = _os2.environ.get("MK_SKIPTAB")
    scr, scr2 = env.scr1, env.scr2
    if skip_tab:
        def snop(s):
            sc.inc(s.copy(out=env.fceA[:, :], in_=env.fceB[:, :]))
        blk.scalar(snop)
        sc_tab = sc.n
        if hstop == "tab":
            barrier(env)
            def gtb(g):
                env.s_st.inc(g.dma_start(out=env.dbg[:, 2, :], in_=env.xln_fm[:, 0, :]), 16)
                g.wait_ge(env.s_st.h, env.s_st.n)
            blk.gpsimd(gtb)
            barrier(env); return
    def gi(g):
        gp.inc(g.iota(scr[:, :], pattern=[[1, T]], base=0, channel_multiplier=0, allow_small_or_imprecise_dtypes=True))
    if not skip_tab:
        blk.gpsimd(gi)
    gp0 = gp.n
    def v(v):
        v.wait_ge(gp.h, gp0)
        v.tensor_scalar(out=scr[:, :], in0=scr[:, :], scalar1=env.colA[:, h:h+1], scalar2=None, op0=ALU.mult)
        v.tensor_scalar(out=scr2[:, :], in0=scr[:, :], scalar1=1.0/TWO_PI, scalar2=RNE_C, op0=ALU.mult, op1=ALU.add)
        v.tensor_scalar(out=scr2[:, :], in0=scr2[:, :], scalar1=RNE_C, scalar2=None, op0=ALU.subtract)
        v.scalar_tensor_tensor(out=scr[:, :], in0=scr2[:, :], scalar=-TP_HI, in1=scr[:, :], op0=ALU.mult, op1=ALU.add)
        v.scalar_tensor_tensor(out=scr[:, :], in0=scr2[:, :], scalar=-TP_LO, in1=scr[:, :], op0=ALU.mult, op1=ALU.add)
        v.tensor_scalar(out=scr[:, :], in0=scr[:, :], scalar1=PI, scalar2=-PI, op0=ALU.min, op1=ALU.max)
        ve.inc(v.tensor_copy(env.fceA[:, :], env.fceB[:, :]))
    if not skip_tab:
        blk.vector(v)
    ve_ang = ve.n
    def s(s):
        s.wait_ge(ve.h, ve_ang)
        s.activation(out=env.sinf[:, :], in_=scr[:, :], func=AF.Sin)
        sc.inc(s.copy(out=env.fceA[:, :], in_=env.fceB[:, :]))
    if not skip_tab:
        blk.scalar(s)
    sc_sin = sc.n
    def vc(v):
        v.wait_ge(sc.h, sc_sin)
        v.tensor_scalar(out=scr2[:, :], in0=scr[:, :], scalar1=1.0, scalar2=HALF_PI, op0=ALU.mult, op1=ALU.add)
        v.tensor_scalar(out=scr[:, :], in0=scr2[:, :], scalar1=1.0/TWO_PI, scalar2=RNE_C, op0=ALU.mult, op1=ALU.add)
        v.tensor_scalar(out=scr[:, :], in0=scr[:, :], scalar1=RNE_C, scalar2=None, op0=ALU.subtract)
        v.scalar_tensor_tensor(out=scr2[:, :], in0=scr[:, :], scalar=-TP_HI, in1=scr2[:, :], op0=ALU.mult, op1=ALU.add)
        v.scalar_tensor_tensor(out=scr2[:, :], in0=scr[:, :], scalar=-TP_LO, in1=scr2[:, :], op0=ALU.mult, op1=ALU.add)
        v.tensor_scalar(out=scr2[:, :], in0=scr2[:, :], scalar1=PI, scalar2=-PI, op0=ALU.min, op1=ALU.max)
        ve.inc(v.tensor_copy(env.fceA[:, :], env.fceB[:, :]))
    if not skip_tab:
        blk.vector(vc)
    ve_cosarg = ve.n
    def s1b(s):
        s.wait_ge(ve.h, ve_cosarg)
        s.activation(out=env.cosf[:, :], in_=scr2[:, :], func=AF.Sin)
        sc.inc(s.copy(out=env.fceA[:, :], in_=env.fceB[:, :]))
    if not skip_tab:
        blk.scalar(s1b)
        sc_tab = sc.n
    if hstop == "tab":
        barrier(env)
        def gtb(g):
            env.s_st.inc(g.dma_start(out=env.dbg[:, 2, :], in_=env.xln_fm[:, 0, :]), 16)
            g.wait_ge(env.s_st.h, env.s_st.n)
        blk.gpsimd(gtb)
        barrier(env); return True

    sc_qk_base = sc.n
    for role in range(2):
        dstf = env.qf if role == 0 else env.kf
        if role == 1:
            mm_before = mm.n
            def gk(g, _mm=mm_before):
                g.wait_ge(mm.h, _mm)
                for c in range(NCH):
                    ld2.inc(g.dma_start(out=env.wpro[:, c, :], in_=wT[c*128:(c+1)*128, 1024+h*128:1024+(h+1)*128]), 16)
            blk.gpsimd(gk)
        ld_w2 = ld2.n
        mm0 = mm.n
        def t(t, _ld=ld_w2, _role=role):
            t.wait_ge(ld2.h, _ld)
            for n4 in range(4):
                ps = env.ps_a if n4 % 2 == 0 else env.ps_b
                kg = _role*4 + n4
                if kg >= 2:
                    t.wait_ge(sc.h, sc_qk_base + (kg - 2 + 1))
                for c in range(NCH):
                    mm.inc(t.matmul(ps[:, :], env.wpro[:, c, :], env.xln_fm[:, c, n4*512:(n4+1)*512],
                                    start=(c == 0), stop=(c == NCH-1)))
        def s2(s, _role=role, _mm0=mm0, _dst=dstf):
            bc = env.l0bias_col[:, h:h+1] if _role == 0 else env.l0bias_col[:, 8+h:9+h]
            for n4 in range(4):
                ps = env.ps_a if n4 % 2 == 0 else env.ps_b
                s.wait_ge(mm.h, _mm0 + (n4+1)*NCH)
                sc.inc(s.activation(out=_dst[:, 0, n4*512:(n4+1)*512], in_=ps[:, :], func=AF.Silu, bias=bc, scale=1.0))
        blk.tensor(t); blk.scalar(s2)
    sc_evac = sc.n
    if hstop == "proj":
        barrier(env)
        def gpj(g):
            env.s_st.inc(g.dma_start(out=env.dbg[:, 0, :], in_=env.qf[:, 0, :]), 16)
            env.s_st.inc(g.dma_start(out=env.dbg[:, 1, :], in_=env.kf[:, 0, :]), 16)
            env.s_st.inc(g.dma_start(out=env.dbg[:, 2, :], in_=env.xln_fm[:, 0, :]), 16)
            env.s_st.inc(g.dma_start(out=env.dbg[:, 3, :], in_=env.xln_fm[:, 4, :]), 16)
            g.wait_ge(env.s_st.h, env.s_st.n)
        blk.gpsimd(gpj)
        barrier(env); return True
    def v3(v):
        v.wait_ge(sc.h, sc_evac)
        v.tensor_tensor(env.qf[:, 1, :], env.qf[:, 0, :], env.sinf[:, :], ALU.mult)
        v.tensor_tensor(env.kf[:, 1, :], env.kf[:, 0, :], env.sinf[:, :], ALU.mult)
        v.tensor_tensor(env.qf[:, 0, :], env.qf[:, 0, :], env.cosf[:, :], ALU.mult)
        ve.inc(v.tensor_tensor(env.kf[:, 0, :], env.kf[:, 0, :], env.cosf[:, :], ALU.mult))
    blk.vector(v3)
    if hstop == "rot":
        barrier(env); return True

    ve_rot = ve.n
    def tw(t):
        t.wait_ge(ve.h, ve_rot)
    blk.tensor(tw)
    pairs = []
    for tc in range(NT):
        for e in range(2):
            pairs.append((env.kf[:, e, tc*128:(tc+1)*128], env.ktm[:, tc, e*128:(e+1)*128]))
    transpose_tiles(env, pairs)
    if hstop == "ktm":
        barrier(env); return True

    mm_vw = mm.n
    def gv(g, _mm=mm_vw):
        g.wait_ge(mm.h, _mm)
        for c in range(NCH):
            ld2.inc(g.dma_start(out=env.wpro[:, c, :], in_=wT[c*128:(c+1)*128, 2048+h*128:2048+(h+1)*128]), 16)
    blk.gpsimd(gv)
    ld_v = ld2.n
    mm_v0 = mm.n
    def t3(t):
        t.wait_ge(ld2.h, ld_v)
        for tc in range(NT):
            for c in range(NCH):
                t.matmul(env.ps_c[:, :], env.xln_fm[:, c, tc*128:(tc+1)*128], env.wpro[:, c, :],
                         start=(c == 0), stop=False)
            mm.inc(t.matmul(env.ps_c[:, :], env.onesb[:, :], env.l0bias_rowb[:, h*128:(h+1)*128],
                     start=False, stop=True))
            t.wait_ge(ve2.h, ve2_v0 + tc + 1)
    ve2_v0 = ve2.n
    def v4(v):
        for tc in range(NT):
            v.wait_ge(mm.h, mm_v0 + tc + 1)
            ve2.inc(v.tensor_copy(env.vtm[:, tc, :], env.ps_c[:, :]))
    blk.tensor(t3); blk.vector(v4)
    if hstop == "vtm":
        barrier(env); return True

    def s5(s):
        s.activation(out=env.scr2[0:1, 0:1], in_=env.l0slope[:, h:h+1], func=AF.Exp)
        sc.inc(s.copy(out=env.fceA[:, :], in_=env.fceB[:, :]))
    blk.scalar(s5)
    sc_m1 = sc.n
    def t4(t):
        t.wait_ge(sc.h, sc_m1)
        mm.inc(t.matmul(env.ps_c[:, 0:1], env.onesf[:, :], env.scr2[0:1, 0:1], start=True, stop=True))
    blk.tensor(t4)
    mm_m1 = mm.n
    def v6(v):
        v.wait_ge(mm.h, mm_m1)
        v.tensor_scalar(out=env.acol[:, 0:1], in0=env.ps_c[:, 0:1], scalar1=-1.0, scalar2=None, op0=ALU.add)
        v.memset(env.Psb[:, :], 0.0)
        ve.inc(v.tensor_scalar(out=env.maskh[:, :], in0=env.triub[:, :], scalar1=env.acol[:, 0:1], scalar2=None,
                               op0=ALU.mult))
    blk.vector(v6)
    barrier(env)
    if hstop == "mask":
        return True

    mm_s0 = mm.n
    ve_s0 = ve.n
    def t5(t):
        for c in range(NT):
            t.matmul(env.ps_a[:, 0:128], env.kf[:, 0, c*128:(c+1)*128], env.qf[:, 0, c*128:(c+1)*128], start=True, stop=False)
            mm.inc(t.matmul(env.ps_a[:, 0:128], env.kf[:, 1, c*128:(c+1)*128], env.qf[:, 1, c*128:(c+1)*128], start=False, stop=True))
            t.matmul(env.ps_P0[:, :], env.ktm[:, c, 0:128], env.vtm[:, c, :], start=True, stop=True)
            mm.inc(t.matmul(env.ps_P1[:, :], env.ktm[:, c, 128:256], env.vtm[:, c, :], start=True, stop=True))
            t.wait_ge(ve.h, ve_s0 + c + 1)
    def v7(v):
        for c in range(NT):
            snap = env.Psnap1[:, c, :] if c < 8 else env.Psnap2[:, c - 8, :]
            v.wait_ge(mm.h, mm_s0 + 2*c + 1)
            v.tensor_tensor(env.Sp[:, c, :], env.maskh[:, :], env.ps_a[:, 0:128], ALU.mult)
            v.wait_ge(mm.h, mm_s0 + 2*c + 2)
            v.tensor_tensor(env.Psb[:, 0:128], env.Psb[:, 0:128], env.ps_P0[:, :], ALU.add)
            v.tensor_tensor(env.Psb[:, 128:256], env.Psb[:, 128:256], env.ps_P1[:, :], ALU.add)
            ve.inc(v.tensor_copy(snap, env.Psb[:, :]))
    blk.tensor(t5); blk.vector(v7)
    if hstop == "sweep1x":
        barrier(env)
        def gs1(g):
            env.s_st.inc(g.dma_start(out=env.dbg[:, 0, :], in_=env.cosf[:, :]), 16)
            env.s_st.inc(g.dma_start(out=env.dbg[:, 1, :], in_=env.scr1[:, :]), 16)
            env.s_st.inc(g.dma_start(out=env.dbg[:, 2, :], in_=env.scr2[:, :]), 16)
            env.s_st.inc(g.dma_start(out=env.dbg[:, 3, 0:256], in_=env.Psb[:, :]), 16)
            env.s_st.inc(g.dma_start(out=env.dbg[:, 4, 0:2048], in_=env.ktm[:, :, :].rearrange("p t c -> p (t c)")[:, 0:2048]), 16)
            env.s_st.inc(g.dma_start(out=env.dbg[:, 5, 0:2048], in_=env.vtm[:, :, :].rearrange("p t c -> p (t c)")), 16)
            g.wait_ge(env.s_st.h, env.s_st.n)
        blk.gpsimd(gs1)
        barrier(env)
        return True
    mm_sweep1 = mm.n
    ve_sweep1 = ve.n
    def s7(s):
        s.wait_ge(mm.h, mm_sweep1)
        s.wait_ge(ve.h, ve_sweep1)
        sc.inc(s.copy(out=env.Pfin[:, :], in_=env.Psb[:, :]))
    blk.scalar(s7)
    sc_pfin = sc.n

    ve_mc0 = ve.n
    mm_s2 = mm.n
    def vmc(v):
        v.wait_ge(sc.h, sc_pfin)
        for c in range(1, NT):
            snap = env.Psnap1[:, c-1, :] if c-1 < 8 else env.Psnap2[:, c-9, :]
            if c >= 5:
                v.wait_ge(mm.h, mm_s2 + c - 3)
            ve.inc(v.scalar_tensor_tensor(out=env.Mc[:, (c-1) % 4, :], in0=snap, scalar=env.acol[:, 0:1],
                                          in1=env.Psb[:, :], op0=ALU.mult, op1=ALU.add))
    blk.vector(vmc)
    def t6(t):
        t.wait_ge(sc.h, sc_pfin)
        for c in range(NT):
            ps = env.ps_c if c % 2 == 0 else env.ps_d
            t.matmul(ps[:, :], env.Sp[:, c, :], env.vtm[:, c, :], start=True, stop=False)
            if c == 0:
                t.matmul(ps[:, :], env.qf[:, 0, c*128:(c+1)*128], env.Pfin[:, 0:128], start=False, stop=False)
                mm.inc(t.matmul(ps[:, :], env.qf[:, 1, c*128:(c+1)*128], env.Pfin[:, 128:256], start=False, stop=True))
            else:
                t.wait_ge(ve.h, ve_mc0 + c)
                mc = env.Mc[:, (c-1) % 4, :]
                t.matmul(ps[:, :], env.qf[:, 0, c*128:(c+1)*128], mc[:, 0:128], start=False, stop=False)
                mm.inc(t.matmul(ps[:, :], env.qf[:, 1, c*128:(c+1)*128], mc[:, 128:256], start=False, stop=True))
            if c >= 1:
                t.wait_ge(sc2.h, sc2_o0 + c)
    sc2_o0 = sc2.n
    def s8(s):
        for c in range(NT):
            s.wait_ge(mm.h, mm_s2 + c + 1)
            sc2.inc(s.copy(out=env.attn_out[:, c, h*128:(h+1)*128], in_=(env.ps_c if c % 2 == 0 else env.ps_d)[:, :]))
    blk.tensor(t6); blk.scalar(s8)
    barrier(env)


def _l0_epilogue(env):
    blk = env.blk
    mm, ve, sc, ld = env.s_mm, env.s_ve, env.s_sc, env.s_ld
    ld2 = env.s_ld2
    stat, scr = env.stat, env.scr1
    EPS = 1e-6
    def v(v):
        for tc in range(NT):
            v.scalar_tensor_tensor(out=scr[:, 0:1024], in0=env.attn_out[:, tc, :], scalar=1.0,
                                   in1=env.attn_out[:, tc, :], op0=ALU.mult, op1=ALU.mult,
                                   accum_out=stat[:, tc, 3:4])
        for tc in range(NT):
            v.tensor_scalar(out=stat[:, tc, 6:7], in0=stat[:, tc, 3:4], scalar1=1.0/D, scalar2=EPS, op0=ALU.mult, op1=ALU.add)
        ve.inc(v.tensor_copy(env.fceA[:, :], env.fceB[:, :]))
    blk.vector(v)
    ve1 = ve.n
    def s(s):
        s.wait_ge(ve.h, ve1)
        for tc in range(NT):
            s.activation(out=stat[:, tc, 7:8], in_=stat[:, tc, 6:7], func=AF.Sqrt)
        sc.inc(s.copy(out=env.fceA[:, :], in_=env.fceB[:, :]))
    blk.scalar(s)
    sc1 = sc.n
    def v2(v):
        v.wait_ge(sc.h, sc1)
        for tc in range(NT):
            v.reciprocal(stat[:, tc, 1:2], stat[:, tc, 7:8])
        for tc in range(NT):
            v.tensor_tensor(stat[:, tc, 5:6], stat[:, tc, 1:2], stat[:, tc, 1:2], ALU.mult)
        for tc in range(NT):
            v.tensor_tensor(stat[:, tc, 5:6], stat[:, tc, 5:6], stat[:, tc, 6:7], ALU.mult)
        for tc in range(NT):
            v.tensor_scalar(out=stat[:, tc, 5:6], in0=stat[:, tc, 5:6], scalar1=-0.5, scalar2=1.5, op0=ALU.mult, op1=ALU.add)
        for tc in range(NT):
            v.tensor_tensor(stat[:, tc, 1:2], stat[:, tc, 1:2], stat[:, tc, 5:6], ALU.mult)
        ve.inc(v.tensor_copy(env.fceA[:, :], env.fceB[:, :]))
    blk.vector(v2)
    wT = env.dram["l0_qkvu_wT"]
    scr2 = env.scr2
    for half in range(2):
        def g(g, _h=half):
            for c in range(NCH):
                ld2.inc(g.dma_start(out=env.wbig[:, c, :], in_=wT[c*128:(c+1)*128, 3072+_h*512:3072+(_h+1)*512]), 16)
        blk.gpsimd(g)
        ld_u = env.s_ld2.n
        barrier(env)
        mm0 = mm.n
        ve_z0 = ve.n
        def t(t, _h=half, _ld=ld_u, _mm0=mm0, _ve0=ve_z0):
            t.wait_ge(env.s_ld2.h, _ld)
            for tc in range(NT):
                ps = env.ps_a if tc % 2 == 0 else env.ps_b
                for c in range(NCH):
                    t.matmul(ps[:, :], env.xln_fm[:, c, tc*128:(tc+1)*128], env.wbig[:, c, :],
                             start=(c == 0), stop=False)
                mm.inc(t.matmul(ps[:, :], env.onesb[:, :], env.l0bias_rowb[:, 1024+_h*512:1024+(_h+1)*512],
                         start=False, stop=True))
                if tc >= 1:
                    t.wait_ge(ve.h, _ve0 + tc)
        def v3(v, _h=half, _mm0=mm0):
            for tc in range(NT):
                v.wait_ge(mm.h, _mm0 + tc + 1)
                ps = env.ps_a if tc % 2 == 0 else env.ps_b
                v.tensor_scalar(out=scr2[:, 0:512], in0=env.attn_out[:, tc, _h*512:(_h+1)*512],
                                scalar1=stat[:, tc, 1:2], scalar2=None, op0=ALU.mult)
                ve.inc(v.tensor_tensor(env.attn_out[:, tc, _h*512:(_h+1)*512], scr2[:, 0:512], ps[:, :], ALU.mult))
        blk.tensor(t); blk.vector(v3)
        barrier(env)

    z_fm = env.xln_fm
    pairs = []
    for tc in range(NT):
        for c in range(NCH):
            pairs.append((env.attn_out[:, tc, c*128:(c+1)*128], z_fm[:, c, tc*128:(tc+1)*128]))
    transpose_tiles(env, pairs)
    barrier(env)

    wTo = env.dram["l0_out_wT"]
    for half in range(2):
        def g2(g, _h=half):
            for c in range(NCH):
                ld2.inc(g.dma_start(out=env.wbig[:, c, :], in_=wTo[c*128:(c+1)*128, _h*512:(_h+1)*512]), 16)
        blk.gpsimd(g2)
        ld_o = env.s_ld2.n
        barrier(env)
        mm_o0 = mm.n
        ve_o0 = ve.n
        def t3(t, _h=half, _ld=ld_o, _mm0=mm_o0, _ve0=ve_o0):
            t.wait_ge(env.s_ld2.h, _ld)
            for tc in range(NT):
                ps = env.ps_a if tc % 2 == 0 else env.ps_b
                for c in range(NCH):
                    t.matmul(ps[:, :], z_fm[:, c, tc*128:(tc+1)*128], env.wbig[:, c, :],
                             start=(c == 0), stop=False)
                mm.inc(t.matmul(ps[:, :], env.onesb[:, :], env.l0bo[:, _h*512:(_h+1)*512],
                         start=False, stop=True))
                if tc >= 1:
                    t.wait_ge(ve.h, _ve0 + tc)
        def v4(v, _h=half, _mm0=mm_o0):
            for tc in range(NT):
                v.wait_ge(mm.h, _mm0 + tc + 1)
                ps = env.ps_a if tc % 2 == 0 else env.ps_b
                ve.inc(v.tensor_tensor(env.stream[:, tc, _h*512:(_h+1)*512], env.stream[:, tc, _h*512:(_h+1)*512],
                                       ps[:, :], ALU.add))
        blk.tensor(t3); blk.vector(v4)
        barrier(env)


def _wquant_matrix(env, wT_dram, out_dram, nchunks, width, mcol):
    blk = env.blk
    ld, mm, ve, sc, gp = env.s_ld, env.s_mm, env.s_ve, env.s_sc, env.s_gp
    HW = 2048
    nh = width // HW if width >= HW else 1
    CW = min(width, HW)
    nld = nchunks * nh
    ve0 = ve.n
    ld_base = ld.n
    def floads(s):
        for k in range(nld):
            c, hh = divmod(k, nh)
            buf = env.scr1 if k % 2 == 0 else env.scr2
            if k >= 2:
                s.wait_ge(ve.h, ve0 + (k - 2 + 1))
            ld.inc(s.dma_start(out=buf[:, 0:CW], in_=wT_dram[c*128:(c+1)*128, hh*CW:(hh+1)*CW]), 16)
    blk.sync(floads)
    def vred(v):
        v.memset(env.wacc[:, 0:2], 0.0)
        for k in range(nld):
            buf = env.scr1 if k % 2 == 0 else env.scr2
            v.wait_ge(ld.h, ld_base + 16*(k+1))
            v.tensor_reduce(env.wacc[:, 2+k%2:3+k%2], buf[:, 0:CW], axis=AX, op=ALU.add, apply_absolute_value=True)
            if k >= 1:
                v.tensor_tensor(env.wacc[:, (k-1)%2:(k-1)%2+1], env.wacc[:, (k-1)%2:(k-1)%2+1], env.wacc[:, 2+(k-1)%2:3+(k-1)%2], ALU.add)
            ve.inc(v.tensor_copy(env.fceA[:, :], env.fceB[:, :]))
        v.tensor_scalar(out=env.wacc[:, 5:6], in0=env.wacc[:, 2:3], scalar1=1.0, scalar2=None, op0=ALU.mult)
        v.tensor_tensor(env.wacc[:, (nld-1)%2:(nld-1)%2+1], env.wacc[:, (nld-1)%2:(nld-1)%2+1], env.wacc[:, 2+(nld-1)%2:3+(nld-1)%2], ALU.add)
        v.tensor_scalar(out=env.wacc[:, 5:6], in0=env.wacc[:, 2:3], scalar1=1.0, scalar2=None, op0=ALU.mult)
        v.tensor_tensor(env.wacc[:, 4:5], env.wacc[:, 0:1], env.wacc[:, 1:2], ALU.add)
    blk.vector(vred)
    barrier(env)
    mm0 = mm.n
    def tsum(t):
        mm.inc(t.matmul(env.ps_c[0:1, 0:1], env.wacc[:, 4:5], env.onescol[:, :], start=True, stop=True))
    blk.tensor(tsum)
    mm_s = mm.n
    def vmean(v):
        v.wait_ge(mm.h, mm_s)
        ve.inc(v.tensor_scalar(out=env.wacc[0:1, 5:6], in0=env.ps_c[0:1, 0:1],
                               scalar1=1.0/(nchunks*128*width), scalar2=1e-5, op0=ALU.mult, op1=ALU.max))
    blk.vector(vmean)
    ve_m = ve.n
    def tb(t):
        t.wait_ge(ve.h, ve_m)
        mm.inc(t.matmul(env.ps_c[:, 0:1], env.onesf[:, :], env.wacc[0:1, 5:6], start=True, stop=True))
    blk.tensor(tb)
    mm_b = mm.n
    def vb(v):
        v.wait_ge(mm.h, mm_b)
        v.tensor_copy(env.wcols[:, mcol:mcol+1], env.ps_c[:, 0:1])
        ve.inc(v.reciprocal(env.wcols[:, mcol+1:mcol+2], env.ps_c[:, 0:1]))
    blk.vector(vb)
    barrier(env)
    ld_b2 = ld.n
    ve_q0 = ve.n
    gp_q0 = env.s_gd.n
    def floads2(s):
        for k in range(nld):
            c, hh = divmod(k, nh)
            buf = env.scr1 if k % 2 == 0 else env.scr2
            if k >= 2:
                s.wait_ge(ve.h, ve_q0 + (k - 2 + 1))
            ld.inc(s.dma_start(out=buf[:, 0:CW], in_=wT_dram[c*128:(c+1)*128, hh*CW:(hh+1)*CW]), 16)
    blk.sync(floads2)
    def vq(v):
        for k in range(nld):
            buf = env.scr1 if k % 2 == 0 else env.scr2
            stg = env.kf[:, k % 2, :]
            v.wait_ge(ld.h, ld_b2 + 16*(k+1))
            if k >= 2:
                v.wait_ge(env.s_gd.h, gp_q0 + 16*(k - 2 + 1))
            v.tensor_scalar(out=buf[:, 0:CW], in0=buf[:, 0:CW], scalar1=env.wcols[:, mcol+1:mcol+2], scalar2=RNE_C, op0=ALU.mult, op1=ALU.add)
            v.tensor_scalar(out=buf[:, 0:CW], in0=buf[:, 0:CW], scalar1=RNE_C, scalar2=None, op0=ALU.subtract)
            ve.inc(v.tensor_scalar(out=stg[0:128, 0:CW], in0=buf[:, 0:CW], scalar1=1.0, scalar2=-1.0, op0=ALU.min, op1=ALU.max))
    blk.vector(vq)
    def gst(g):
        for k in range(nld):
            c, hh = divmod(k, nh)
            stg = env.kf[:, k % 2, :]
            g.wait_ge(ve.h, ve_q0 + (k + 1))
            env.s_gd.inc(g.dma_start(out=out_dram[c, :, hh*CW:(hh+1)*CW], in_=stg[0:128, 0:CW]), 16)
    blk.gpsimd(gst)
    barrier(env)


def p_l0_ffn(env):
    blk = env.blk
    ld, ld2, mm, ve, sc, gp = env.s_ld, env.s_ld2, env.s_mm, env.s_ve, env.s_sc, env.s_gp
    ve2, sc2 = env.s_ve2, env.s_sc2
    stat, scr1, scr2 = env.stat, env.scr1, env.scr2
    emit_layernorm(env, need_fm=False)

    _wquant_matrix(env, env.dram["l0f_wgT"], env.wqg_dram, NCH, 4096, 0)
    _wquant_matrix(env, env.dram["l0f_wuT"], env.wqu_dram, NCH, 4096, 2)
    _wquant_matrix(env, env.dram["l0f_wdT"], env.wqd_dram, 32, 1024, 4)
    import os as _os0
    if _os0.environ.get("MK_L0F") == "wq":
        barrier(env); return

    xq_fm = env.xln_fm
    xqt = env.vtm.ap().rearrange("p t c -> p (t c)")[:, 0:1024]
    for tc in range(NT):
        ve_t = ve.n
        mm_prev = mm.n
        def va(v, _tc=tc, _mmp=mm_prev):
            if _tc >= 1:
                v.wait_ge(mm.h, _mmp)
            v.tensor_scalar(out=scr1[:, 0:1024], in0=env.stream[:, _tc, :], scalar1=stat[:, _tc, 1:2], scalar2=stat[:, _tc, 4:5], op0=ALU.mult, op1=ALU.add)
            v.tensor_reduce(stat[:, _tc, 0:1], scr1[:, 0:1024], axis=AX, op=ALU.max, apply_absolute_value=True)
            v.tensor_scalar(out=scr1[:, 0:1024], in0=scr1[:, 0:1024], scalar1=1.0, scalar2=None, op0=ALU.mult)
            v.tensor_scalar(out=stat[:, _tc, 3:4], in0=stat[:, _tc, 0:1], scalar1=1e-5, scalar2=1.0/127.0, op0=ALU.max, op1=ALU.mult)
            v.tensor_scalar(out=scr2[:, 0:1024], in0=scr1[:, 0:1024], scalar1=1.0, scalar2=None, op0=ALU.mult)
            v.reciprocal(stat[:, _tc, 2:3], stat[:, _tc, 3:4])
            v.tensor_scalar(out=scr2[:, 0:1024], in0=scr1[:, 0:1024], scalar1=1.0, scalar2=None, op0=ALU.mult)
            v.tensor_scalar(out=scr1[:, 0:1024], in0=scr1[:, 0:1024], scalar1=stat[:, _tc, 2:3], scalar2=RNE_C, op0=ALU.mult, op1=ALU.add)
            v.tensor_scalar(out=scr1[:, 0:1024], in0=scr1[:, 0:1024], scalar1=RNE_C, scalar2=None, op0=ALU.subtract)
            ve.inc(v.tensor_scalar(out=xqt, in0=scr1[:, 0:1024], scalar1=127.0, scalar2=-128.0, op0=ALU.min, op1=ALU.max))
        blk.vector(va)
        pairs = []
        for c in range(NCH):
            pairs.append((xqt.rearrange("p (c x) -> p c x", c=NCH)[:, c, :], xq_fm[:, c, tc*128:(tc+1)*128]))
        def tw(t, _v=ve.n):
            t.wait_ge(ve.h, _v)
        blk.tensor(tw)
        transpose_tiles(env, pairs)
    barrier(env)

    if _os0.environ.get("MK_L0F") == "aq":
        return
    _l0ffn_pass1(env)
    if _os0.environ.get("MK_L0F") == "p1":
        return
    _l0ffn_pass2(env)


def _l0ffn_pass1(env):
    blk = env.blk
    ld, mm, ve, sc, gp = env.s_ld, env.s_mm, env.s_ve, env.s_sc, env.s_gp
    stat, scr1, scr2 = env.stat, env.scr1, env.scr2
    xq_fm = env.xln_fm
    wsl = env.xln_tm.ap().rearrange("p t c -> p (t c)")
    wg_sl = wsl[:, 0:8192].rearrange("p (c i) -> p c i", c=NCH)
    wu_sl = wsl[:, 8192:16384].rearrange("p (c i) -> p c i", c=NCH)
    gtile = env.ktm.ap().rearrange("p t c -> p (t c)")[:, 0:1024]
    utile = env.ktm.ap().rearrange("p t c -> p (t c)")[:, 1024:2048]
    h2stg = env.ktm.ap().rearrange("p t c -> p (t c)")[:, 2048:3072]
    def vz(v):
        for tc in range(NT):
            v.memset(stat[:, tc, 5:6], 0.0)
        for tc in range(NT):
            v.tensor_tensor(env.stat2[:, tc, 0:1], stat[:, tc, 3:4], env.wcols[:, 0:1], ALU.mult)
        for tc in range(NT):
            ve.inc(v.tensor_tensor(env.stat2[:, tc, 1:2], stat[:, tc, 3:4], env.wcols[:, 2:3], ALU.mult))
    blk.vector(vz)
    barrier(env)
    for isl in range(4):
        ld0 = ld.n
        def fw(s, _i=isl):
            for c in range(NCH):
                ld.inc(s.dma_start(out=wg_sl[:, c, :], in_=env.wqg_dram[c, :, _i*1024:(_i+1)*1024]), 16)
                ld.inc(s.dma_start(out=wu_sl[:, c, :], in_=env.wqu_dram[c, :, _i*1024:(_i+1)*1024]), 16)
        blk.sync(fw)
        ld_w = ld.n
        mm0 = mm.n
        sc_e0 = sc.n
        ve_e0 = ve.n
        gp_h0 = env.s_gd.n
        def tmm(t, _ld=ld_w, _sc0=sc_e0, _mm0=mm0):
            t.wait_ge(ld.h, _ld)
            for tc in range(NT):
                for half in range(2):
                    for role in range(2):
                        w = (wg_sl, wu_sl)[role]
                        ps = (env.ps_a, env.ps_b)[role]
                        k4 = tc*4 + half*2 + role
                        if k4 >= 2:
                            t.wait_ge(sc.h, _sc0 + (k4 - 2) + 1)
                        for c in range(NCH):
                            last = (c == NCH-1)
                            ins = t.matmul(ps[:, :], xq_fm[:, c, tc*128:(tc+1)*128], w[:, c, half*512:(half+1)*512],
                                           start=(c == 0), stop=last)
                            if last:
                                mm.inc(ins)
        def sevac(s, _mm0=mm0, _sc0=sc_e0, _ve0=ve_e0):
            for tc in range(NT):
                if tc >= 1:
                    s.wait_ge(ve.h, _ve0 + tc)
                for half in range(2):
                    s.wait_ge(mm.h, _mm0 + tc*4 + half*2 + 1)
                    sc.inc(s.activation(out=gtile.rearrange("p (h x) -> p h x", h=2)[:, half, :], in_=env.ps_a[:, :],
                                        func=AF.Silu, scale=env.stat2[:, tc, 0:1]))
                    s.wait_ge(mm.h, _mm0 + tc*4 + half*2 + 2)
                    sc.inc(s.activation(out=utile.rearrange("p (h x) -> p h x", h=2)[:, half, :], in_=env.ps_b[:, :],
                                        func=AF.Identity, scale=env.stat2[:, tc, 1:2]))
        def vprod(v, _sc0=sc_e0, _ve0=ve_e0, _gp0=gp_h0):
            for tc in range(NT):
                v.wait_ge(sc.h, _sc0 + tc*4 + 4)
                if tc >= 1:
                    v.wait_ge(env.s_gd.h, _gp0 + 16*tc)
                v.tensor_tensor(scr1[:, 0:1024], gtile, utile, ALU.mult)
                v.tensor_reduce(stat[:, tc, 7:8], scr1[:, 0:1024], axis=AX, op=ALU.max, apply_absolute_value=True)
                v.tensor_copy(h2stg, scr1[:, 0:1024])
                ve.inc(v.tensor_tensor(stat[:, tc, 5:6], stat[:, tc, 5:6], stat[:, tc, 7:8], ALU.max))
        def gh2(g, _ve0=ve_e0, _i=isl):
            for tc in range(NT):
                g.wait_ge(ve.h, _ve0 + tc + 1)
                env.s_gd.inc(g.dma_start(out=env.h2_dram[tc, _i, :, :], in_=h2stg), 16)
        blk.tensor(tmm); blk.scalar(sevac); blk.vector(vprod); blk.gpsimd(gh2)
        barrier(env)
    def vf(v):
        for tc in range(NT):
            v.tensor_scalar(out=stat[:, tc, 3:4], in0=stat[:, tc, 5:6], scalar1=1e-5, scalar2=1.0/127.0, op0=ALU.max, op1=ALU.mult)
        for tc in range(NT):
            v.reciprocal(stat[:, tc, 2:3], stat[:, tc, 3:4])
        for tc in range(NT):
            ve.inc(v.tensor_tensor(env.stat2[:, tc, 2:3], stat[:, tc, 3:4], env.wcols[:, 4:5], ALU.mult))
    blk.vector(vf)
    barrier(env)


def _l0ffn_pass2(env):
    blk = env.blk
    ld, mm, ve, sc, gp = env.s_ld, env.s_mm, env.s_ve, env.s_sc, env.s_gp
    stat, scr1 = env.stat, env.scr1
    wsl = env.xln_tm.ap().rearrange("p t c -> p (t c)")
    wd_sl2 = [wsl[:, 0:8192].rearrange("p (c o) -> p c o", c=NCH),
              wsl[:, 8192:16384].rearrange("p (c o) -> p c o", c=NCH)]
    ktmf = env.ktm.ap().rearrange("p t c -> p (t c)")
    vtmf = env.vtm.ap().rearrange("p t c -> p (t c)")
    h2t2 = [ktmf[:, 0:1024], ktmf[:, 2048:3072]]
    xq2t2 = [ktmf[:, 1024:2048], ktmf[:, 3072:4096]]
    xq2fm2 = [vtmf[:, 0:1024].rearrange("p (c x) -> p c x", c=8),
              vtmf[:, 1024:2048].rearrange("p (c x) -> p c x", c=8)]
    vq_done = {}
    mm_done = {}
    for ib in range(4):
        mm_wprev = mm_done.get((ib-2)*NT + NT - 1, 0)
        def fw(s, _ib=ib, _mw=mm_wprev):
            if _ib >= 2:
                s.wait_ge(mm.h, _mw)
            for c in range(NCH):
                ld.inc(s.dma_start(out=wd_sl2[_ib % 2][:, c, :], in_=env.wqd_dram[_ib*8 + c, :, :]), 16)
        blk.sync(fw)
        ld_w = ld.n
        for tc in range(NT):
            k = ib * NT + tc
            pp = k % 2
            h2t = h2t2[pp]; xq2t = xq2t2[pp]; xq2fm = xq2fm2[pp]
            def gh(g, _ib=ib, _tc=tc, _vp=vq_done.get(k-2, 0), _h2=h2t, _k=k):
                if _k >= 2:
                    g.wait_ge(ve.h, _vp)
                g.dma_start(out=_h2, in_=env.h2_dram[_tc, _ib, :, :]).then_inc(env.s_ld2.h, 16)
            env.s_ld2.n += 16
            blk.gpsimd(gh)
            def vq(v, _tc=tc, _l2=env.s_ld2.n, _mmp=mm_done.get(k-2, 0), _k=k, _h2=h2t, _xq=xq2t):
                v.wait_ge(env.s_ld2.h, _l2)
                if _k >= 2:
                    v.wait_ge(mm.h, _mmp)
                v.tensor_scalar(out=scr1[:, 0:1024], in0=_h2, scalar1=stat[:, _tc, 2:3], scalar2=RNE_C, op0=ALU.mult, op1=ALU.add)
                v.tensor_scalar(out=scr1[:, 0:1024], in0=scr1[:, 0:1024], scalar1=RNE_C, scalar2=None, op0=ALU.subtract)
                ve.inc(v.tensor_scalar(out=_xq, in0=scr1[:, 0:1024], scalar1=127.0, scalar2=-128.0, op0=ALU.min, op1=ALU.max))
            blk.vector(vq)
            vq_done[k] = ve.n
            ve_done = ve.n
            def twt(t, _v=ve_done):
                t.wait_ge(ve.h, _v)
            blk.tensor(twt)
            pairs = []
            for c in range(NCH):
                pairs.append((xq2t.rearrange("p (c x) -> p c x", c=NCH)[:, c, :], xq2fm[:, c, :]))
            transpose_tiles(env, pairs)
            def tdn(t, _ld=ld_w, _tc=tc, _ib=ib, _xf=xq2fm):
                t.wait_ge(ld.h, _ld)
                for half in range(2):
                    ps = (env.ps_a, env.ps_b)[half]
                    for c in range(NCH):
                        ins = t.matmul(ps[:, :], _xf[:, c, :], wd_sl2[_ib % 2][:, c, half*512:(half+1)*512],
                                       start=(c == 0), stop=(c == NCH-1))
                        if c == NCH-1:
                            mm.inc(ins)
            blk.tensor(tdn)
            mm_done[k] = mm.n
            mm_d = mm.n
            def vadd(v, _tc=tc, _mm=mm_d):
                v.wait_ge(mm.h, _mm - 1)
                v.scalar_tensor_tensor(out=env.stream[:, _tc, 0:512], in0=env.ps_a[:, :], scalar=env.stat2[:, _tc, 2:3],
                                       in1=env.stream[:, _tc, 0:512], op0=ALU.mult, op1=ALU.add)
                v.wait_ge(mm.h, _mm)
                ve.inc(v.scalar_tensor_tensor(out=env.stream[:, _tc, 512:1024], in0=env.ps_b[:, :], scalar=env.stat2[:, _tc, 2:3],
                                       in1=env.stream[:, _tc, 512:1024], op0=ALU.mult, op1=ALU.add))
            blk.vector(vadd)
            ve_a = ve.n
            def tg(t, _v=ve_a):
                t.wait_ge(ve.h, _v)
            blk.tensor(tg)
        barrier(env)


def p_l1_attn(env):
    blk = env.blk
    ld2, mm, ve, sc = env.s_ld2, env.s_mm, env.s_ve, env.s_sc
    ve2, sc2, mm2 = env.s_ve2, env.s_sc2, env.s_mm2
    stat, scr1, scr2 = env.stat, env.scr1, env.scr2
    emit_layernorm(env, need_fm=True)
    barrier(env)
    SCL = float(128.0 ** -0.5)

    kfm = env.qf.ap()[:, 0, :]
    ktmp = env.qf.ap()[:, 1, :]
    qfm = env.kf
    cosr = env.cosf
    sinr = env.sinf
    wq_sb = env.wpro
    attn_tm = env.ktm
    dmask = env.maskh
    qraw = scr2[:, 0:1024].bitcast(BF16)
    probs_b = scr2[:, 0:1024].bitcast(BF16)

    def gld(g):
        ld2.inc(g.dma_start(out=cosr[0:64, :], in_=env.dram["c_rope_cos"][:, :]), 16)
        ld2.inc(g.dma_start(out=cosr[64:128, :], in_=env.dram["c_rope_cos"][:, :]), 16)
        ld2.inc(g.dma_start(out=sinr[0:64, :], in_=env.dram["c_rope_sin"][:, :]), 16)
        ld2.inc(g.dma_start(out=sinr[64:128, :], in_=env.dram["c_rope_sin"][:, :]), 16)
        ld2.inc(g.dma_start(out=dmask[:, :], in_=env.dram["c_diagmask"][:, :]), 16)
        for c in range(NCH):
            ld2.inc(g.dma_start(out=wq_sb[:, c, :], in_=env.dram["l1_wkT"][c*128:(c+1)*128, :]), 16)
    blk.gpsimd(gld)
    ld_k = ld2.n

    mm0 = mm.n
    def tk(t):
        t.wait_ge(ld2.h, ld_k)
        for n4 in range(4):
            ps = env.ps_a if n4 % 2 == 0 else env.ps_b
            if n4 >= 2:
                t.wait_ge(ve.h, ve_k0 + (n4 - 2 + 1))
            for c in range(NCH):
                mm.inc(t.matmul(ps[:, 0:512], wq_sb[:, c, :], env.xln_fm[:, c, n4*512:(n4+1)*512],
                                start=(c == 0), stop=(c == NCH-1)))
    ve_k0 = ve.n
    def vk(v):
        for n4 in range(4):
            ps = env.ps_a if n4 % 2 == 0 else env.ps_b
            v.wait_ge(mm.h, mm0 + (n4+1)*NCH)
            ve.inc(v.tensor_copy(ktmp.rearrange("p (a b) -> p a b", a=4)[:, n4, :], ps[:, 0:512]))
    blk.tensor(tk); blk.vector(vk)
    ve_kraw = ve.n
    kswp = scr1[:, 0:1024].bitcast(BF16)
    rtmp = scr1[:, 1024:2048].bitcast(BF16)
    def gkswp(g):
        g.wait_ge(ve.h, ve_kraw)
        ld2.inc(g.dma_start(out=kswp[0:64, :], in_=ktmp[64:128, :]), 16)
        ld2.inc(g.dma_start(out=kswp[64:128, :], in_=ktmp[0:64, :]), 16)
    blk.gpsimd(gkswp)
    ld_kswp = ld2.n
    def vrot(v):
        v.wait_ge(ld2.h, ld_kswp)
        v.tensor_tensor(kfm[:, :], ktmp[:, :], cosr[:, :], ALU.mult)
        v.tensor_tensor(rtmp[:, :], kswp[:, :], sinr[:, :], ALU.mult)
        v.tensor_tensor(kfm[0:64, :], kfm[0:64, :], rtmp[0:64, :], ALU.subtract)
        ve.inc(v.tensor_tensor(kfm[64:128, :], kfm[64:128, :], rtmp[64:128, :], ALU.add))
    blk.vector(vrot)

    def gv(g):
        g.wait_ge(mm.h, mm0 + 32)
        for c in range(NCH):
            ld2.inc(g.dma_start(out=wq_sb[:, c, :], in_=env.dram["l1_wvT"][c*128:(c+1)*128, :]), 16)
    blk.gpsimd(gv)
    ld_v = ld2.n
    mm_v0 = mm.n
    def tv(t):
        t.wait_ge(ld2.h, ld_v)
        for tc in range(NT):
            for c in range(NCH):
                mm.inc(t.matmul(env.ps_c[:, :], env.xln_fm[:, c, tc*128:(tc+1)*128], wq_sb[:, c, :],
                                start=(c == 0), stop=(c == NCH-1)))
            t.wait_ge(ve2.h, ve2_v0 + tc + 1)
    ve2_v0 = ve2.n
    def vv(v):
        for tc in range(NT):
            v.wait_ge(mm.h, mm_v0 + (tc+1)*NCH)
            ve2.inc(v.tensor_copy(env.vtm[:, tc, :], env.ps_c[:, :]))
    blk.tensor(tv); blk.vector(vv)
    barrier(env)

    for hh in range(2):
        mm_qprev = mm.n
        def gq(g, _hh=hh, _mmp=mm_qprev):
            g.wait_ge(mm.h, _mmp)
            for c in range(NCH):
                ld2.inc(g.dma_start(out=wq_sb[:, c, :], in_=env.dram["l1_wqT"][_hh, c*128:(c+1)*128, :]), 16)
        blk.gpsimd(gq)
        ld_q = ld2.n
        mm_q0 = mm.n
        ve_qpre = ve.n
        def tq(t, _ld=ld_q, _vq=ve_qpre):
            t.wait_ge(ld2.h, _ld)
            t.wait_ge(ve.h, _vq)
            for n4 in range(4):
                ps = env.ps_a if n4 % 2 == 0 else env.ps_b
                if n4 >= 2:
                    t.wait_ge(ve.h, vq_e0 + (n4 - 2 + 1))
                for c in range(NCH):
                    mm.inc(t.matmul(ps[:, 0:512], wq_sb[:, c, :], env.xln_fm[:, c, n4*512:(n4+1)*512],
                                    start=(c == 0), stop=(c == NCH-1)))
        vq_e0 = ve.n
        def vq(v, _mm0=mm_q0):
            for n4 in range(4):
                ps = env.ps_a if n4 % 2 == 0 else env.ps_b
                v.wait_ge(mm.h, _mm0 + (n4+1)*NCH)
                ve.inc(v.tensor_copy(qraw.rearrange("p (a b) -> p a b", a=4)[:, n4, :], ps[:, 0:512]))
        blk.tensor(tq); blk.vector(vq)
        ve_qraw = ve.n
        def gqswp(g, _vq=ve_qraw):
            g.wait_ge(ve.h, _vq)
            ld2.inc(g.dma_start(out=kswp[0:64, :], in_=qraw[64:128, :]), 16)
            ld2.inc(g.dma_start(out=kswp[64:128, :], in_=qraw[0:64, :]), 16)
        blk.gpsimd(gqswp)
        ld_qswp = ld2.n
        def vqr(v, _hh=hh, _lq=ld_qswp):
            v.wait_ge(ld2.h, _lq)
            dst = qfm[:, _hh, :]
            v.tensor_tensor(dst[:, :], qraw[:, :], cosr[:, :], ALU.mult)
            v.tensor_tensor(rtmp[:, :], kswp[:, :], sinr[:, :], ALU.mult)
            v.tensor_tensor(dst[0:64, :], dst[0:64, :], rtmp[0:64, :], ALU.subtract)
            ve.inc(v.tensor_tensor(dst[64:128, :], dst[64:128, :], rtmp[64:128, :], ALU.add))
        blk.vector(vqr)
    barrier(env)

    prev_se = sc.n
    prev_tp = mm2.n
    prev_av = mm.n
    for hh in range(2):
        for qc in range(NT):
            mm_s = mm.n
            ve_pre = ve.n
            def ts(t, _hh=hh, _qc=qc, _vp=ve_pre):
                _kmax = (_qc + 1) * 128
                t.wait_ge(ve.h, _vp)
                for ks in range((_kmax + 511) // 512):
                    kw = min(512, _kmax - ks*512)
                    ps = env.ps_a if ks % 2 == 0 else env.ps_b
                    if ks >= 2:
                        t.wait_ge(ve.h, ve_sc0 + (ks - 2 + 1))
                    mm.inc(t.matmul(ps[:, 0:kw], qfm[:, _hh, _qc*128:(_qc+1)*128], kfm[:, ks*512:ks*512+kw],
                                    start=True, stop=True))
            ve_sc0 = ve.n
            def vs(v, _qc=qc, _mm=mm_s, _psc=prev_se):
                _kmax = (_qc + 1) * 128
                v.wait_ge(sc.h, _psc)
                for ks in range((_kmax + 511) // 512):
                    kw = min(512, _kmax - ks*512)
                    ps = env.ps_a if ks % 2 == 0 else env.ps_b
                    v.wait_ge(mm.h, _mm + ks + 1)
                    if ks*512 + kw == _kmax:
                        if kw > 128:
                            v.tensor_copy(scr1[:, ks*512:ks*512+kw-128], ps[:, 0:kw-128])
                        ve.inc(v.tensor_tensor(scr1[:, _kmax-128:_kmax], ps[:, kw-128:kw], dmask[:, :], ALU.add))
                    else:
                        ve.inc(v.tensor_copy(scr1[:, ks*512:ks*512+kw], ps[:, 0:kw]))
            blk.tensor(ts); blk.vector(vs)
            def vm(v, _qc=qc):
                _kmax = (_qc + 1) * 128
                v.tensor_reduce(stat[:, 0, 2:3], scr1[:, 0:_kmax], axis=AX, op=ALU.max)
                v.tensor_copy(env.fceA[:, :], env.fceB[:, :])
                ve.inc(v.tensor_scalar(out=stat[:, 0, 3:4], in0=stat[:, 0, 2:3], scalar1=-SCL, scalar2=None, op0=ALU.mult))
            blk.vector(vm)
            ve_m = ve.n
            def se(s, _qc=qc, _ptp=prev_tp, _vm=ve_m):
                _kmax = (_qc + 1) * 128
                s.wait_ge(ve.h, _vm)
                s.wait_ge(mm2.h, _ptp)
                sc.inc(s.activation(out=probs_b[:, 0:_kmax], in_=scr1[:, 0:_kmax], func=AF.Exp,
                                    bias=stat[:, 0, 3:4], scale=SCL, accum_out=stat[:, 0, 4:5]))
            blk.scalar(se)
            prev_se = sc.n
            sc_edone = sc.n
            mm2_t0 = mm2.n
            def tav(t, _qc=qc, _sce=sc_edone):
                t.wait_ge(sc.h, _sce)
                for kt in range(_qc + 1):
                    if kt >= 2:
                        t.wait_ge(sc2.h, sc2_t0 + (kt - 2 + 1))
                    mm2.inc(t.transpose(env.ps_t[kt % 2][:, :], probs_b[:, kt*128:(kt+1)*128], env.identb[:]))
            sc2_t0 = sc2.n
            def s2t(s, _qc=qc, _m2=mm2_t0, _pav=prev_av):
                s.wait_ge(mm.h, _pav)
                for kt in range(_qc + 1):
                    s.wait_ge(mm2.h, _m2 + kt + 1)
                    sc2.inc(s.copy(out=env.Sp[:, kt, :], in_=env.ps_t[kt % 2][:, :]))
            blk.tensor(tav); blk.scalar(s2t)
            prev_tp = mm2.n
            def tav2(t, _hh=hh, _qc=qc, _s2=sc2_t0):
                psv = env.ps_c if (_hh*NT+_qc) % 2 == 0 else env.ps_d
                for kt in range(_qc + 1):
                    t.wait_ge(sc2.h, _s2 + kt + 1)
                    mm.inc(t.matmul(psv[:, :], env.Sp[:, kt, :], env.vtm[:, kt, :],
                                    start=(kt == 0), stop=(kt == _qc)))
            blk.tensor(tav2)
            mm_av = mm.n
            prev_av = mm.n
            def vf(v, _hh=hh, _qc=qc, _mma=mm_av):
                psv = env.ps_c if (_hh*NT+_qc) % 2 == 0 else env.ps_d
                v.wait_ge(mm.h, _mma)
                v.reciprocal(stat[:, 0, 5:6], stat[:, 0, 4:5])
                v.tensor_copy(env.fceA[:, :], env.fceB[:, :])
                v.tensor_tensor(stat[:, 0, 6:7], stat[:, 0, 4:5], stat[:, 0, 5:6], ALU.mult)
                v.tensor_copy(env.fceA[:, :], env.fceB[:, :])
                v.tensor_scalar(out=stat[:, 0, 6:7], in0=stat[:, 0, 6:7], scalar1=-1.0, scalar2=2.0, op0=ALU.mult, op1=ALU.add)
                v.tensor_copy(env.fceA[:, :], env.fceB[:, :])
                v.tensor_tensor(stat[:, 0, 5:6], stat[:, 0, 5:6], stat[:, 0, 6:7], ALU.mult)
                v.tensor_copy(env.fceA[:, :], env.fceB[:, :])
                ve2.inc(v.tensor_scalar(out=attn_tm[:, _qc, _hh*128:(_hh+1)*128], in0=psv[:, :],
                                        scalar1=stat[:, 0, 5:6], scalar2=None, op0=ALU.mult))
            blk.vector(vf)
            ve2_f = ve2.n
            def tg(t, _v=ve2_f):
                t.wait_ge(ve2.h, _v)
            blk.tensor(tg)
    barrier(env)

    attn_fm = env.qf
    pairs = []
    for ch in range(2):
        for tc in range(NT):
            pairs.append((attn_tm[:, tc, ch*128:(ch+1)*128], attn_fm[:, ch, tc*128:(tc+1)*128]))
    transpose_tiles(env, pairs)
    barrier(env)
    wo_sb = env.wbig.ap().rearrange("p c x -> p (c x)")[:, 0:2048].rearrange("p (c x) -> p c x", c=2)
    def gw(g):
        for ch in range(2):
            ld2.inc(g.dma_start(out=wo_sb[:, ch, :], in_=env.dram["l1_woT"][ch*128:(ch+1)*128, :]), 16)
    blk.gpsimd(gw)
    ld_o = ld2.n
    mm_o0 = mm.n
    ve_o0 = ve.n
    def to(t):
        t.wait_ge(ld2.h, ld_o)
        for tc in range(NT):
            if tc >= 1:
                t.wait_ge(ve.h, ve_o0 + tc)
            for half in range(2):
                ps = env.ps_a if half == 0 else env.ps_b
                for ch in range(2):
                    ins = t.matmul(ps[:, :], attn_fm[:, ch, tc*128:(tc+1)*128], wo_sb[:, ch, half*512:(half+1)*512],
                                   start=(ch == 0), stop=(ch == 1))
                    if ch == 1:
                        mm.inc(ins)
    def vo(v):
        for tc in range(NT):
            v.wait_ge(mm.h, mm_o0 + 2*tc + 1)
            v.tensor_tensor(env.stream[:, tc, 0:512], env.stream[:, tc, 0:512], env.ps_a[:, :], ALU.add)
            v.wait_ge(mm.h, mm_o0 + 2*tc + 2)
            ve.inc(v.tensor_tensor(env.stream[:, tc, 512:1024], env.stream[:, tc, 512:1024], env.ps_b[:, :], ALU.add))
    blk.tensor(to); blk.vector(vo)
    barrier(env)


def p_l1_ffn(env):
    blk = env.blk
    ld2, mm, ve, sc, gp = env.s_ld2, env.s_mm, env.s_ve, env.s_sc, env.s_gp
    ve2, sc2 = env.s_ve2, env.s_sc2
    gs = env.s_gd
    stat2 = env.stat2
    emit_layernorm(env, need_fm=True)
    barrier(env)
    N_TOK = 16384.0

    st2 = stat2.ap().rearrange("p a b -> p (a b)")
    gmb = env.l0bias_col
    wqv = env.wbig.ap().rearrange("p c x -> p (c x)")[:, 0:2048].rearrange("p (a c x) -> p a c x", a=2, c=8)
    keysv = env.vtm.ap().rearrange("p a b -> p (a b)")[:, 0:512].rearrange("p (a b) -> p a b", a=2)
    qhat = env.ktm.ap().rearrange("p a b -> p (a b)")[:, 0:2048]
    scr1, scr2 = env.scr1, env.scr2
    sxF = env.qf.ap().rearrange("p a b -> p (a b)").bitcast(F32)
    sxI = env.qf.ap().rearrange("p a b -> p (a b)").bitcast(I32)
    ixI = env.kf.ap().rearrange("p a b -> p (a b)").bitcast(I32)
    w8F = env.sinf.ap().bitcast(F32)
    idxI = env.cosf.ap().bitcast(I32)
    ring = env.xln_fm
    iot = env.iota256
    ztF = env.zt.ap().bitcast(F32)
    ztI = env.zt.ap().bitcast(I32)

    def gld(g):
        ld2.inc(g.dma_start(out=gmb[:, 0:16], in_=env.dram["l1f_gamma"][:, :]), 16)
        ld2.inc(g.dma_start(out=gmb[:, 16:32], in_=env.dram["l1f_beta"][:, :]), 16)
        ld2.inc(g.dma_start(out=iot[:, :], in_=env.dram["c_iota256"][:, :]), 16)
    blk.gpsimd(gld)

    p1_mm0 = mm.n
    p1_ve0 = ve.n
    p1_qmm_end = []
    for c in range(16):
        mmp = (p1_qmm_end[c-2] if c >= 2 else p1_mm0)
        def gw(g, _c=c, _mmp=mmp):
            g.wait_ge(mm.h, _mmp)
            for dch in range(NCH):
                ld2.inc(g.dma_start(out=wqv[:, _c % 2, dch, :],
                                    in_=env.dram["l1f_wq"][dch*128:(dch+1)*128, _c*128:(_c+1)*128]), 16)
        blk.gpsimd(gw)
        ld_w = ld2.n
        def tq(t, _c=c, _ld=ld_w):
            t.wait_ge(ld2.h, _ld)
            for n4 in range(4):
                gq = _c * 4 + n4
                ps = env.ps_a if gq % 2 == 0 else env.ps_b
                if gq >= 2:
                    t.wait_ge(ve.h, p1_ve0 + gq - 1)
                for dch in range(NCH):
                    mm.inc(t.matmul(ps[:, 0:512], wqv[:, _c % 2, dch, :],
                                    env.xln_fm[:, dch, n4*512:(n4+1)*512],
                                    start=(dch == 0), stop=(dch == NCH-1)))
        blk.tensor(tq)
        p1_qmm_end.append(mm.n)
        def vq(v, _c=c):
            for n4 in range(4):
                gq = _c * 4 + n4
                ps = env.ps_a if gq % 2 == 0 else env.ps_b
                v.wait_ge(mm.h, p1_mm0 + (gq+1)*NCH)
                ve.inc(v.tensor_copy(qhat[:, n4*512:(n4+1)*512], ps[:, 0:512]))
            v.reduce_sum(st2[:, _c:_c+1], qhat[:, :], axis=AX)
            ve2.inc(v.scalar_tensor_tensor(out=scr1[:, :], in0=qhat[:, :], scalar=1.0,
                                           in1=qhat[:, :], op0=ALU.mult, op1=ALU.mult,
                                           accum_out=st2[:, 16+_c:17+_c]))
        blk.vector(vq)
    def vfin(v):
        ve2.inc(v.tensor_copy(scr1[:, 0:512], scr1[:, 512:1024]))
    blk.vector(vfin)
    p1_done = ve2.n

    def gbn(g):
        g.wait_ge(ve2.h, p1_done)
        gs.inc(g.dma_start(out=env.bn_in[:, :], in_=st2[:, 0:32]), 16)
        g.wait_ge(gs.h, gs.n)
        gp.inc(g.collective_compute("AllReduce", ALU.add,
                                    replica_groups=[env.bn_group],
                                    ins=[env.bn_in[:, :]], outs=[env.bn_out[:, :]]), 1)
        g.wait_ge(gp.h, gp.n)
        ld2.inc(g.dma_start(out=st2[:, 0:32], in_=env.bn_out[:, :]), 16)
    blk.gpsimd(gbn)
    ld_bn = ld2.n

    def vab(v):
        def sp():
            for _ in range(6):
                v.tensor_copy(env.fceA[:, :], env.fceB[:, :])
        v.wait_ge(ld2.h, ld_bn)
        v.tensor_scalar(out=scr1[:, 0:16], in0=st2[:, 0:16], scalar1=1.0/N_TOK, scalar2=None, op0=ALU.mult)
        v.tensor_scalar(out=scr1[:, 16:32], in0=st2[:, 16:32], scalar1=1.0/N_TOK, scalar2=1e-5, op0=ALU.mult, op1=ALU.add)
        sp()
        v.tensor_tensor(scr1[:, 32:48], scr1[:, 0:16], scr1[:, 0:16], ALU.mult)
        sp()
        ve.inc(v.tensor_tensor(scr1[:, 48:64], scr1[:, 16:32], scr1[:, 32:48], ALU.subtract))
    blk.vector(vab)
    ve_var = ve.n
    def sab(s):
        s.wait_ge(ve.h, ve_var)
        sc.inc(s.activation(out=scr1[:, 64:80], in_=scr1[:, 48:64], func=AF.Sqrt))
    blk.scalar(sab)
    sc_sqrt = sc.n
    def vab2(v):
        def sp():
            for _ in range(6):
                v.tensor_copy(env.fceA[:, :], env.fceB[:, :])
        v.wait_ge(sc.h, sc_sqrt)
        v.reciprocal(scr1[:, 80:96], scr1[:, 64:80])
        sp()
        v.tensor_tensor(scr1[:, 96:112], scr1[:, 64:80], scr1[:, 80:96], ALU.mult)
        sp()
        v.tensor_scalar(out=scr1[:, 112:128], in0=scr1[:, 96:112], scalar1=-1.0, scalar2=2.0, op0=ALU.mult, op1=ALU.add)
        sp()
        v.tensor_tensor(scr1[:, 128:144], scr1[:, 80:96], scr1[:, 112:128], ALU.mult)
        sp()
        v.tensor_tensor(st2[:, 32:48], gmb[:, 0:16], scr1[:, 128:144], ALU.mult)
        sp()
        v.tensor_tensor(scr1[:, 144:160], scr1[:, 0:16], st2[:, 32:48], ALU.mult)
        sp()
        v.tensor_tensor(st2[:, 48:64], gmb[:, 16:32], scr1[:, 144:160], ALU.subtract)
        sp()
        ve2.inc(v.tensor_copy(scr1[:, 0:512], scr1[:, 512:1024]))
    blk.vector(vab2)
    ab_done = ve2.n

    p2_mm0 = mm.n
    p1_all_ve = None
    p2_sc0 = sc.n
    p2_qmm_end = []
    p2_sim_end = []
    p2_v20 = ve2.n
    pk_ve0 = ve.n
    for c in range(16):
        p, h = c // 8, c % 8
        mmp = (p2_qmm_end[c-2] if c >= 2 else (p1_qmm_end[-1] if p1_qmm_end else p2_mm0))
        mmk = (p2_sim_end[c-2] if c >= 2 else p2_mm0)
        def gw2(g, _c=c, _mmp=mmp, _mmk=mmk):
            g.wait_ge(mm.h, _mmp)
            for dch in range(NCH):
                ld2.inc(g.dma_start(out=wqv[:, _c % 2, dch, :],
                                    in_=env.dram["l1f_wq"][dch*128:(dch+1)*128, _c*128:(_c+1)*128]), 16)
            g.wait_ge(mm.h, _mmk)
            ld2.inc(g.dma_start(out=keysv[:, _c % 2, :], in_=env.dram["l1f_keysT"][_c, :, :]), 16)
        blk.gpsimd(gw2)
        ld_w = ld2.n
        mm_q0 = mm.n
        def tq2(t, _c=c, _ld=ld_w, _m0=mm_q0):
            t.wait_ge(ld2.h, _ld)
            if _c == 0:
                t.wait_ge(ve.h, pk_ve0)
            for n4 in range(4):
                gq = _c * 4 + n4
                ps = env.ps_a if gq % 2 == 0 else env.ps_b
                if gq >= 2:
                    t.wait_ge(sc.h, p2_sc0 + gq - 1)
                if n4 < 2 and _c > 0:
                    t.wait_ge(ve.h, pk_ve0 + _c * 16)
                for dch in range(NCH):
                    mm.inc(t.matmul(ps[:, 0:512], wqv[:, _c % 2, dch, :],
                                    env.xln_fm[:, dch, n4*512:(n4+1)*512],
                                    start=(dch == 0), stop=(dch == NCH-1)))
        blk.tensor(tq2)
        p2_qmm_end.append(mm.n)
        def saf(s, _c=c, _m0=mm_q0):
            if _c == 0:
                s.wait_ge(ve2.h, ab_done)
            for n4 in range(4):
                gq = _c * 4 + n4
                ps = env.ps_a if gq % 2 == 0 else env.ps_b
                s.wait_ge(mm.h, _m0 + (n4+1)*NCH)
                sc.inc(s.activation(out=qhat[:, n4*512:(n4+1)*512], in_=ps[:, 0:512], func=AF.Identity,
                                    scale=st2[:, 32+_c:33+_c], bias=st2[:, 48+_c:49+_c]))
        blk.scalar(saf)
        sc_af = sc.n
        mm_s0 = mm.n
        def tsim(t, _c=c, _saf=sc_af):
            t.wait_ge(sc.h, _saf)
            for n in range(16):
                ps = env.ps_a if n % 2 == 0 else env.ps_b
                if n >= 2:
                    t.wait_ge(ve.h, pk_ve0 + _c*16 + n - 1)
                mm.inc(t.matmul(ps[:, 0:256], qhat[:, n*128:(n+1)*128], keysv[:, _c % 2, :],
                                start=True, stop=True))
        blk.tensor(tsim)
        p2_sim_end.append(mm.n)
        def vpk(v, _c=c, _m0=mm_s0):
            v.wait_ge(ve2.h, p2_v20 + _c*17)
            for n in range(16):
                ps = env.ps_a if n % 2 == 0 else env.ps_b
                psI = ps.ap().bitcast(I32)
                dstF = (scr1[:, (n % 8)*256:(n % 8 + 1)*256] if n < 8
                        else scr2[:, (n % 8)*256:(n % 8 + 1)*256])
                dstI = dstF.bitcast(I32)
                v.wait_ge(mm.h, _m0 + n + 1)
                ve.inc(v.tensor_scalar(out=dstI, in0=psI[:, 0:256], scalar1=-256, scalar2=None, op0=ALU.bitwise_and))
                v.tensor_tensor(dstI, dstI, iot[:, :], ALU.bitwise_or)
                ve2.inc(v.max(ztF[:, n*8:(n+1)*8], dstF))
            cb0 = p * 1024 + 0 * 64 + h * 8
            for n in range(16):
                cb = p * 1024 + n * 64 + h * 8
                v.tensor_scalar(out=ixI[:, cb:cb+8], in0=ztI[:, n*8:(n+1)*8], scalar1=255, scalar2=None, op0=ALU.bitwise_and)
            for n in range(16):
                cb = p * 1024 + n * 64 + h * 8
                last = v.tensor_scalar(out=sxI[:, cb:cb+8], in0=ztI[:, n*8:(n+1)*8], scalar1=-256, scalar2=None, op0=ALU.bitwise_and)
                if n == 15:
                    ve2.inc(last)
        blk.vector(vpk)
    barrier(env)

    import os as _osf
    if _osf.environ.get("MK_L1F") == "pass2":
        return
    RNE_OFF = RNE_C - 0.4375
    gp2 = env.s_gp2
    vst_done = {}
    sst_done = {}

    def stage2_vst_ops(n):
        ops = []
        ub = [u * 256 for u in range(8)]
        def S(u, col, w=8):
            return scr2[:, ub[u]+col:ub[u]+col+w]
        def step(fn):
            for u in range(8):
                ops.append(lambda v, _u=u, _f=fn: _f(v, _u))
        step(lambda v, u: v.tensor_scalar(out=S(u, 152), in0=ixI[:, n*64+u*8:n*64+u*8+8],
                                          scalar1=1.0, scalar2=0.0, op0=ALU.mult, op1=ALU.add))
        step(lambda v, u: v.tensor_scalar(out=S(u, 160), in0=ixI[:, 1024+n*64+u*8:1024+n*64+u*8+8],
                                          scalar1=1.0, scalar2=0.0, op0=ALU.mult, op1=ALU.add))
        for a in range(8):
            step(lambda v, u, _a=a: v.tensor_scalar(
                out=S(u, _a*8), in0=sxF[:, 1024+n*64+u*8:1024+n*64+u*8+8],
                scalar1=sxF[:, n*64+u*8+_a:n*64+u*8+_a+1], scalar2=None, op0=ALU.add))
        step(lambda v, u: v.tensor_scalar(out=S(u, 0, 64).bitcast(I32), in0=S(u, 0, 64).bitcast(I32),
                                          scalar1=-64, scalar2=None, op0=ALU.bitwise_and))
        step(lambda v, u: v.tensor_tensor(S(u, 0, 64).bitcast(I32), S(u, 0, 64).bitcast(I32),
                                          iot[:, 0:64], ALU.bitwise_or))
        step(lambda v, u: v.max(S(u, 64), S(u, 0, 64)))
        step(lambda v, u: v.tensor_scalar(out=S(u, 72).bitcast(I32), in0=S(u, 64).bitcast(I32),
                                          scalar1=63, scalar2=None, op0=ALU.bitwise_and))
        step(lambda v, u: v.tensor_scalar(out=S(u, 64).bitcast(I32), in0=S(u, 64).bitcast(I32),
                                          scalar1=-64, scalar2=None, op0=ALU.bitwise_and))
        step(lambda v, u: v.tensor_scalar(out=S(u, 80), in0=S(u, 72).bitcast(I32),
                                          scalar1=1.0, scalar2=0.0, op0=ALU.mult, op1=ALU.add))
        step(lambda v, u: v.tensor_scalar(out=S(u, 88), in0=S(u, 80),
                                          scalar1=0.125, scalar2=RNE_OFF, op0=ALU.mult, op1=ALU.add))
        step(lambda v, u: v.tensor_scalar(out=S(u, 88), in0=S(u, 88),
                                          scalar1=1.0, scalar2=RNE_C, op0=ALU.mult, op1=ALU.subtract))
        step(lambda v, u: v.scalar_tensor_tensor(out=S(u, 96), in0=S(u, 88), scalar=-8.0,
                                                 in1=S(u, 80), op0=ALU.mult, op1=ALU.add))
        for base, afc, srcc in ((120, 88, 152), (128, 96, 160)):
            for j in range(8):
                step(lambda v, u, _j=j, _a=afc: v.tensor_scalar(out=S(u, 104), in0=S(u, _a),
                                                                scalar1=float(_j), scalar2=None, op0=ALU.is_equal))
                step(lambda v, u, _j=j, _s=srcc: v.tensor_scalar(out=S(u, 112), in0=S(u, 104),
                                                                 scalar1=S(u, _s+_j, 1), scalar2=None, op0=ALU.mult))
                if j == 0:
                    step(lambda v, u, _b=base: v.tensor_copy(S(u, _b), S(u, 112)))
                else:
                    step(lambda v, u, _b=base: v.tensor_tensor(S(u, _b), S(u, _b), S(u, 112), ALU.add))
        step(lambda v, u: v.scalar_tensor_tensor(out=S(u, 136), in0=S(u, 120), scalar=256.0,
                                                 in1=S(u, 128), op0=ALU.mult, op1=ALU.add))
        step(lambda v, u: v.tensor_scalar(out=idxI[:, n*64+u*8:n*64+u*8+8], in0=S(u, 136),
                                          scalar1=1.0, scalar2=0.0, op0=ALU.mult, op1=ALU.add))
        step(lambda v, u: v.tensor_reduce(S(u, 144, 1), S(u, 64), axis=AX, op=ALU.max))
        for u in range(8):
            if u < 7:
                ops.append(lambda v, _u=u: v.tensor_scalar(out=S(_u, 145, 1), in0=S(_u, 144, 1),
                                                           scalar1=-1.0, scalar2=None, op0=ALU.mult))
            else:
                ops.append(lambda v, _u=u: sc2.inc(v.tensor_scalar(out=S(_u, 145, 1), in0=S(_u, 144, 1),
                                                                   scalar1=-1.0, scalar2=None, op0=ALU.mult)))
        return ops

    def stage2_vsf_ops(n):
        ops = []
        ub = [u * 256 for u in range(8)]
        def S(u, col, w=8):
            return scr2[:, ub[u]+col:ub[u]+col+w]
        def first(v):
            v.wait_ge(gp2.h, sst_done[n])
            v.reciprocal(S(0, 147, 1), S(0, 146, 1))
        ops.append(first)
        for u in range(1, 8):
            ops.append(lambda v, _u=u: v.reciprocal(S(_u, 147, 1), S(_u, 146, 1)))
        for _ in range(8):
            ops.append(lambda v: v.tensor_copy(env.fceA[:, :], env.fceB[:, :]))
        for u in range(8):
            ops.append(lambda v, _u=u: v.tensor_scalar(out=w8F[:, n*64+_u*8:n*64+_u*8+8],
                                                       in0=w8F[:, n*64+_u*8:n*64+_u*8+8],
                                                       scalar1=S(_u, 147, 1), scalar2=None, op0=ALU.mult))
        return ops

    def emit_sst(n):
        ub = [u * 256 for u in range(8)]
        def S(u, col, w=8):
            return scr2[:, ub[u]+col:ub[u]+col+w]
        def sst(s, _n=n):
            s.wait_ge(sc2.h, vst_done[_n])
            for u in range(8):
                ins = s.activation(out=w8F[:, _n*64+u*8:_n*64+u*8+8], in_=S(u, 64), func=AF.Exp,
                                   bias=S(u, 145, 1), scale=1.0, accum_out=S(u, 146, 1))
                if u == 7:
                    gp2.inc(ins)
        blk.scalar(sst)
        sst_done[n] = gp2.n

    if _osf.environ.get("MK_L1F") == "stage2":
        for n in range(NT):
            ops = stage2_vst_ops(n)
            def vv(v, _o=ops):
                for op in _o:
                    op(v)
            blk.vector(vv)
            vst_done[n] = sc2.n
            emit_sst(n)
            ops2 = stage2_vsf_ops(n)
            def vv2(v, _o=ops2):
                for op in _o:
                    op(v)
            blk.vector(vv2)
        barrier(env)
        return

    ops0 = stage2_vst_ops(0)
    def vv0(v, _o=ops0):
        for op in _o:
            op(v)
    blk.vector(vv0)
    vst_done[0] = sc2.n
    emit_sst(0)
    ops0f = stage2_vsf_ops(0)

    ring_slots = ([ring[:, i, :] for i in range(8)]
                  + [env.ktm.ap().rearrange("p a b -> p (a b)")[:, 0:2048],
                     env.ktm.ap().rearrange("p a b -> p (a b)")[:, 2048:4096],
                     env.vtm.ap().rearrange("p a b -> p (a b)")])
    NSLOT = len(ring_slots)
    dn0 = ve.n
    up0 = mm.n
    ps0 = sc.n
    gs_base = gs.n
    res_done = []
    tcat = env.dram["l1f_tcat"]
    for tc in range(NT):
        def gg(g, _tc=tc, _sd=vst_done[tc]):
            g.wait_ge(sc2.h, _sd)
            for j in range(64):
                gidx = _tc * 64 + j
                if gidx >= NSLOT:
                    g.wait_ge(ve.h, dn0 + (gidx - NSLOT + 1))
                    g.wait_ge(mm.h, up0 + 2 * (gidx - NSLOT + 1))
                gs.inc(g.indirect_dma_start(
                    ring_slots[gidx % NSLOT], None, tcat[:, :],
                    IndirectOffsetOnAxis(ap=idxI[:, gidx:gidx+1], axis=0)), 16)
        blk.gpsimd(gg)
        inter = stage2_vst_ops(tc + 1) if tc + 1 < NT else []
        vsf_pre = (stage2_vsf_ops(tc) if tc > 0 else ops0f)
        nper = (len(inter) + 63) // 64
        def vdn(v, _tc=tc, _in=inter, _pre=vsf_pre, _np=nper):
            for op in _pre:
                op(v)
            if _tc >= 2:
                v.wait_ge(sc.h, ps0 + (_tc - 1) * 64)
            k = 0
            for j in range(64):
                gidx = _tc * 64 + j
                v.wait_ge(gs.h, gs_base + 16 * (gidx + 1))
                ve.inc(v.scalar_tensor_tensor(out=scr1[:, 0:1024], in0=ring_slots[gidx % NSLOT][:, 0:1024],
                                              scalar=1.0, in1=env.xln_tm[:, _tc, :],
                                              op0=ALU.mult, op1=ALU.mult,
                                              accum_out=scr1[:, 1152+(_tc % 2)*64+j:1153+(_tc % 2)*64+j]))
                for _ in range(_np):
                    if k < len(_in):
                        _in[k](v)
                        k += 1
            while k < len(_in):
                _in[k](v)
                k += 1
        blk.vector(vdn)
        if tc + 1 < NT:
            vst_done[tc + 1] = sc2.n
        def sgu(s, _tc=tc):
            def gelu(j):
                gidx = _tc * 64 + j
                s.wait_ge(ve.h, dn0 + gidx + 1)
                s.activation(out=scr1[:, 1024+j:1025+j], in_=scr1[:, 1152+(_tc % 2)*64+j:1153+(_tc % 2)*64+j],
                             func=AF.Gelu_apprx_tanh)
            def gcol(j):
                gidx = _tc * 64 + j
                s.activation(out=scr1[:, 1088+j:1089+j], in_=scr1[:, 1024+j:1025+j], func=AF.Identity,
                             scale=w8F[:, gidx:gidx+1])
            def presc(j):
                gidx = _tc * 64 + j
                sc.inc(s.activation(out=ring_slots[gidx % NSLOT][:, 1024:2048], in_=ring_slots[gidx % NSLOT][:, 1024:2048],
                                    func=AF.Identity, scale=scr1[:, 1088+j:1089+j]))
            for j in range(64):
                gelu(j)
                if j >= 2:
                    presc(j - 2)
                gcol(j)
            presc(62)
            presc(63)
        blk.scalar(sgu)
        if tc + 1 < NT:
            emit_sst(tc + 1)
        def tup(t, _tc=tc):
            if _tc > 0:
                t.wait_ge(ve2.h, res_done[_tc - 1])
            for j in range(64):
                gidx = _tc * 64 + j
                t.wait_ge(sc.h, ps0 + _tc * 64 + j + 1)
                mm.inc(t.matmul(env.ps_a[:, :], env.identb[:], ring_slots[gidx % NSLOT][:, 1024:1536],
                                start=(j == 0), stop=(j == 63)))
                mm.inc(t.matmul(env.ps_b[:, :], env.identb[:], ring_slots[gidx % NSLOT][:, 1536:2048],
                                start=(j == 0), stop=(j == 63)))
        blk.tensor(tup)
        mm_up_end = mm.n
        def vres(v, _tc=tc, _m=mm_up_end):
            v.wait_ge(mm.h, _m)
            v.tensor_tensor(env.stream[:, _tc, 0:512], env.stream[:, _tc, 0:512], env.ps_a[:, :], ALU.add)
            ve2.inc(v.tensor_tensor(env.stream[:, _tc, 512:1024], env.stream[:, _tc, 512:1024], env.ps_b[:, :], ALU.add))
        blk.vector(vres)
        res_done.append(ve2.n)
    barrier(env)



def p_store(env):
    blk = env.blk; st = env.s_st
    def g(g):
        for tc in range(NT):
            st.inc(g.dma_start(out=env.y[tc*128:(tc+1)*128, :], in_=env.stream[:, tc, :]), 16)
        g.wait_ge(st.h, st.n)
    blk.gpsimd(g)


def host_prep(inputs):
    x = np.ascontiguousarray(np.asarray(inputs["L0_attn_x"], dtype=np.float32))
    qkvu_wT = np.ascontiguousarray(np.asarray(inputs["L0_attn_qkvu_w"], dtype=np.float32).T)
    qkvu_b = np.asarray(inputs["L0_attn_qkvu_b"], dtype=np.float32).reshape(1, -1)
    out_wT = np.ascontiguousarray(np.asarray(inputs["L0_attn_out_w"], dtype=np.float32).T)
    out_b = np.asarray(inputs["L0_attn_out_b"], dtype=np.float32).reshape(1, -1)
    theta = np.ascontiguousarray(np.asarray(inputs["L0_attn_theta"], dtype=np.float32).reshape(8, 128).T)
    slope = np.asarray(inputs["L0_attn_slope_rate"], dtype=np.float32).reshape(1, 8)
    ident = np.eye(128, dtype=np.float32)
    triu = np.triu(np.ones((128, 128), dtype=np.float32))
    wgT = np.ascontiguousarray(np.asarray(inputs["L0_ffn_w_gate"], dtype=np.float32).T)
    wuT = np.ascontiguousarray(np.asarray(inputs["L0_ffn_w_up"], dtype=np.float32).T)
    wdT = np.ascontiguousarray(np.asarray(inputs["L0_ffn_w_down"], dtype=np.float32).T)
    perm = np.concatenate([np.arange(0, 128, 2), np.arange(1, 128, 2)])
    Wq = np.asarray(inputs["L1_attn_Wq"], dtype=np.float32)
    l1_wqT = np.ascontiguousarray(np.stack([Wq[hh].T[:, perm] for hh in range(2)]))
    l1_wkT = np.ascontiguousarray(np.asarray(inputs["L1_attn_Wk"], dtype=np.float32).T[:, perm])
    l1_wvT = np.ascontiguousarray(np.asarray(inputs["L1_attn_Wv"], dtype=np.float32).T)
    l1_woT = np.ascontiguousarray(np.asarray(inputs["L1_attn_Wo"], dtype=np.float32).T)
    pos = np.arange(2048, dtype=np.float64)
    th_j = 10000.0 ** (-np.arange(0, 128, 2, dtype=np.float64) / 64.0)
    ang = th_j[:, None] * pos[None, :]
    rope_cos = np.cos(ang).astype(np.float32)
    rope_sin = np.sin(ang).astype(np.float32)
    qi = np.arange(128)[:, None]; ki = np.arange(128)[None, :]
    diagmask = np.where(ki > qi, np.float32(-1e30), np.float32(0.0)).astype(np.float32)
    l1f_wq = np.ascontiguousarray(np.asarray(inputs["L1_ffn_w_q"], dtype=np.float32))
    keys = np.asarray(inputs["L1_ffn_keys"], dtype=np.float32)
    keysT = np.zeros((16, 128, 256), dtype=np.float32)
    for p_ in range(2):
        for h_ in range(8):
            keysT[p_*8+h_] = keys[h_, :, p_, :].T
    l1f_gamma = np.ascontiguousarray(np.asarray(inputs["L1_ffn_bn_gamma"], dtype=np.float32).reshape(16, 128).T)
    l1f_beta = np.ascontiguousarray(np.asarray(inputs["L1_ffn_bn_beta"], dtype=np.float32).reshape(16, 128).T)
    l1f_tcat = np.ascontiguousarray(np.concatenate(
        [np.asarray(inputs["L1_ffn_down_embed"], dtype=np.float32),
         np.asarray(inputs["L1_ffn_up_embed"], dtype=np.float32)], axis=1))
    c_iota256 = np.ascontiguousarray(np.tile(np.arange(256, dtype=np.int32), (128, 1)))
    common = {
        "l1f_wq": l1f_wq, "l1f_keysT": keysT, "l1f_gamma": l1f_gamma, "l1f_beta": l1f_beta,
        "l1f_tcat": l1f_tcat, "c_iota256": c_iota256,
        "l1_wqT": l1_wqT, "l1_wkT": l1_wkT, "l1_wvT": l1_wvT, "l1_woT": l1_woT,
        "c_rope_cos": rope_cos, "c_rope_sin": rope_sin, "c_diagmask": diagmask,
        "l0f_wgT": wgT, "l0f_wuT": wuT, "l0f_wdT": wdT,
        "l0_qkvu_wT": qkvu_wT, "l0_qkvu_b": np.ascontiguousarray(qkvu_b),
        "l0_out_wT": out_wT, "l0_out_b": np.ascontiguousarray(out_b),
        "l0_theta": theta, "l0_slope": np.ascontiguousarray(slope),
        "c_ident": ident, "c_triu": np.ascontiguousarray(triu),
        "l0_qkvu_b_col": np.ascontiguousarray(qkvu_b.reshape(32, 128).T),
    }
    return [dict(common, x=np.ascontiguousarray(x[i])) for i in range(8)]


def _l0_attn_ref_jax(xln, inputs):
    import jax, jax.numpy as jnp
    EPS = 1e-6
    x = xln
    slope_rate = jnp.asarray(inputs["L0_attn_slope_rate"])
    qkvu_w = jnp.asarray(inputs["L0_attn_qkvu_w"]); qkvu_b = jnp.asarray(inputs["L0_attn_qkvu_b"])
    out_w = jnp.asarray(inputs["L0_attn_out_w"]); out_b = jnp.asarray(inputs["L0_attn_out_b"])
    theta = jnp.asarray(inputs["L0_attn_theta"])
    def _lrpe(xh, theta, n):
        idx = jnp.arange(n, dtype=xh.dtype).reshape(1, n, 1)
        th = theta * idx; th = th[None]
        return jnp.concatenate([xh * jnp.cos(th), xh * jnp.sin(th)], axis=-1)
    b, n, hd = x.shape; h = 8
    qkvu = x @ qkvu_w.T + qkvu_b
    q, k, v, u = jnp.split(qkvu, 4, axis=-1)
    to_heads = lambda t: t.reshape(b, n, h, hd // h).transpose(0, 2, 1, 3)
    q, k, v = to_heads(q), to_heads(k), to_heads(v)
    q = jax.nn.silu(q); k = jax.nn.silu(k)
    q = _lrpe(q, theta, n); k = _lrpe(k, theta, n)
    tril = jnp.tril(jnp.ones((n, n), dtype=x.dtype))
    mask = jnp.exp(slope_rate * tril)
    energy = jnp.einsum('bhne,bhme->bhnm', q, k) * mask[None]
    out = jnp.einsum('bhnm,bhmd->bhnd', energy, v)
    out = out.transpose(0, 2, 1, 3).reshape(b, n, hd)
    ms = jnp.mean(jnp.square(out), axis=-1, keepdims=True)
    out = out * jax.lax.rsqrt(ms + EPS)
    out = u * out
    out = out @ out_w.T + out_b
    return out



def kernel(**inputs):
    in_maps = host_prep(inputs)
    nc = build_program(upto=5)
    res = run_bass_kernel_spmd(nc, in_maps, core_ids=list(range(8)))
    return np.stack([np.asarray(r["y"]) for r in res.results])
